# version 01
# baseline (speedup 1.0000x reference)
import numpy as np
from contextlib import ExitStack

import concourse.bass as bass
import concourse.mybir as mybir
from concourse.alu_op_type import AluOpType as OP
from concourse.bass_utils import run_bass_kernel_spmd

F32 = mybir.dt.float32
BF16 = mybir.dt.bfloat16
AF = mybir.ActivationFunctionType
AX = mybir.AxisListType

NCORES = 8
B, L, D = 4, 4096, 1024
TOK = (B * L) // NCORES
NT = TOK // 128
NG = TOK // 512
DC = D // 128
H = 512
E, FF = 6, 1536
FH = FF // 2
NH, HD, FD = 16, 64, 64
EPS = 1e-5


class Buf:
    __slots__ = ("name", "lw", "readers", "fence", "first")

    def __init__(self, name, fence):
        self.name = name
        self.lw = None
        self.readers = {}
        self.fence = fence
        self.first = True


class Prog:
    SEMS = ["pe", "act", "dve", "pool", "sp", "poold", "cc"]

    def __init__(self, nc):
        self.nc = nc
        self.eng = {"pe": nc.tensor, "act": nc.scalar, "dve": nc.vector,
                    "pool": nc.gpsimd, "sp": nc.sync}
        self.ctx = ExitStack()
        self.sems = {s: self.ctx.enter_context(nc.semaphore("s_" + s)) for s in self.SEMS}
        self.cnt = {s: 0 for s in self.SEMS}
        self.waited = {s: {} for s in self.eng}
        self.fence_counts = {s: 0 for s in self.SEMS}

    def buf(self, name):
        return Buf(name, dict(self.fence_counts))

    def fence(self):
        self.fence_counts = dict(self.cnt)

    def sb(self, name, shape, dt, ctx=None):
        side = None if ctx is None else "right"
        return (ctx or self.ctx).enter_context(
            self.nc.sbuf_tensor(name, shape, dt, side=side))

    def psum(self, name, shape, dt, ctx=None):
        return (ctx or self.ctx).enter_context(self.nc.psum_tensor(name, shape, dt))

    def emit(self, stream, fn, reads=(), hard_reads=(), writes=(), dma=False, asyncwb=False,
             hard_in=False, cc=False):
        hard_reads = list(hard_reads) + list(reads)
        reads = ()
        if cc:
            semname_own = "cc"
        elif dma and stream == "pool":
            semname_own = "poold"
        else:
            semname_own = stream
        need = {}

        def add(sem, val):
            if val > need.get(sem, 0):
                need[sem] = val

        for b in hard_reads:
            if b.lw is not None:
                s2, v2, _ = b.lw
                add(s2, v2)
        for b in writes:
            if b.first:
                for s2, v2 in b.fence.items():
                    if s2 != semname_own and v2 > 0:
                        add(s2, v2)
            if b.lw is not None:
                s2, v2, hard = b.lw
                if s2 != semname_own or hard:
                    add(s2, v2)
            for s2, v2 in b.readers.items():
                if s2 != semname_own:
                    add(s2, v2)
        eng = self.eng[stream]
        for semname, val in sorted(need.items()):
            if self.waited[stream].get(semname, 0) < val:
                eng.wait_ge(self.sems[semname], val)
                self.waited[stream][semname] = val
        inst = fn(eng)
        inc = 16 if dma else 1
        inst.then_inc(self.sems[semname_own], inc)
        self.cnt[semname_own] += inc
        v = self.cnt[semname_own]
        for b in writes:
            b.lw = (semname_own, v, asyncwb)
            b.readers = {}
            b.first = False
        for b in list(reads) + list(hard_reads):
            if b.readers.get(semname_own, 0) < v:
                b.readers[semname_own] = v

    def final_wait(self, stream="sp"):
        for semname in self.SEMS:
            v = self.cnt[semname]
            if v > 0 and self.waited[stream].get(semname, 0) < v:
                self.eng[stream].wait_ge(self.sems[semname], v)
                self.waited[stream][semname] = v


DEBUG_DUMP = False
STOP_AFTER = None
PASSA_LEVEL = 4

WEIGHT_NAMES = [
    "L0_attn_a_r1_w", "L0_attn_a_r1_b", "L0_attn_a_ln_g", "L0_attn_a_ln_b",
    "L0_attn_a_r2_w", "L0_attn_a_r2_b", "L0_attn_a_temp", "L0_attn_a_route_bias",
    "L0_attn_a_W1", "L0_attn_a_b1", "L0_attn_a_W2", "L0_attn_a_b2",
    "L0_attn_b_Wq", "L0_attn_b_Wk", "L0_attn_b_Wv", "L0_attn_b_Wo",
    "L0_attn_b_bo", "L0_attn_b_Wf", "L0_attn_b_bf",
    "L0_ffn_weight", "L0_ffn_aux_weight1", "L0_ffn_aux_weight2", "L0_ffn_bias",
]


def build_program(n_sub=4, debug_out=None):
    nc = bass.Bass("TRN2", target_bir_lowering=False, debug=False)
    p = Prog(nc)
    dbg_extra = []
    nc_in = {}
    shapes = {
        "L0_attn_a_r1_w": [D, H], "L0_attn_a_r1_b": [H], "L0_attn_a_ln_g": [H],
        "L0_attn_a_ln_b": [H], "L0_attn_a_r2_w": [H, E], "L0_attn_a_r2_b": [E],
        "L0_attn_a_temp": [1], "L0_attn_a_route_bias": [E],
        "L0_attn_a_W1": [E, D, FF], "L0_attn_a_b1": [E, FF],
        "L0_attn_a_W2": [E, FF, D], "L0_attn_a_b2": [E, D],
        "L0_attn_b_Wq": [D, D], "L0_attn_b_Wk": [D, D], "L0_attn_b_Wv": [D, D],
        "L0_attn_b_Wo": [D, D], "L0_attn_b_bo": [D], "L0_attn_b_Wf": [HD, FD],
        "L0_attn_b_bf": [FD], "L0_ffn_weight": [D, D], "L0_ffn_aux_weight1": [D, D],
        "L0_ffn_aux_weight2": [D, D], "L0_ffn_bias": [D],
    }
    for n in WEIGHT_NAMES:
        nc_in[n] = nc.dram_tensor(n, shapes[n], F32, kind="ExternalInput").ap()
    x_in = nc.dram_tensor("x_shard", [TOK, D], F32, kind="ExternalInput").ap()
    ident_in = nc.dram_tensor("ident", [128, 128], F32, kind="ExternalInput").ap()
    y_out = nc.dram_tensor("y", [TOK, D], F32, kind="ExternalOutput").ap()
    cc_in = nc.dram_tensor("cc_in", [128, 8 * 130], F32, kind="Internal").ap()
    cc_out = nc.dram_tensor("cc_out", [128, 8 * 130], F32, kind="Internal").ap()

    resid = p.sb("resid", [128, NT * D], F32)
    resid3 = resid.ap().rearrange("q (t d) -> q t d", t=NT)
    xT = p.sb("xT", [128, DC * TOK], BF16)
    xT3 = xT.ap().rearrange("q (c n) -> q c n", c=DC)
    id_f32 = p.sb("id_f32", [128, 128], F32)
    id_bf = p.sb("id_bf", [128, 128], BF16)
    ones_bf = p.sb("ones_bf", [1, 128], BF16)
    ones_f32 = p.sb("ones_f32", [1, 128], F32)
    ones_col_bf = p.sb("ones_col_bf", [128, 1], BF16)
    pow_exp = p.sb("pow_exp", [128, 1], F32)
    g_bc = p.sb("g_bc", [128, H], F32)
    b_bc = p.sb("b_bc", [128, H], F32)
    rb_bc = p.sb("rb_bc", [128, E], F32)
    rt_col = p.sb("rt_col", [128, 1], F32)
    r1b_row = p.sb("r1b_row", [1, H], BF16)
    r2b_row = p.sb("r2b_row", [1, E], BF16)
    ffnb_row = p.sb("ffnb_row", [1, D], BF16)
    bo_row = p.sb("bo_row", [1, D], BF16)
    bf_col = p.sb("bf_col", [128, 1], F32)
    bf_row16 = p.sb("bf_row16", [1, D], BF16)
    wfpad_e = p.sb("wfpad_e", [128, FD], BF16)
    wfpad_o = p.sb("wfpad_o", [128, FD], BF16)
    wfpair = p.sb("wfpair", [128, 128], BF16)
    b1_cols = p.sb("b1_cols", [128, 72], F32)
    b2_sb = p.sb("b2_sb", [6, D], BF16)
    sctx = ExitStack()
    b1_72 = p.sb("b1_72", [72, 128], F32, sctx)
    temp_sb = p.sb("temp_sb", [1, 1], F32, sctx)
    rt_row = p.sb("rt_row", [1, 1], F32, sctx)
    grow = p.sb("grow", [1, H], F32, sctx)
    brow = p.sb("brow", [1, H], F32, sctx)
    rbrow = p.sb("rbrow", [1, E], F32, sctx)

    st6 = [p.sb(f"st6_{i}", [128, 12], F32) for i in range(2)]
    mv_all = p.sb("mv_all", [128, 2 * NT], F32)
    veps_all = p.sb("veps_all", [128, NT], F32)
    rstd_all = p.sb("rstd_all", [128, NT], F32)
    xln = [p.sb(f"xln_{i}", [128, D], BF16) for i in range(2)]
    logits_all = p.sb("logits_all", [128, 6 * NT], F32)
    probs_all = p.sb("probs_all", [128, 6 * NT], F32)
    probsT = p.sb("probsT", [6, TOK], BF16)
    mv2 = [p.sb(f"mv2_{i}", [128, 2], F32) for i in range(2)]
    veps2 = [p.sb(f"veps2_{i}", [128, 1], F32) for i in range(2)]
    rstd2 = [p.sb(f"rstd2_{i}", [128, 1], F32) for i in range(2)]
    t1 = [p.sb(f"t1_{i}", [128, H], F32) for i in range(2)]
    h2b = [p.sb(f"h2b_{i}", [128, H], BF16) for i in range(2)]
    h2T = [p.sb(f"h2T_{i}", [128, H], BF16) for i in range(2)]
    sm_e = [p.sb(f"sm_e_{i}", [128, 6], F32) for i in range(2)]
    sm_c = [p.sb(f"sm_c_{i}", [128, 6], F32) for i in range(2)]
    sm_m = [p.sb(f"sm_m_{i}", [128, 1], F32) for i in range(2)]
    sm_m2 = [p.sb(f"sm_m2_{i}", [128, 1], F32) for i in range(2)]
    sm_s = [p.sb(f"sm_s_{i}", [128, 1], F32) for i in range(2)]
    probs_bf = [p.sb(f"probs_bf_{i}", [128, 6], BF16) for i in range(2)]

    ps_big = [p.psum(f"ps_big{i}", [128, 1024], F32) for i in range(2)]
    ps_med = [p.psum(f"ps_med{i}", [128, 512], F32) for i in range(2)]
    ps_aux = [p.psum(f"ps_aux{i}", [128, 512], F32) for i in range(2)]

    B_resid = [p.buf(f"resid{t}") for t in range(NT)]
    B_xT = [p.buf(f"xT{t}") for t in range(NT)]
    B_const = p.buf("const")
    B_st6 = [p.buf(f"st6_{i}") for i in range(2)]
    B_mv = [p.buf(f"mv{t}") for t in range(NT)]
    B_veps = [p.buf(f"veps{t}") for t in range(NT)]
    B_rstd = [p.buf(f"rstd{t}") for t in range(NT)]
    B_xln = [p.buf(f"xln{i}") for i in range(2)]
    B_logits = [p.buf(f"logits{t}") for t in range(NT)]
    B_probs = [p.buf(f"probs{t}") for t in range(NT)]
    B_probsT = [p.buf(f"probsT{t}") for t in range(NT)]
    B_mv2 = [p.buf(f"mv2_{i}") for i in range(2)]
    B_veps2 = [p.buf(f"veps2_{i}") for i in range(2)]
    B_rstd2 = [p.buf(f"rstd2_{i}") for i in range(2)]
    B_t1 = [p.buf(f"t1_{i}") for i in range(2)]
    B_h2b = [p.buf(f"h2b_{i}") for i in range(2)]
    B_h2T = [p.buf(f"h2T_{i}") for i in range(2)]
    B_sm = [p.buf(f"sm{i}") for i in range(2)]
    B_pbf = [p.buf(f"pbf{i}") for i in range(2)]
    B_big = [p.buf(f"big{i}") for i in range(2)]
    B_med = [p.buf(f"med{i}") for i in range(2)]
    B_aux = [p.buf(f"aux{i}") for i in range(2)]

    EM = p.emit

    def dma(stream, out_ap, in_ap, writes, reads=(), **kw):
        EM(stream, lambda e: e.dma_start(out=out_ap, in_=in_ap, **kw),
           reads=reads, writes=writes, dma=True)

    dma("sp", resid3, x_in.rearrange("(t q) d -> q t d", q=128), writes=B_resid)
    dma("sp", id_f32[:], ident_in, writes=[B_const])
    dma("pool", id_bf[:], ident_in, writes=[B_const])
    dma("pool", r1b_row[:], nc_in["L0_attn_a_r1_b"][None, :], writes=[B_const])
    dma("pool", r2b_row[:], nc_in["L0_attn_a_r2_b"][None, :], writes=[B_const])
    dma("pool", ffnb_row[:], nc_in["L0_ffn_bias"][None, :], writes=[B_const])
    dma("pool", bo_row[:], nc_in["L0_attn_b_bo"][None, :], writes=[B_const])
    dma("pool", b2_sb[:], nc_in["L0_attn_a_b2"], writes=[B_const])
    EM("dve", lambda e: e.memset(wfpad_e[:], 0.0), writes=[B_const])
    EM("dve", lambda e: e.memset(wfpad_o[:], 0.0), writes=[B_const])
    EM("dve", lambda e: e.memset(wfpair[:], 0.0), writes=[B_const])
    dma("pool", wfpad_e[0:64, :], nc_in["L0_attn_b_Wf"], reads=[B_const], writes=[B_const])
    dma("pool", wfpad_o[64:128, :], nc_in["L0_attn_b_Wf"], reads=[B_const], writes=[B_const])
    dma("pool", wfpair[0:64, 0:64], nc_in["L0_attn_b_Wf"], reads=[B_const], writes=[B_const])
    dma("pool", wfpair[64:128, 64:128], nc_in["L0_attn_b_Wf"], reads=[B_const], writes=[B_const])
    for hh in range(NH):
        dma("pool", bf_row16[0:1, hh * FD:(hh + 1) * FD], nc_in["L0_attn_b_bf"][None, :],
            writes=[B_const])
    dma("sp", bf_col[0:64, :], nc_in["L0_attn_b_bf"][:, None], writes=[B_const])
    dma("sp", bf_col[64:128, :], nc_in["L0_attn_b_bf"][:, None], writes=[B_const])
    dma("sp", b1_72[:], nc_in["L0_attn_a_b1"].rearrange("e (c q) -> (e c) q", q=128),
        writes=[B_const])
    dma("sp", temp_sb[:], nc_in["L0_attn_a_temp"][None, :], writes=[B_const])
    dma("sp", grow[:], nc_in["L0_attn_a_ln_g"][None, :], writes=[B_const])
    dma("sp", brow[:], nc_in["L0_attn_a_ln_b"][None, :], writes=[B_const])
    dma("sp", rbrow[:], nc_in["L0_attn_a_route_bias"][None, :], writes=[B_const])

    EM("dve", lambda e: e.memset(ones_bf[:], 1.0), writes=[B_const])
    EM("dve", lambda e: e.memset(ones_f32[:], 1.0), writes=[B_const])
    EM("dve", lambda e: e.memset(ones_col_bf[:], 1.0), writes=[B_const])
    EM("dve", lambda e: e.memset(pow_exp[:], -0.5), writes=[B_const])
    EM("dve", lambda e: e.reciprocal(rt_row[:], temp_sb[:]), reads=[B_const], writes=[B_const])

    def bcast(row_ap, ncols, dst_ap):
        EM("pe", lambda e: e.matmul(ps_med[0][:, 0:ncols], ones_f32[:], row_ap,
                                    start=True, stop=True),
           reads=[B_const], writes=[B_med[0]])
        EM("act", lambda e: e.activation(dst_ap, ps_med[0][:, 0:ncols], AF.Copy),
           reads=[B_med[0]], writes=[B_const])

    bcast(grow[:], H, g_bc[:])
    bcast(brow[:], H, b_bc[:])
    bcast(rbrow[:], E, rb_bc[:])
    bcast(rt_row[:], 1, rt_col[:])
    EM("pe", lambda e: e.matmul(ps_med[1][:, 0:72], b1_72[:], id_f32[0:72, 0:72],
                                start=True, stop=True),
       reads=[B_const], writes=[B_med[1]])
    EM("act", lambda e: e.activation(b1_cols[:], ps_med[1][:, 0:72], AF.Copy),
       reads=[B_med[1]], writes=[B_const])
    sctx.close()
    p.fence()

    def ln_stats(t):
        s = t % 2
        EM("dve", lambda e: e.bn_stats(st6[s][:, 0:6], resid3[:, t, 0:512]),
           reads=[B_resid[t]], writes=[B_st6[s]], asyncwb=True, hard_in=True)
        EM("dve", lambda e: e.bn_stats(st6[s][:, 6:12], resid3[:, t, 512:1024]),
           reads=[B_resid[t]], writes=[B_st6[s]], asyncwb=True, hard_in=True)
        EM("dve", lambda e: e.bn_aggr(mv_all[:, 2 * t:2 * t + 2], st6[s][:]),
           reads=[B_st6[s]], writes=[B_mv[t]], asyncwb=True, hard_in=True)
        EM("dve", lambda e: e.tensor_scalar(veps_all[:, t:t + 1], mv_all[:, 2 * t + 1:2 * t + 2],
                                            EPS, None, op0=OP.add),
           reads=[B_mv[t]], writes=[B_veps[t]])
        EM("pool", lambda e: e.tensor_tensor(rstd_all[:, t:t + 1], veps_all[:, t:t + 1],
                                             pow_exp[:], op=OP.pow),
           reads=[B_veps[t], B_const], writes=[B_rstd[t]], asyncwb=True, hard_in=True)

    def ln_apply_transpose(t):
        s = t % 2
        EM("dve", lambda e: e.tensor_scalar(xln[s][:], resid3[:, t, :],
                                            mv_all[:, 2 * t:2 * t + 1],
                                            rstd_all[:, t:t + 1],
                                            op0=OP.subtract, op1=OP.mult),
           reads=[B_resid[t]], hard_reads=[B_mv[t], B_rstd[t]], writes=[B_xln[s]])
        bp = B_big[t % 2]
        for c in range(DC):
            EM("pe", lambda e, c=c: e.matmul(ps_big[t % 2][:, c * 128:(c + 1) * 128],
                                             xln[s][:, c * 128:(c + 1) * 128], id_bf[:],
                                             start=True, stop=True),
               reads=[B_xln[s], B_const], writes=[bp])
        EM("act", lambda e: e.activation(
            xT3[:, :, t * 128:(t + 1) * 128],
            ps_big[t % 2].ap().rearrange("q (c n) -> q c n", c=8), AF.Copy),
           reads=[bp], writes=[B_xT[t]])

    def ln_pass():
        if STOP_AFTER == "stats":
            for t in range(NT):
                ln_stats(t)
            return
        for t in range(NT):
            ln_stats(t)
            if t >= 2:
                ln_apply_transpose(t - 2)
        ln_apply_transpose(NT - 2)
        ln_apply_transpose(NT - 1)

    def attn_a():
        actx = ExitStack()
        r1w = p.sb("r1w", [128, DC, H], BF16, actx)
        r2w = p.sb("r2w", [128, 4, E], BF16, actx)
        B_r1w, B_r2w = p.buf("r1w"), p.buf("r2w")
        dma("pool", r1w[:], nc_in["L0_attn_a_r1_w"].rearrange("(c q) h -> q c h", q=128),
            writes=[B_r1w])
        dma("pool", r2w[:], nc_in["L0_attn_a_r2_w"].rearrange("(c q) s -> q c s", q=128),
            writes=[B_r2w])

        ln_pass()
        if STOP_AFTER in ("ln", "stats"):
            actx.close(); p.fence(); return

        for t in range(NT):
            s = t % 2
            hb = B_med[s]
            for c in range(DC):
                EM("pe", lambda e, c=c: e.matmul(ps_med[s][:], xT3[:, c, t * 128:(t + 1) * 128],
                                                 r1w[:, c, :], start=(c == 0), stop=False),
                   reads=[B_xT[t], B_r1w], writes=[hb])
            EM("pe", lambda e: e.matmul(ps_med[s][:], ones_bf[:], r1b_row[:],
                                        start=False, stop=True),
               reads=[B_const], writes=[hb])
            EM("dve", lambda e: e.bn_stats(st6[s][:, 0:6], ps_med[s][:]),
               reads=[hb], writes=[B_st6[s]], asyncwb=True, hard_in=True)
            EM("dve", lambda e: e.bn_aggr(mv2[s][:], st6[s][:, 0:6]),
               reads=[B_st6[s]], writes=[B_mv2[s]], asyncwb=True, hard_in=True)
            EM("dve", lambda e: e.tensor_scalar(veps2[s][:], mv2[s][:, 1:2], EPS, None, op0=OP.add),
               reads=[B_mv2[s]], writes=[B_veps2[s]])
            EM("pool", lambda e: e.tensor_tensor(rstd2[s][:], veps2[s][:], pow_exp[:], op=OP.pow),
               reads=[B_veps2[s], B_const], writes=[B_rstd2[s]], asyncwb=True, hard_in=True)
            EM("dve", lambda e: e.tensor_scalar(t1[s][:], ps_med[s][:], mv2[s][:, 0:1],
                                                rstd2[s][:], op0=OP.subtract, op1=OP.mult),
               reads=[hb], hard_reads=[B_mv2[s], B_rstd2[s]], writes=[B_t1[s]])
            EM("dve", lambda e: e.tensor_tensor(t1[s][:], t1[s][:], g_bc[:], op=OP.mult),
               reads=[B_t1[s], B_const], writes=[B_t1[s]])
            EM("dve", lambda e: e.tensor_tensor(t1[s][:], t1[s][:], b_bc[:], op=OP.add),
               reads=[B_t1[s], B_const], writes=[B_t1[s]])
            EM("act", lambda e: e.activation(h2b[s][:], t1[s][:], AF.Gelu),
               reads=[B_t1[s]], writes=[B_h2b[s]])
            bg = B_big[s]
            for c in range(4):
                EM("pe", lambda e, c=c: e.matmul(ps_big[s][:, c * 128:(c + 1) * 128],
                                                 h2b[s][:, c * 128:(c + 1) * 128], id_bf[:],
                                                 start=True, stop=True),
                   reads=[B_h2b[s], B_const], writes=[bg])
            EM("act", lambda e: e.activation(h2T[s][:], ps_big[s][:, 0:512], AF.Copy),
               reads=[bg], writes=[B_h2T[s]])
            ab = B_aux[s]
            for c in range(4):
                EM("pe", lambda e, c=c: e.matmul(ps_aux[s][:, 0:E],
                                                 h2T[s][:, c * 128:(c + 1) * 128],
                                                 r2w[:, c, :], start=(c == 0), stop=False),
                   reads=[B_h2T[s], B_r2w], writes=[ab])
            EM("pe", lambda e: e.matmul(ps_aux[s][:, 0:E], ones_bf[:], r2b_row[:],
                                        start=False, stop=True),
               reads=[B_const], writes=[ab])
            EM("dve", lambda e: e.scalar_tensor_tensor(
                logits_all[:, 6 * t:6 * t + 6], ps_aux[s][:, 0:E], rt_col[:, 0:1], rb_bc[:],
                op0=OP.mult, op1=OP.add),
               reads=[ab, B_const], hard_reads=[B_const], writes=[B_logits[t]])

        if STOP_AFTER == "router":
            actx.close(); p.fence(); return
        for t in range(NT):
            s = t % 2
            lsl = logits_all[:, 6 * t:6 * t + 6]
            psl = probs_all[:, 6 * t:6 * t + 6]
            EM("dve", lambda e: e.reduce_max(sm_m[s][:], lsl, axis=AX.X),
               reads=[B_logits[t]], writes=[B_sm[s]], asyncwb=True, hard_in=True)
            EM("dve", lambda e: e.tensor_scalar(sm_m[s][:], sm_m[s][:], -1.0, None, op0=OP.mult),
               reads=[B_sm[s]], writes=[B_sm[s]])
            EM("act", lambda e: e.activation(sm_e[s][:], lsl, AF.Exp, bias=sm_m[s][:, 0:1],
                                             scale=1.0),
               reads=[B_logits[t]], hard_reads=[B_sm[s]], writes=[B_sm[s]])
            EM("dve", lambda e: e.reduce_sum(sm_s[s][:], sm_e[s][:], axis=AX.X),
               reads=[B_sm[s]], writes=[B_sm[s]], asyncwb=True, hard_in=True)
            EM("dve", lambda e: e.reciprocal(sm_s[s][:], sm_s[s][:]),
               reads=[B_sm[s]], writes=[B_sm[s]])
            EM("dve", lambda e: e.tensor_scalar(psl, sm_e[s][:], sm_s[s][:, 0:1], None,
                                                op0=OP.mult),
               reads=[B_sm[s]], hard_reads=[B_sm[s]], writes=[B_probs[t]])
            EM("dve", lambda e: e.reduce_max(sm_m[s][:], psl, axis=AX.X),
               reads=[B_probs[t]], writes=[B_sm[s]], asyncwb=True, hard_in=True)
            EM("dve", lambda e: e.tensor_scalar(sm_c[s][:], psl, sm_m[s][:, 0:1], None,
                                                op0=OP.is_equal),
               reads=[B_probs[t]], hard_reads=[B_sm[s]], writes=[B_sm[s]])
            EM("dve", lambda e: e.scalar_tensor_tensor(sm_e[s][:], sm_c[s][:], -1e30, psl,
                                                       op0=OP.mult, op1=OP.add),
               reads=[B_sm[s], B_probs[t]], writes=[B_sm[s]])
            EM("dve", lambda e: e.reduce_max(sm_m2[s][:], sm_e[s][:], axis=AX.X),
               reads=[B_sm[s]], writes=[B_sm[s]], asyncwb=True, hard_in=True)
            EM("dve", lambda e: e.tensor_scalar(sm_c[s][:], psl, sm_m2[s][:, 0:1], None,
                                                op0=OP.is_ge),
               reads=[B_probs[t]], hard_reads=[B_sm[s]], writes=[B_sm[s]])
            EM("dve", lambda e: e.tensor_tensor(psl, psl, sm_c[s][:], op=OP.mult),
               reads=[B_probs[t], B_sm[s]], writes=[B_probs[t]])
            EM("dve", lambda e: e.tensor_copy(probs_bf[s][:], psl),
               reads=[B_probs[t]], writes=[B_pbf[s]])
            EM("pe", lambda e: e.matmul(ps_aux[s][0:6, 0:128], probs_bf[s][:], id_bf[:],
                                        start=True, stop=True),
               reads=[B_pbf[s], B_const], writes=[B_aux[s]])
            EM("act", lambda e: e.activation(probsT.ap()[:, t * 128:(t + 1) * 128],
                                             ps_aux[s][0:6, 0:128], AF.Copy),
               reads=[B_aux[s]], writes=[B_probsT[t]])

        if STOP_AFTER == "softmax":
            actx.close(); p.fence(); return
        for t in range(NT):
            s = t % 2
            bg = B_big[s]
            for nh2 in range(2):
                EM("pe", lambda e, nh2=nh2: e.matmul(
                    ps_big[s][:, nh2 * 512:(nh2 + 1) * 512],
                    probsT.ap()[:, t * 128:(t + 1) * 128],
                    b2_sb[:, nh2 * 512:(nh2 + 1) * 512], start=True, stop=True),
                   reads=[B_probsT[t], B_const], writes=[bg])
            EM("dve", lambda e: e.tensor_tensor(resid3[:, t, :], resid3[:, t, :],
                                                ps_big[s][:], op=OP.add),
               reads=[B_resid[t], bg], writes=[B_resid[t]])

        if STOP_AFTER == "b2":
            actx.close(); p.fence(); return
        w1b = [p.sb(f"w1b{i}", [128, DC, FH], BF16, actx) for i in range(2)]
        w2b = [p.sb(f"w2b{i}", [128, 6, D], BF16, actx) for i in range(2)]
        actb = [p.sb(f"actb{i}", [128, 6, 512], BF16, actx) for i in range(2)]
        B_w1 = [p.buf(f"w1b{i}") for i in range(2)]
        B_w2 = [p.buf(f"w2b{i}") for i in range(2)]
        B_act = [p.buf(f"actb{i}") for i in range(2)]

        order = [0, 2, 4, 1, 3, 5]
        steps = [(eo, fh) for eo in order for fh in range(2)]

        def prefetch(i):
            ex, fh = steps[i]
            sl = i % 2
            dma("pool", w1b[sl][:],
                nc_in["L0_attn_a_W1"][ex, :, fh * FH:(fh + 1) * FH]
                .rearrange("(c q) f -> q c f", q=128), writes=[B_w1[sl]])
            dma("pool", w2b[sl][:],
                nc_in["L0_attn_a_W2"][ex, fh * FH:(fh + 1) * FH, :]
                .rearrange("(c q) n -> q c n", q=128), writes=[B_w2[sl]])

        def w1_group(i, g):
            ex, fh = steps[i]
            sl = i % 2
            func = AF.Gelu if ex % 2 == 0 else AF.Silu
            asl = g % 2
            for c in range(6):
                mb = B_med[c % 2]
                for kc in range(DC):
                    EM("pe", lambda e, c=c, kc=kc, g=g: e.matmul(
                        ps_med[c % 2][:], w1b[sl][:, kc, c * 128:(c + 1) * 128],
                        xT3[:, kc, g * 512:(g + 1) * 512],
                        start=(kc == 0), stop=(kc == DC - 1)),
                       reads=[B_w1[sl]] + [B_xT[4 * g + j] for j in range(4)],
                       writes=[mb])
                bcol = b1_cols[:, ex * 12 + fh * 6 + c: ex * 12 + fh * 6 + c + 1]
                EM("act", lambda e, c=c, g=g, func=func, bcol=bcol: e.activation(
                    actb[asl][:, c, :], ps_med[c % 2][:], func, bias=bcol, scale=1.0),
                   reads=[mb], hard_reads=[B_const], writes=[B_act[asl]])

        def eo_group(i, g):
            ex, fh = steps[i]
            sl = i % 2
            asl = g % 2
            for bb in range(4):
                t = 4 * g + bb
                bg = B_big[t % 2]
                for nh2 in range(2):
                    for c in range(6):
                        EM("pe", lambda e, c=c, nh2=nh2, bb=bb, g=g: e.matmul(
                            ps_big[t % 2][:, nh2 * 512:(nh2 + 1) * 512],
                            actb[asl][:, c, bb * 128:(bb + 1) * 128],
                            w2b[sl][:, c, nh2 * 512:(nh2 + 1) * 512],
                            start=(c == 0), stop=(c == 5)),
                           reads=[B_act[asl], B_w2[sl]], writes=[bg])
                EM("dve", lambda e, t=t: e.scalar_tensor_tensor(
                    resid3[:, t, :], ps_big[t % 2][:],
                    probs_all[:, 6 * t + ex: 6 * t + ex + 1], resid3[:, t, :],
                    op0=OP.mult, op1=OP.add),
                   reads=[bg, B_resid[t]], hard_reads=[B_probs[t]],
                   writes=[B_resid[t]])

        prefetch(0)
        for i in range(len(steps)):
            if i + 1 < len(steps):
                prefetch(i + 1)
            for g in range(NG):
                w1_group(i, g)
                eo_group(i, g)
        actx.close()
        p.fence()

    ffn_calls = [0]

    def ffn():
        tag = f"f{ffn_calls[0]}"
        ffn_calls[0] += 1
        fctx = ExitStack()
        wb = [p.sb(f"{tag}w{i}", [128, DC, D], BF16, fctx) for i in range(3)]
        B_wb = [p.buf(f"{tag}w{i}") for i in range(3)]
        h1sb = [p.sb(f"{tag}h1sb{i}", [128, DC, 512], BF16, fctx) for i in range(2)]
        h2sb = [p.sb(f"{tag}h2sb0", [128, DC, 512], BF16, fctx)] * 2
        tsb = [p.sb(f"{tag}tsb0", [128, D], F32, fctx)] * 2
        B_h1 = [p.buf(f"{tag}h1sb{i}") for i in range(2)]
        B_h2 = [p.buf(f"{tag}h2sb0")] * 2
        B_t = [p.buf(f"{tag}tsb0")] * 2
        for wi, wn in enumerate(["L0_ffn_weight", "L0_ffn_aux_weight1", "L0_ffn_aux_weight2"]):
            dma("pool", wb[wi][:], nc_in[wn].rearrange("(c q) n -> q c n", q=128),
                writes=[B_wb[wi]])

        ln_pass()

        def mm_fm(dst_sb, dst_buf, src_fn, src_bufs, w_i):
            for oc in range(DC):
                mb = B_med[oc % 2]
                for kc in range(DC):
                    EM("pe", lambda e, oc=oc, kc=kc: e.matmul(
                        ps_med[oc % 2][:], wb[w_i][:, kc, oc * 128:(oc + 1) * 128],
                        src_fn(kc), start=(kc == 0), stop=(kc == DC - 1)),
                       reads=[B_wb[w_i]] + src_bufs, writes=[mb])
                EM("act", lambda e, oc=oc: e.activation(dst_sb[:, oc, :], ps_med[oc % 2][:],
                                                        AF.Copy),
                   reads=[mb], writes=[dst_buf])

        for g in range(NG):
            sl = g % 2
            src_bufs = [B_xT[4 * g + j] for j in range(4)]
            mm_fm(h1sb[sl], B_h1[sl],
                  lambda kc, g=g: xT3[:, kc, g * 512:(g + 1) * 512], src_bufs, 0)
            mm_fm(h2sb[sl], B_h2[sl],
                  lambda kc, sl=sl: h1sb[sl][:, kc, :], [B_h1[sl]], 1)
            for bb in range(4):
                t = 4 * g + bb
                bg = B_big[t % 2]
                for nh2 in range(2):
                    for kc in range(DC):
                        EM("pe", lambda e, kc=kc, nh2=nh2, bb=bb: e.matmul(
                            ps_big[t % 2][:, nh2 * 512:(nh2 + 1) * 512],
                            h2sb[sl][:, kc, bb * 128:(bb + 1) * 128],
                            wb[2][:, kc, nh2 * 512:(nh2 + 1) * 512],
                            start=(kc == 0), stop=False),
                           reads=[B_h2[sl], B_wb[2]], writes=[bg])
                    EM("pe", lambda e, nh2=nh2: e.matmul(
                        ps_big[t % 2][:, nh2 * 512:(nh2 + 1) * 512], ones_bf[:],
                        ffnb_row[0:1, nh2 * 512:(nh2 + 1) * 512], start=False, stop=True),
                       reads=[B_const], writes=[bg])
                EM("act", lambda e, t=t: e.activation(tsb[t % 2][:], ps_big[t % 2][:], AF.Tanh),
                   reads=[bg], writes=[B_t[t % 2]])
                EM("dve", lambda e, t=t: e.tensor_tensor(resid3[:, t, :], resid3[:, t, :],
                                                         tsb[t % 2][:], op=OP.add),
                   reads=[B_resid[t], B_t[t % 2]], writes=[B_resid[t]])
        fctx.close()
        p.fence()

    def attn_b():
        bctx = ExitStack()
        wk = p.sb("wk", [128, DC, D], BF16, bctx)
        wv = p.sb("wv", [128, DC, D], BF16, bctx)
        B_wk, B_wv = p.buf("wk"), p.buf("wv")
        ksb = [p.sb(f"ksb{i}", [128, DC, 512], BF16, bctx) for i in range(2)]
        kfe = [p.sb(f"kfe{i}", [128, D], BF16, bctx) for i in range(2)]
        kfo = [p.sb(f"kfo{i}", [128, D], BF16, bctx) for i in range(2)]
        vev = [p.sb(f"vev{i}", [128, D], BF16, bctx) for i in range(2)]
        vod = [p.sb(f"vod{i}", [128, D], BF16, bctx) for i in range(2)]
        kv_all = p.sb("kv_all", [128, 8 * 130], F32, bctx)
        B_ksb = [p.buf(f"ksb{i}") for i in range(2)]
        B_kfe = [p.buf(f"kfe{i}") for i in range(2)]
        B_kfo = [p.buf(f"kfo{i}") for i in range(2)]
        B_vev = [p.buf(f"vev{i}") for i in range(2)]
        B_vod = [p.buf(f"vod{i}") for i in range(2)]
        B_kv_all = p.buf("kv_all")
        B_ccin, B_ccout = p.buf("cc_in"), p.buf("cc_out")
        for i in range(2):
            for bufT, bufB in ((kfe[i], B_kfe[i]), (kfo[i], B_kfo[i]),
                               (vev[i], B_vev[i]), (vod[i], B_vod[i])):
                EM("dve", lambda e, bufT=bufT: e.memset(bufT[:], 0.0), writes=[bufB])
        pview = lambda pst: pst.ap().rearrange("q (pr j) -> q pr j", pr=8)
        ev3 = lambda sbT: sbT.ap().rearrange("q (pr j) -> q pr j", pr=8)

        dma("pool", wk[:], nc_in["L0_attn_b_Wk"].rearrange("(c q) n -> q c n", q=128),
            writes=[B_wk])
        dma("pool", wv[:], nc_in["L0_attn_b_Wv"].rearrange("(c q) n -> q c n", q=128),
            writes=[B_wv])

        ln_pass()
        if STOP_AFTER == "bln":
            bctx.close(); p.fence(); return

        kv3 = kv_all.ap().rearrange("q (pr j) -> q pr j", pr=8)
        EM("dve", lambda e: e.memset(kv_all[:], 0.0), writes=[B_kv_all])
        B_ksump = B_aux[1]
        for g in range(NG):
            sl = g % 2
            for oc in range(DC):
                mb = B_med[oc % 2]
                for kc in range(DC):
                    EM("pe", lambda e, oc=oc, kc=kc, g=g: e.matmul(
                        ps_med[oc % 2][:], wk[:, kc, oc * 128:(oc + 1) * 128],
                        xT3[:, kc, g * 512:(g + 1) * 512],
                        start=(kc == 0), stop=(kc == DC - 1)),
                       reads=[B_wk] + [B_xT[4 * g + j] for j in range(4)], writes=[mb])
                EM("act", lambda e, oc=oc: e.activation(ksb[sl][:, oc, :], ps_med[oc % 2][:],
                                                        AF.Copy),
                   reads=[mb], writes=[B_ksb[sl]])
            for bb in range(4):
                t = 4 * g + bb
                ts = t % 2
                vb = B_big[ts]
                for nh2 in range(2):
                    for kc in range(DC):
                        EM("pe", lambda e, kc=kc, nh2=nh2, t=t: e.matmul(
                            ps_big[ts][:, nh2 * 512:(nh2 + 1) * 512],
                            xT3[:, kc, t * 128:(t + 1) * 128],
                            wv[:, kc, nh2 * 512:(nh2 + 1) * 512],
                            start=(kc == 0), stop=(kc == DC - 1)),
                           reads=[B_xT[t], B_wv], writes=[vb])
                EM("act", lambda e, ts=ts: e.activation(
                    ev3(vev[ts])[:, :, 0:64], pview(ps_big[ts])[:, :, 0:64], AF.Copy),
                   reads=[vb], writes=[B_vev[ts]])
                EM("act", lambda e, ts=ts: e.activation(
                    ev3(vod[ts])[:, :, 64:128], pview(ps_big[ts])[:, :, 64:128], AF.Copy),
                   reads=[vb], writes=[B_vod[ts]])
                if PASSA_LEVEL < 2:
                    continue
                for pr in range(8):
                    for par in range(2):
                        wf_p = wfpad_e if par == 0 else wfpad_o
                        EM("pe", lambda e, pr=pr, par=par, bb=bb, sl=sl, wf_p=wf_p: e.matmul(
                            ps_big[ts][:, pr * 128 + par * 64: pr * 128 + par * 64 + 64],
                            ksb[sl][:, pr, bb * 128:(bb + 1) * 128],
                            wf_p[:], start=((pr == 0 or pr == 4) and par == 0),
                            stop=False, skip_group_check=True),
                           reads=[B_ksb[sl], B_const], writes=[vb])
                for nh2 in range(2):
                    EM("pe", lambda e, nh2=nh2, ts=ts: e.matmul(
                        ps_big[ts][:, nh2 * 512:(nh2 + 1) * 512], ones_bf[:],
                        bf_row16[0:1, nh2 * 512:(nh2 + 1) * 512], start=False, stop=True,
                        skip_group_check=True),
                       reads=[B_const], writes=[vb])
                EM("act", lambda e, ts=ts: e.activation(
                    ev3(kfe[ts])[:, :, 0:64], pview(ps_big[ts])[:, :, 0:64], AF.Relu),
                   reads=[vb], writes=[B_kfe[ts]])
                EM("act", lambda e, ts=ts: e.activation(
                    ev3(kfo[ts])[:, :, 64:128], pview(ps_big[ts])[:, :, 64:128], AF.Relu),
                   reads=[vb], writes=[B_kfo[ts]])
                if PASSA_LEVEL < 3:
                    continue
                kb = (ts + 1) % 2
                B_kvp = B_big[kb]
                for pr in range(8):
                    EM("pe", lambda e, pr=pr, ts=ts, kb=kb: e.matmul(
                        ps_big[kb][:, pr * 128:(pr + 1) * 128],
                        kfe[ts][:, pr * 128:(pr + 1) * 128],
                        vev[ts][:, pr * 128:(pr + 1) * 128],
                        start=(pr == 0 or pr == 4), stop=False, skip_group_check=True),
                       reads=[B_kfe[ts], B_vev[ts]], writes=[B_kvp])
                    EM("pe", lambda e, pr=pr, ts=ts, kb=kb: e.matmul(
                        ps_big[kb][:, pr * 128:(pr + 1) * 128],
                        kfo[ts][:, pr * 128:(pr + 1) * 128],
                        vod[ts][:, pr * 128:(pr + 1) * 128],
                        start=False, stop=(pr == 7), skip_group_check=True),
                       reads=[B_kfo[ts], B_vod[ts]], writes=[B_kvp])
                    EM("pe", lambda e, pr=pr, ts=ts: e.matmul(
                        ps_aux[1][:, pr * 2: pr * 2 + 1],
                        kfe[ts][:, pr * 128:(pr + 1) * 128], ones_col_bf[:],
                        start=(pr == 0), stop=False, skip_group_check=True),
                       reads=[B_kfe[ts], B_const], writes=[B_ksump])
                    EM("pe", lambda e, pr=pr, ts=ts: e.matmul(
                        ps_aux[1][:, pr * 2 + 1: pr * 2 + 2],
                        kfo[ts][:, pr * 128:(pr + 1) * 128], ones_col_bf[:],
                        start=False, stop=(pr == 7), skip_group_check=True),
                       reads=[B_kfo[ts], B_const], writes=[B_ksump])
                EM("dve", lambda e, kb=kb: e.tensor_tensor(
                    kv3[:, :, 0:128], kv3[:, :, 0:128],
                    ps_big[kb].ap().rearrange("q (pr j) -> q pr j", pr=8), op=OP.add),
                   reads=[B_kvp, B_kv_all], writes=[B_kv_all])
                EM("dve", lambda e: e.tensor_tensor(
                    kv3[:, :, 128:130], kv3[:, :, 128:130],
                    ps_aux[1].ap()[:, 0:16].rearrange("q (pr j) -> q pr j", pr=8), op=OP.add),
                   reads=[B_ksump, B_kv_all], writes=[B_kv_all])

        if STOP_AFTER == "kvacc":
            dbg_extra.append(("dbg_kv", kv_all, [128, 8 * 130], F32))
            return
        dma("sp", cc_in, kv_all[:], reads=[B_kv_all], writes=[B_ccin])
        EM("pool", lambda e: e.collective_compute(
            "AllReduce", OP.add, replica_groups=[[0, 1], [2, 3], [4, 5], [6, 7]],
            ins=[cc_in.opt()], outs=[cc_out.opt()]),
           reads=[B_ccin], writes=[B_ccout], asyncwb=True, hard_in=True, cc=True)

        bctx.close()
        p.fence()
        bctx = ExitStack()
        wq = p.sb("wq", [128, DC, D], BF16, bctx)
        wo = p.sb("wo", [128, DC, D], BF16, bctx)
        B_wq, B_wo = p.buf("wq"), p.buf("wo")
        dma("pool", wq[:], nc_in["L0_attn_b_Wq"].rearrange("(c q) n -> q c n", q=128),
            writes=[B_wq])
        dma("pool", wo[:], nc_in["L0_attn_b_Wo"].rearrange("(c q) n -> q c n", q=128),
            writes=[B_wo])

        kvt = p.sb("kvt", [128, 8 * 130], F32, bctx)
        kvt_bf = p.sb("kvt_bf", [128, 8 * 130], BF16, bctx)
        B_kvt, B_kvtbf = p.buf("kvt"), p.buf("kvt_bf")
        dma("sp", kvt[:], cc_out, reads=[B_ccout], writes=[B_kvt])
        EM("dve", lambda e: e.tensor_copy(kvt_bf[:], kvt[:]), reads=[B_kvt], writes=[B_kvtbf])
        kvt3 = kvt_bf.ap().rearrange("q (pr j) -> q pr j", pr=8)
        if STOP_AFTER == "kv":
            dbg_extra.append(("dbg_kv", kvt, [128, 8 * 130], F32))
            return

        qsb = [p.sb(f"qsb{i}", [128, 512], BF16, bctx) for i in range(2)]
        qfsb = [p.sb(f"qfsb{i}", [128, DC, 512], BF16, bctx) for i in range(2)]
        nrm = [p.sb(f"nrm{i}", [128, NH], F32, bctx) for i in range(2)]
        aob = [p.sb(f"aob{i}", [128, D], BF16, bctx) for i in range(2)]
        aoT = [p.sb(f"aoT{i}", [128, D], BF16, bctx) for i in range(2)]
        B_qsb = [p.buf(f"qsb{i}") for i in range(2)]
        B_qf = [p.buf(f"qfsb{i}") for i in range(2)]
        B_nrm = [p.buf(f"nrm{i}") for i in range(2)]
        B_ao = [p.buf(f"aob{i}") for i in range(2)]
        B_aoT = [p.buf(f"aoT{i}") for i in range(2)]

        for g in range(NG):
            sl = g % 2
            for oc in range(DC):
                mb = B_med[oc % 2]
                qb = B_qsb[oc % 2]
                for kc in range(DC):
                    EM("pe", lambda e, oc=oc, kc=kc, g=g: e.matmul(
                        ps_med[oc % 2][:], wq[:, kc, oc * 128:(oc + 1) * 128],
                        xT3[:, kc, g * 512:(g + 1) * 512],
                        start=(kc == 0), stop=(kc == DC - 1)),
                       reads=[B_wq] + [B_xT[4 * g + j] for j in range(4)], writes=[mb])
                EM("act", lambda e, oc=oc: e.activation(qsb[oc % 2][:], ps_med[oc % 2][:],
                                                        AF.Copy),
                   reads=[mb], writes=[qb])
                ax = B_aux[oc % 2]
                EM("pe", lambda e, oc=oc: e.matmul(
                    ps_aux[oc % 2][:], wfpair[:], qsb[oc % 2][:],
                    start=True, stop=True),
                   reads=[qb, B_const], writes=[ax])
                EM("act", lambda e, oc=oc, sl=sl: e.activation(
                    qfsb[sl][:, oc, :], ps_aux[oc % 2][:], AF.Relu, bias=bf_col[:, 0:1],
                    scale=1.0),
                   reads=[ax], hard_reads=[B_const], writes=[B_qf[sl]])
            for bb in range(4):
                t = 4 * g + bb
                ts = t % 2
                bg = B_big[ts]
                ax = B_aux[ts]
                for pr in range(8):
                    EM("pe", lambda e, pr=pr, bb=bb, sl=sl, ts=ts: e.matmul(
                        ps_big[ts][:, pr * 128:(pr + 1) * 128],
                        qfsb[sl][:, pr, bb * 128:(bb + 1) * 128],
                        kvt3[:, pr, 0:128], start=True, stop=True,
                        skip_group_check=True),
                       reads=[B_qf[sl], B_kvtbf], writes=[bg])
                    EM("pe", lambda e, pr=pr, bb=bb, sl=sl, ts=ts: e.matmul(
                        ps_aux[ts][:, pr * 2: pr * 2 + 2],
                        qfsb[sl][:, pr, bb * 128:(bb + 1) * 128],
                        kvt3[:, pr, 128:130], start=True, stop=True,
                        skip_group_check=True),
                       reads=[B_qf[sl], B_kvtbf], writes=[ax])
                EM("dve", lambda e, ts=ts: e.tensor_scalar(nrm[ts][:], ps_aux[ts][:, 0:NH],
                                                           1e-6, None, op0=OP.max),
                   reads=[ax], writes=[B_nrm[ts]])
                EM("dve", lambda e, ts=ts: e.reciprocal(nrm[ts][:], nrm[ts][:]),
                   reads=[B_nrm[ts]], writes=[B_nrm[ts]], asyncwb=True, hard_in=True)
                for hh in range(NH):
                    EM("dve", lambda e, hh=hh, ts=ts: e.tensor_scalar(
                        aob[ts][:, hh * 64:(hh + 1) * 64], ps_big[ts][:, hh * 64:(hh + 1) * 64],
                        nrm[ts][:, hh:hh + 1], None, op0=OP.mult),
                       reads=[bg], hard_reads=[B_nrm[ts]], writes=[B_ao[ts]])
                for c in range(DC):
                    EM("pe", lambda e, c=c, ts=ts: e.matmul(
                        ps_big[ts][:, c * 128:(c + 1) * 128], aob[ts][:, c * 128:(c + 1) * 128],
                        id_bf[:], start=True, stop=True),
                       reads=[B_ao[ts], B_const], writes=[bg])
                EM("act", lambda e, ts=ts: e.activation(aoT[ts][:], ps_big[ts][:], AF.Copy),
                   reads=[bg], writes=[B_aoT[ts]])
                for nh2 in range(2):
                    for kc in range(DC):
                        EM("pe", lambda e, kc=kc, nh2=nh2, ts=ts: e.matmul(
                            ps_big[ts][:, nh2 * 512:(nh2 + 1) * 512],
                            aoT[ts][:, kc * 128:(kc + 1) * 128],
                            wo[:, kc, nh2 * 512:(nh2 + 1) * 512],
                            start=(kc == 0), stop=False),
                           reads=[B_aoT[ts], B_wo], writes=[bg])
                    EM("pe", lambda e, nh2=nh2, ts=ts: e.matmul(
                        ps_big[ts][:, nh2 * 512:(nh2 + 1) * 512], ones_bf[:],
                        bo_row[0:1, nh2 * 512:(nh2 + 1) * 512], start=False, stop=True),
                       reads=[B_const], writes=[bg])
                EM("dve", lambda e, t=t, ts=ts: e.tensor_tensor(
                    resid3[:, t, :], resid3[:, t, :], ps_big[ts][:], op=OP.add),
                   reads=[B_resid[t], bg], writes=[B_resid[t]])
        bctx.close()
        p.fence()


    subs = [attn_a, ffn, attn_b, ffn]
    for i in range(n_sub):
        subs[i]()

    B_y = p.buf("y")
    dma("sp", y_out.rearrange("(t q) d -> q t d", q=128), resid3,
        reads=B_resid, writes=[B_y])
    if DEBUG_DUMP:
        p.fence()
        for name, sb_t, shape, dt in [
            ("dbg_logits", logits_all, [128, 6 * NT], F32),
            ("dbg_probs", probs_all, [128, 6 * NT], F32),
            ("dbg_mv", mv_all, [128, 2 * NT], F32),
            ("dbg_rstd", rstd_all, [128, NT], F32),
            ("dbg_xT", xT, [128, DC * TOK], BF16),
            ("dbg_probsT", probsT, [6, TOK], BF16),
            ("dbg_sm_e", sm_e[1], [128, 6], F32),
            ("dbg_sm_c", sm_c[1], [128, 6], F32),
            ("dbg_sm_m", sm_m[1], [128, 1], F32),
            ("dbg_sm_m2", sm_m2[1], [128, 1], F32),
            ("dbg_sm_s", sm_s[1], [128, 1], F32),
        ] + dbg_extra:
            o = nc.dram_tensor(name, shape, dt, kind="ExternalOutput").ap()
            dma("sp", o, sb_t[:], reads=[], writes=[p.buf(name)])
    p.final_wait("sp")
    return nc


def kernel(**inputs):
    x = np.ascontiguousarray(np.asarray(inputs["L0_attn_a_x"], dtype=np.float32)
                             .reshape(B * L, D))
    common = {n: np.ascontiguousarray(np.asarray(inputs[n], dtype=np.float32))
              for n in WEIGHT_NAMES}
    ident = np.eye(128, dtype=np.float32)
    nc = build_program(4)
    in_maps = []
    for c in range(NCORES):
        m = dict(common)
        m["x_shard"] = np.ascontiguousarray(x[c * TOK:(c + 1) * TOK])
        m["ident"] = ident
        in_maps.append(m)
    res = run_bass_kernel_spmd(nc, in_maps, core_ids=list(range(NCORES)))
    y = np.concatenate([res.results[c]["y"] for c in range(NCORES)], axis=0)
    return y.reshape(B, L, D).astype(np.float32)


# version 03
# speedup vs baseline: 1.0230x; 1.0230x over previous
import numpy as np
from contextlib import ExitStack

import concourse.bass as bass
import concourse.mybir as mybir
from concourse.alu_op_type import AluOpType as OP
from concourse.bass_utils import run_bass_kernel_spmd

F32 = mybir.dt.float32
BF16 = mybir.dt.bfloat16
AF = mybir.ActivationFunctionType
AX = mybir.AxisListType

NCORES = 8
B, L, D = 4, 4096, 1024
TOK = (B * L) // NCORES
NT = TOK // 128
NG = TOK // 512
DC = D // 128
H = 512
E, FF = 6, 1536
FH = FF // 2
NH, HD, FD = 16, 64, 64
EPS = 1e-5


class Buf:
    __slots__ = ("name", "lw", "readers", "fence", "first")

    def __init__(self, name, fence):
        self.name = name
        self.lw = None
        self.readers = {}
        self.fence = fence
        self.first = True


class Prog:
    SEMS = ["pe", "act", "dve", "pool", "sp", "poold", "cc"]

    def __init__(self, nc):
        self.nc = nc
        self.eng = {"pe": nc.tensor, "act": nc.scalar, "dve": nc.vector,
                    "pool": nc.gpsimd, "sp": nc.sync}
        self.ctx = ExitStack()
        self.sems = {s: self.ctx.enter_context(nc.semaphore("s_" + s)) for s in self.SEMS}
        self.cnt = {s: 0 for s in self.SEMS}
        self.waited = {s: {} for s in self.eng}
        self.fence_counts = {s: 0 for s in self.SEMS}

    def buf(self, name):
        return Buf(name, dict(self.fence_counts))

    def fence(self):
        self.fence_counts = dict(self.cnt)

    def sb(self, name, shape, dt, ctx=None):
        side = None if ctx is None else "right"
        return (ctx or self.ctx).enter_context(
            self.nc.sbuf_tensor(name, shape, dt, side=side))

    def psum(self, name, shape, dt, ctx=None):
        return (ctx or self.ctx).enter_context(self.nc.psum_tensor(name, shape, dt))

    def emit(self, stream, fn, reads=(), hard_reads=(), writes=(), dma=False, asyncwb=False,
             hard_in=False, cc=False):
        hard_reads = list(hard_reads) + list(reads)
        reads = ()
        if cc:
            semname_own = "cc"
        elif dma and stream == "pool":
            semname_own = "poold"
        else:
            semname_own = stream
        need = {}

        def add(sem, val):
            if val > need.get(sem, 0):
                need[sem] = val

        for b in hard_reads:
            if b.lw is not None:
                s2, v2, _ = b.lw
                add(s2, v2)
        for b in writes:
            if b.first:
                for s2, v2 in b.fence.items():
                    if s2 != semname_own and v2 > 0:
                        add(s2, v2)
            if b.lw is not None:
                s2, v2, hard = b.lw
                if s2 != semname_own or hard:
                    add(s2, v2)
            for s2, v2 in b.readers.items():
                if s2 != semname_own:
                    add(s2, v2)
        eng = self.eng[stream]
        for semname, val in sorted(need.items()):
            if self.waited[stream].get(semname, 0) < val:
                eng.wait_ge(self.sems[semname], val)
                self.waited[stream][semname] = val
        inst = fn(eng)
        inc = 16 if dma else 1
        inst.then_inc(self.sems[semname_own], inc)
        self.cnt[semname_own] += inc
        v = self.cnt[semname_own]
        for b in writes:
            b.lw = (semname_own, v, asyncwb)
            b.readers = {}
            b.first = False
        for b in list(reads) + list(hard_reads):
            if b.readers.get(semname_own, 0) < v:
                b.readers[semname_own] = v

    def final_wait(self, stream="sp"):
        for semname in self.SEMS:
            v = self.cnt[semname]
            if v > 0 and self.waited[stream].get(semname, 0) < v:
                self.eng[stream].wait_ge(self.sems[semname], v)
                self.waited[stream][semname] = v


DEBUG_DUMP = False
STOP_AFTER = None
PASSA_LEVEL = 4

WEIGHT_NAMES = [
    "L0_attn_a_r1_w", "L0_attn_a_r1_b", "L0_attn_a_ln_g", "L0_attn_a_ln_b",
    "L0_attn_a_r2_w", "L0_attn_a_r2_b", "L0_attn_a_temp", "L0_attn_a_route_bias",
    "L0_attn_a_W1", "L0_attn_a_b1", "L0_attn_a_W2", "L0_attn_a_b2",
    "L0_attn_b_Wq", "L0_attn_b_Wk", "L0_attn_b_Wv", "L0_attn_b_Wo",
    "L0_attn_b_bo", "L0_attn_b_Wf", "L0_attn_b_bf",
    "L0_ffn_weight", "L0_ffn_aux_weight1", "L0_ffn_aux_weight2", "L0_ffn_bias",
]


def build_program(n_sub=4, debug_out=None):
    nc = bass.Bass("TRN2", target_bir_lowering=False, debug=False)
    p = Prog(nc)
    dbg_extra = []
    nc_in = {}
    shapes = {
        "L0_attn_a_r1_w": [D, H], "L0_attn_a_r1_b": [H], "L0_attn_a_ln_g": [H],
        "L0_attn_a_ln_b": [H], "L0_attn_a_r2_w": [H, E], "L0_attn_a_r2_b": [E],
        "L0_attn_a_temp": [1], "L0_attn_a_route_bias": [E],
        "L0_attn_a_W1": [E, D, FF], "L0_attn_a_b1": [E, FF],
        "L0_attn_a_W2": [E, FF, D], "L0_attn_a_b2": [E, D],
        "L0_attn_b_Wq": [D, D], "L0_attn_b_Wk": [D, D], "L0_attn_b_Wv": [D, D],
        "L0_attn_b_Wo": [D, D], "L0_attn_b_bo": [D], "L0_attn_b_Wf": [HD, FD],
        "L0_attn_b_bf": [FD], "L0_ffn_weight": [D, D], "L0_ffn_aux_weight1": [D, D],
        "L0_ffn_aux_weight2": [D, D], "L0_ffn_bias": [D],
    }
    for n in WEIGHT_NAMES:
        nc_in[n] = nc.dram_tensor(n, shapes[n], F32, kind="ExternalInput").ap()
    x_in = nc.dram_tensor("x_shard", [TOK, D], F32, kind="ExternalInput").ap()
    ident_in = nc.dram_tensor("ident", [128, 128], F32, kind="ExternalInput").ap()
    y_out = nc.dram_tensor("y", [TOK, D], F32, kind="ExternalOutput").ap()
    cc_in = nc.dram_tensor("cc_in", [128, 8 * 130], F32, kind="Internal").ap()
    cc_out = nc.dram_tensor("cc_out", [128, 8 * 130], F32, kind="Internal").ap()

    resid = p.sb("resid", [128, NT * D], F32)
    resid3 = resid.ap().rearrange("q (t d) -> q t d", t=NT)
    xT = p.sb("xT", [128, DC * TOK], BF16)
    xT3 = xT.ap().rearrange("q (c n) -> q c n", c=DC)
    id_f32 = p.sb("id_f32", [128, 128], F32)
    id_bf = p.sb("id_bf", [128, 128], BF16)
    ones_bf = p.sb("ones_bf", [1, 128], BF16)
    ones_f32 = p.sb("ones_f32", [1, 128], F32)
    ones_col_bf = p.sb("ones_col_bf", [128, 1], BF16)
    pow_exp = p.sb("pow_exp", [128, 1], F32)
    g_bc = p.sb("g_bc", [128, H], F32)
    b_bc = p.sb("b_bc", [128, H], F32)
    rb_bc = p.sb("rb_bc", [128, E], F32)
    rt_col = p.sb("rt_col", [128, 1], F32)
    r1b_row = p.sb("r1b_row", [1, H], BF16)
    r2b_row = p.sb("r2b_row", [1, E], BF16)
    ffnb_row = p.sb("ffnb_row", [1, D], BF16)
    bo_row = p.sb("bo_row", [1, D], BF16)
    bf_col = p.sb("bf_col", [128, 1], F32)
    bf_row16 = p.sb("bf_row16", [1, D], BF16)
    wfpad_e = p.sb("wfpad_e", [128, FD], BF16)
    wfpad_o = p.sb("wfpad_o", [128, FD], BF16)
    wfpair = p.sb("wfpair", [128, 128], BF16)
    b1_cols = p.sb("b1_cols", [128, 72], F32)
    b2_sb = p.sb("b2_sb", [6, D], BF16)
    sctx = ExitStack()
    b1_72 = p.sb("b1_72", [72, 128], F32, sctx)
    temp_sb = p.sb("temp_sb", [1, 1], F32, sctx)
    rt_row = p.sb("rt_row", [1, 1], F32, sctx)
    grow = p.sb("grow", [1, H], F32, sctx)
    brow = p.sb("brow", [1, H], F32, sctx)
    rbrow = p.sb("rbrow", [1, E], F32, sctx)

    st6 = [p.sb(f"st6_{i}", [128, 12], F32) for i in range(2)]
    mv_all = p.sb("mv_all", [128, 2 * NT], F32)
    veps_all = p.sb("veps_all", [128, NT], F32)
    rstd_all = p.sb("rstd_all", [128, NT], F32)
    xln = [p.sb(f"xln_{i}", [128, D], BF16) for i in range(2)]
    logits_all = p.sb("logits_all", [128, 6 * NT], F32)
    probs_all = p.sb("probs_all", [128, 6 * NT], F32)
    probsT = p.sb("probsT", [6, TOK], BF16)
    mv2 = [p.sb(f"mv2_{i}", [128, 2], F32) for i in range(2)]
    veps2 = [p.sb(f"veps2_{i}", [128, 1], F32) for i in range(2)]
    rstd2 = [p.sb(f"rstd2_{i}", [128, 1], F32) for i in range(2)]
    t1 = [p.sb(f"t1_{i}", [128, H], F32) for i in range(2)]
    h2b = [p.sb(f"h2b_{i}", [128, H], BF16) for i in range(2)]
    h2T = [p.sb(f"h2T_{i}", [128, H], BF16) for i in range(2)]
    sm_e = [p.sb(f"sm_e_{i}", [128, 6], F32) for i in range(2)]
    sm_c = [p.sb(f"sm_c_{i}", [128, 6], F32) for i in range(2)]
    sm_m = [p.sb(f"sm_m_{i}", [128, 1], F32) for i in range(2)]
    sm_m2 = [p.sb(f"sm_m2_{i}", [128, 1], F32) for i in range(2)]
    sm_s = [p.sb(f"sm_s_{i}", [128, 1], F32) for i in range(2)]
    probs_bf = [p.sb(f"probs_bf_{i}", [128, 6], BF16) for i in range(2)]

    ps_big = [p.psum(f"ps_big{i}", [128, 1024], F32) for i in range(2)]
    ps_med = [p.psum(f"ps_med{i}", [128, 512], F32) for i in range(2)]
    ps_aux = [p.psum(f"ps_aux{i}", [128, 512], F32) for i in range(2)]

    B_resid = [p.buf(f"resid{t}") for t in range(NT)]
    B_xT = [p.buf(f"xT{t}") for t in range(NT)]
    B_const = p.buf("const")
    B_st6 = [p.buf(f"st6_{i}") for i in range(2)]
    B_mv = [p.buf(f"mv{t}") for t in range(NT)]
    B_veps = [p.buf(f"veps{t}") for t in range(NT)]
    B_rstd = [p.buf(f"rstd{t}") for t in range(NT)]
    B_xln = [p.buf(f"xln{i}") for i in range(2)]
    B_logits = [p.buf(f"logits{t}") for t in range(NT)]
    B_probs = [p.buf(f"probs{t}") for t in range(NT)]
    B_probsT = [p.buf(f"probsT{t}") for t in range(NT)]
    B_mv2 = [p.buf(f"mv2_{i}") for i in range(2)]
    B_veps2 = [p.buf(f"veps2_{i}") for i in range(2)]
    B_rstd2 = [p.buf(f"rstd2_{i}") for i in range(2)]
    B_t1 = [p.buf(f"t1_{i}") for i in range(2)]
    B_h2b = [p.buf(f"h2b_{i}") for i in range(2)]
    B_h2T = [p.buf(f"h2T_{i}") for i in range(2)]
    B_sm = [p.buf(f"sm{i}") for i in range(2)]
    B_pbf = [p.buf(f"pbf{i}") for i in range(2)]
    B_big = [p.buf(f"big{i}") for i in range(2)]
    B_med = [p.buf(f"med{i}") for i in range(2)]
    B_aux = [p.buf(f"aux{i}") for i in range(2)]

    EM = p.emit

    def dma(stream, out_ap, in_ap, writes, reads=(), **kw):
        EM(stream, lambda e: e.dma_start(out=out_ap, in_=in_ap, **kw),
           reads=reads, writes=writes, dma=True)

    dma("sp", resid3, x_in.rearrange("(t q) d -> q t d", q=128), writes=B_resid)
    dma("sp", id_f32[:], ident_in, writes=[B_const])
    dma("pool", id_bf[:], ident_in, writes=[B_const])
    dma("pool", r1b_row[:], nc_in["L0_attn_a_r1_b"][None, :], writes=[B_const])
    dma("pool", r2b_row[:], nc_in["L0_attn_a_r2_b"][None, :], writes=[B_const])
    dma("pool", ffnb_row[:], nc_in["L0_ffn_bias"][None, :], writes=[B_const])
    dma("pool", bo_row[:], nc_in["L0_attn_b_bo"][None, :], writes=[B_const])
    dma("pool", b2_sb[:], nc_in["L0_attn_a_b2"], writes=[B_const])
    EM("dve", lambda e: e.memset(wfpad_e[:], 0.0), writes=[B_const])
    EM("dve", lambda e: e.memset(wfpad_o[:], 0.0), writes=[B_const])
    EM("dve", lambda e: e.memset(wfpair[:], 0.0), writes=[B_const])
    dma("pool", wfpad_e[0:64, :], nc_in["L0_attn_b_Wf"], reads=[B_const], writes=[B_const])
    dma("pool", wfpad_o[64:128, :], nc_in["L0_attn_b_Wf"], reads=[B_const], writes=[B_const])
    dma("pool", wfpair[0:64, 0:64], nc_in["L0_attn_b_Wf"], reads=[B_const], writes=[B_const])
    dma("pool", wfpair[64:128, 64:128], nc_in["L0_attn_b_Wf"], reads=[B_const], writes=[B_const])
    for hh in range(NH):
        dma("pool", bf_row16[0:1, hh * FD:(hh + 1) * FD], nc_in["L0_attn_b_bf"][None, :],
            writes=[B_const])
    dma("sp", bf_col[0:64, :], nc_in["L0_attn_b_bf"][:, None], writes=[B_const])
    dma("sp", bf_col[64:128, :], nc_in["L0_attn_b_bf"][:, None], writes=[B_const])
    dma("sp", b1_72[:], nc_in["L0_attn_a_b1"].rearrange("e (c q) -> (e c) q", q=128),
        writes=[B_const])
    dma("sp", temp_sb[:], nc_in["L0_attn_a_temp"][None, :], writes=[B_const])
    dma("sp", grow[:], nc_in["L0_attn_a_ln_g"][None, :], writes=[B_const])
    dma("sp", brow[:], nc_in["L0_attn_a_ln_b"][None, :], writes=[B_const])
    dma("sp", rbrow[:], nc_in["L0_attn_a_route_bias"][None, :], writes=[B_const])

    EM("dve", lambda e: e.memset(ones_bf[:], 1.0), writes=[B_const])
    EM("dve", lambda e: e.memset(ones_f32[:], 1.0), writes=[B_const])
    EM("dve", lambda e: e.memset(ones_col_bf[:], 1.0), writes=[B_const])
    EM("dve", lambda e: e.memset(pow_exp[:], -0.5), writes=[B_const])
    EM("dve", lambda e: e.reciprocal(rt_row[:], temp_sb[:]), reads=[B_const], writes=[B_const])

    def bcast(row_ap, ncols, dst_ap):
        EM("pe", lambda e: e.matmul(ps_med[0][:, 0:ncols], ones_f32[:], row_ap,
                                    start=True, stop=True),
           reads=[B_const], writes=[B_med[0]])
        EM("act", lambda e: e.activation(dst_ap, ps_med[0][:, 0:ncols], AF.Copy),
           reads=[B_med[0]], writes=[B_const])

    bcast(grow[:], H, g_bc[:])
    bcast(brow[:], H, b_bc[:])
    bcast(rbrow[:], E, rb_bc[:])
    bcast(rt_row[:], 1, rt_col[:])
    EM("pe", lambda e: e.matmul(ps_med[1][:, 0:72], b1_72[:], id_f32[0:72, 0:72],
                                start=True, stop=True),
       reads=[B_const], writes=[B_med[1]])
    EM("act", lambda e: e.activation(b1_cols[:], ps_med[1][:, 0:72], AF.Copy),
       reads=[B_med[1]], writes=[B_const])
    sctx.close()
    p.fence()

    def ln_stats(t):
        s = t % 2
        EM("dve", lambda e: e.bn_stats(st6[s][:, 0:6], resid3[:, t, 0:512]),
           reads=[B_resid[t]], writes=[B_st6[s]], asyncwb=True, hard_in=True)
        EM("dve", lambda e: e.bn_stats(st6[s][:, 6:12], resid3[:, t, 512:1024]),
           reads=[B_resid[t]], writes=[B_st6[s]], asyncwb=True, hard_in=True)
        EM("dve", lambda e: e.bn_aggr(mv_all[:, 2 * t:2 * t + 2], st6[s][:]),
           reads=[B_st6[s]], writes=[B_mv[t]], asyncwb=True, hard_in=True)
        EM("dve", lambda e: e.tensor_scalar(veps_all[:, t:t + 1], mv_all[:, 2 * t + 1:2 * t + 2],
                                            EPS, None, op0=OP.add),
           reads=[B_mv[t]], writes=[B_veps[t]])
        EM("pool", lambda e: e.tensor_tensor(rstd_all[:, t:t + 1], veps_all[:, t:t + 1],
                                             pow_exp[:], op=OP.pow),
           reads=[B_veps[t], B_const], writes=[B_rstd[t]], asyncwb=True, hard_in=True)

    def ln_apply_transpose(t):
        s = t % 2
        EM("dve", lambda e: e.tensor_scalar(xln[s][:], resid3[:, t, :],
                                            mv_all[:, 2 * t:2 * t + 1],
                                            rstd_all[:, t:t + 1],
                                            op0=OP.subtract, op1=OP.mult),
           reads=[B_resid[t]], hard_reads=[B_mv[t], B_rstd[t]], writes=[B_xln[s]])
        bp = B_big[t % 2]
        for c in range(DC):
            EM("pe", lambda e, c=c: e.matmul(ps_big[t % 2][:, c * 128:(c + 1) * 128],
                                             xln[s][:, c * 128:(c + 1) * 128], id_bf[:],
                                             start=True, stop=True),
               reads=[B_xln[s], B_const], writes=[bp])
        EM("act", lambda e: e.activation(
            xT3[:, :, t * 128:(t + 1) * 128],
            ps_big[t % 2].ap().rearrange("q (c n) -> q c n", c=8), AF.Copy),
           reads=[bp], writes=[B_xT[t]])

    def ln_pass():
        if STOP_AFTER == "stats":
            for t in range(NT):
                ln_stats(t)
            return
        for t in range(NT):
            ln_stats(t)
            if t >= 2:
                ln_apply_transpose(t - 2)
        ln_apply_transpose(NT - 2)
        ln_apply_transpose(NT - 1)

    def attn_a():
        actx = ExitStack()
        r1w = p.sb("r1w", [128, DC, H], BF16, actx)
        r2w = p.sb("r2w", [128, 4, E], BF16, actx)
        B_r1w, B_r2w = p.buf("r1w"), p.buf("r2w")
        dma("pool", r1w[:], nc_in["L0_attn_a_r1_w"].rearrange("(c q) h -> q c h", q=128),
            writes=[B_r1w])
        dma("pool", r2w[:], nc_in["L0_attn_a_r2_w"].rearrange("(c q) s -> q c s", q=128),
            writes=[B_r2w])

        ln_pass()
        if STOP_AFTER in ("ln", "stats"):
            actx.close(); p.fence(); return

        for t in range(NT):
            s = t % 2
            hb = B_med[s]
            for c in range(DC):
                EM("pe", lambda e, c=c: e.matmul(ps_med[s][:], xT3[:, c, t * 128:(t + 1) * 128],
                                                 r1w[:, c, :], start=(c == 0), stop=False),
                   reads=[B_xT[t], B_r1w], writes=[hb])
            EM("pe", lambda e: e.matmul(ps_med[s][:], ones_bf[:], r1b_row[:],
                                        start=False, stop=True),
               reads=[B_const], writes=[hb])
            EM("dve", lambda e: e.bn_stats(st6[s][:, 0:6], ps_med[s][:]),
               reads=[hb], writes=[B_st6[s]], asyncwb=True, hard_in=True)
            EM("dve", lambda e: e.bn_aggr(mv2[s][:], st6[s][:, 0:6]),
               reads=[B_st6[s]], writes=[B_mv2[s]], asyncwb=True, hard_in=True)
            EM("dve", lambda e: e.tensor_scalar(veps2[s][:], mv2[s][:, 1:2], EPS, None, op0=OP.add),
               reads=[B_mv2[s]], writes=[B_veps2[s]])
            EM("pool", lambda e: e.tensor_tensor(rstd2[s][:], veps2[s][:], pow_exp[:], op=OP.pow),
               reads=[B_veps2[s], B_const], writes=[B_rstd2[s]], asyncwb=True, hard_in=True)
            EM("dve", lambda e: e.tensor_scalar(t1[s][:], ps_med[s][:], mv2[s][:, 0:1],
                                                rstd2[s][:], op0=OP.subtract, op1=OP.mult),
               reads=[hb], hard_reads=[B_mv2[s], B_rstd2[s]], writes=[B_t1[s]])
            EM("dve", lambda e: e.tensor_tensor(t1[s][:], t1[s][:], g_bc[:], op=OP.mult),
               reads=[B_t1[s], B_const], writes=[B_t1[s]])
            EM("dve", lambda e: e.tensor_tensor(t1[s][:], t1[s][:], b_bc[:], op=OP.add),
               reads=[B_t1[s], B_const], writes=[B_t1[s]])
            EM("act", lambda e: e.activation(h2b[s][:], t1[s][:], AF.Gelu),
               reads=[B_t1[s]], writes=[B_h2b[s]])
            bg = B_big[s]
            for c in range(4):
                EM("pe", lambda e, c=c: e.matmul(ps_big[s][:, c * 128:(c + 1) * 128],
                                                 h2b[s][:, c * 128:(c + 1) * 128], id_bf[:],
                                                 start=True, stop=True),
                   reads=[B_h2b[s], B_const], writes=[bg])
            EM("act", lambda e: e.activation(h2T[s][:], ps_big[s][:, 0:512], AF.Copy),
               reads=[bg], writes=[B_h2T[s]])
            ab = B_aux[s]
            for c in range(4):
                EM("pe", lambda e, c=c: e.matmul(ps_aux[s][:, 0:E],
                                                 h2T[s][:, c * 128:(c + 1) * 128],
                                                 r2w[:, c, :], start=(c == 0), stop=False),
                   reads=[B_h2T[s], B_r2w], writes=[ab])
            EM("pe", lambda e: e.matmul(ps_aux[s][:, 0:E], ones_bf[:], r2b_row[:],
                                        start=False, stop=True),
               reads=[B_const], writes=[ab])
            EM("dve", lambda e: e.scalar_tensor_tensor(
                logits_all[:, 6 * t:6 * t + 6], ps_aux[s][:, 0:E], rt_col[:, 0:1], rb_bc[:],
                op0=OP.mult, op1=OP.add),
               reads=[ab, B_const], hard_reads=[B_const], writes=[B_logits[t]])

        if STOP_AFTER == "router":
            actx.close(); p.fence(); return
        for t in range(NT):
            s = t % 2
            lsl = logits_all[:, 6 * t:6 * t + 6]
            psl = probs_all[:, 6 * t:6 * t + 6]
            EM("dve", lambda e: e.reduce_max(sm_m[s][:], lsl, axis=AX.X),
               reads=[B_logits[t]], writes=[B_sm[s]], asyncwb=True, hard_in=True)
            EM("dve", lambda e: e.tensor_scalar(sm_m[s][:], sm_m[s][:], -1.0, None, op0=OP.mult),
               reads=[B_sm[s]], writes=[B_sm[s]])
            EM("act", lambda e: e.activation(sm_e[s][:], lsl, AF.Exp, bias=sm_m[s][:, 0:1],
                                             scale=1.0),
               reads=[B_logits[t]], hard_reads=[B_sm[s]], writes=[B_sm[s]])
            EM("dve", lambda e: e.reduce_sum(sm_s[s][:], sm_e[s][:], axis=AX.X),
               reads=[B_sm[s]], writes=[B_sm[s]], asyncwb=True, hard_in=True)
            EM("dve", lambda e: e.reciprocal(sm_s[s][:], sm_s[s][:]),
               reads=[B_sm[s]], writes=[B_sm[s]])
            EM("dve", lambda e: e.tensor_scalar(psl, sm_e[s][:], sm_s[s][:, 0:1], None,
                                                op0=OP.mult),
               reads=[B_sm[s]], hard_reads=[B_sm[s]], writes=[B_probs[t]])
            EM("dve", lambda e: e.reduce_max(sm_m[s][:], psl, axis=AX.X),
               reads=[B_probs[t]], writes=[B_sm[s]], asyncwb=True, hard_in=True)
            EM("dve", lambda e: e.tensor_scalar(sm_c[s][:], psl, sm_m[s][:, 0:1], None,
                                                op0=OP.is_equal),
               reads=[B_probs[t]], hard_reads=[B_sm[s]], writes=[B_sm[s]])
            EM("dve", lambda e: e.scalar_tensor_tensor(sm_e[s][:], sm_c[s][:], -1e30, psl,
                                                       op0=OP.mult, op1=OP.add),
               reads=[B_sm[s], B_probs[t]], writes=[B_sm[s]])
            EM("dve", lambda e: e.reduce_max(sm_m2[s][:], sm_e[s][:], axis=AX.X),
               reads=[B_sm[s]], writes=[B_sm[s]], asyncwb=True, hard_in=True)
            EM("dve", lambda e: e.tensor_scalar(sm_c[s][:], psl, sm_m2[s][:, 0:1], None,
                                                op0=OP.is_ge),
               reads=[B_probs[t]], hard_reads=[B_sm[s]], writes=[B_sm[s]])
            EM("dve", lambda e: e.tensor_tensor(psl, psl, sm_c[s][:], op=OP.mult),
               reads=[B_probs[t], B_sm[s]], writes=[B_probs[t]])
            EM("dve", lambda e: e.tensor_copy(probs_bf[s][:], psl),
               reads=[B_probs[t]], writes=[B_pbf[s]])
            EM("pe", lambda e: e.matmul(ps_aux[s][0:6, 0:128], probs_bf[s][:], id_bf[:],
                                        start=True, stop=True),
               reads=[B_pbf[s], B_const], writes=[B_aux[s]])
            EM("act", lambda e: e.activation(probsT.ap()[:, t * 128:(t + 1) * 128],
                                             ps_aux[s][0:6, 0:128], AF.Copy),
               reads=[B_aux[s]], writes=[B_probsT[t]])

        if STOP_AFTER == "softmax":
            actx.close(); p.fence(); return
        for t in range(NT):
            s = t % 2
            bg = B_big[s]
            for nh2 in range(2):
                EM("pe", lambda e, nh2=nh2: e.matmul(
                    ps_big[s][:, nh2 * 512:(nh2 + 1) * 512],
                    probsT.ap()[:, t * 128:(t + 1) * 128],
                    b2_sb[:, nh2 * 512:(nh2 + 1) * 512], start=True, stop=True),
                   reads=[B_probsT[t], B_const], writes=[bg])
            EM("dve", lambda e: e.tensor_tensor(resid3[:, t, :], resid3[:, t, :],
                                                ps_big[s][:], op=OP.add),
               reads=[B_resid[t], bg], writes=[B_resid[t]])

        if STOP_AFTER == "b2":
            actx.close(); p.fence(); return
        w1b = [p.sb(f"w1b{i}", [128, DC, FH], BF16, actx) for i in range(2)]
        w2b = [p.sb(f"w2b{i}", [128, 6, D], BF16, actx) for i in range(2)]
        actb = [p.sb(f"actb{i}", [128, 6, 512], BF16, actx) for i in range(2)]
        B_w1 = [p.buf(f"w1b{i}") for i in range(2)]
        B_w2 = [p.buf(f"w2b{i}") for i in range(2)]
        B_act = [p.buf(f"actb{i}") for i in range(2)]

        order = [0, 2, 4, 1, 3, 5]
        steps = [(eo, fh) for eo in order for fh in range(2)]

        def prefetch(i):
            ex, fh = steps[i]
            sl = i % 2
            dma("pool", w1b[sl][:],
                nc_in["L0_attn_a_W1"][ex, :, fh * FH:(fh + 1) * FH]
                .rearrange("(c q) f -> q c f", q=128), writes=[B_w1[sl]])
            dma("pool", w2b[sl][:],
                nc_in["L0_attn_a_W2"][ex, fh * FH:(fh + 1) * FH, :]
                .rearrange("(c q) n -> q c n", q=128), writes=[B_w2[sl]])

        def w1_group(i, g):
            ex, fh = steps[i]
            sl = i % 2
            func = AF.Gelu if ex % 2 == 0 else AF.Silu
            asl = g % 2
            for c in range(6):
                mb = B_med[c % 2]
                for kc in range(DC):
                    EM("pe", lambda e, c=c, kc=kc, g=g: e.matmul(
                        ps_med[c % 2][:], w1b[sl][:, kc, c * 128:(c + 1) * 128],
                        xT3[:, kc, g * 512:(g + 1) * 512],
                        start=(kc == 0), stop=(kc == DC - 1)),
                       reads=[B_w1[sl]] + [B_xT[4 * g + j] for j in range(4)],
                       writes=[mb])
                bcol = b1_cols[:, ex * 12 + fh * 6 + c: ex * 12 + fh * 6 + c + 1]
                EM("act", lambda e, c=c, g=g, func=func, bcol=bcol: e.activation(
                    actb[asl][:, c, :], ps_med[c % 2][:], func, bias=bcol, scale=1.0),
                   reads=[mb], hard_reads=[B_const], writes=[B_act[asl]])

        def eo_group(i, g):
            ex, fh = steps[i]
            sl = i % 2
            asl = g % 2
            for bb in range(4):
                t = 4 * g + bb
                bg = B_big[t % 2]
                for nh2 in range(2):
                    for c in range(6):
                        EM("pe", lambda e, c=c, nh2=nh2, bb=bb, g=g: e.matmul(
                            ps_big[t % 2][:, nh2 * 512:(nh2 + 1) * 512],
                            actb[asl][:, c, bb * 128:(bb + 1) * 128],
                            w2b[sl][:, c, nh2 * 512:(nh2 + 1) * 512],
                            start=(c == 0), stop=(c == 5)),
                           reads=[B_act[asl], B_w2[sl]], writes=[bg])
                EM("dve", lambda e, t=t: e.scalar_tensor_tensor(
                    resid3[:, t, :], ps_big[t % 2][:],
                    probs_all[:, 6 * t + ex: 6 * t + ex + 1], resid3[:, t, :],
                    op0=OP.mult, op1=OP.add),
                   reads=[bg, B_resid[t]], hard_reads=[B_probs[t]],
                   writes=[B_resid[t]])

        prefetch(0)
        stages = [(i, g) for i in range(len(steps)) for g in range(NG)]
        prefetched = 0
        for j, (i, g) in enumerate(stages):
            if g == 1 and i + 1 < len(steps) and prefetched <= i:
                prefetch(i + 1)
                prefetched = i + 1
            w1_group(i, g)
            if j >= 1:
                eo_group(*stages[j - 1])
        eo_group(*stages[-1])
        actx.close()
        p.fence()

    ffn_calls = [0]

    def ffn():
        tag = f"f{ffn_calls[0]}"
        ffn_calls[0] += 1
        fctx = ExitStack()
        wb = [p.sb(f"{tag}w{i}", [128, DC, D], BF16, fctx) for i in range(3)]
        B_wb = [p.buf(f"{tag}w{i}") for i in range(3)]
        h1sb = [p.sb(f"{tag}h1sb{i}", [128, DC, 512], BF16, fctx) for i in range(2)]
        h2sb = [p.sb(f"{tag}h2sb0", [128, DC, 512], BF16, fctx)] * 2
        tsb = [p.sb(f"{tag}tsb0", [128, D], F32, fctx)] * 2
        B_h1 = [p.buf(f"{tag}h1sb{i}") for i in range(2)]
        B_h2 = [p.buf(f"{tag}h2sb0")] * 2
        B_t = [p.buf(f"{tag}tsb0")] * 2
        for wi, wn in enumerate(["L0_ffn_weight", "L0_ffn_aux_weight1", "L0_ffn_aux_weight2"]):
            dma("pool", wb[wi][:], nc_in[wn].rearrange("(c q) n -> q c n", q=128),
                writes=[B_wb[wi]])

        ln_pass()

        def mm_fm(dst_sb, dst_buf, src_fn, src_bufs, w_i):
            for oc in range(DC):
                mb = B_med[oc % 2]
                for kc in range(DC):
                    EM("pe", lambda e, oc=oc, kc=kc: e.matmul(
                        ps_med[oc % 2][:], wb[w_i][:, kc, oc * 128:(oc + 1) * 128],
                        src_fn(kc), start=(kc == 0), stop=(kc == DC - 1)),
                       reads=[B_wb[w_i]] + src_bufs, writes=[mb])
                EM("act", lambda e, oc=oc: e.activation(dst_sb[:, oc, :], ps_med[oc % 2][:],
                                                        AF.Copy),
                   reads=[mb], writes=[dst_buf])

        for g in range(NG):
            sl = g % 2
            src_bufs = [B_xT[4 * g + j] for j in range(4)]
            mm_fm(h1sb[sl], B_h1[sl],
                  lambda kc, g=g: xT3[:, kc, g * 512:(g + 1) * 512], src_bufs, 0)
            mm_fm(h2sb[sl], B_h2[sl],
                  lambda kc, sl=sl: h1sb[sl][:, kc, :], [B_h1[sl]], 1)
            for bb in range(4):
                t = 4 * g + bb
                bg = B_big[t % 2]
                for nh2 in range(2):
                    for kc in range(DC):
                        EM("pe", lambda e, kc=kc, nh2=nh2, bb=bb: e.matmul(
                            ps_big[t % 2][:, nh2 * 512:(nh2 + 1) * 512],
                            h2sb[sl][:, kc, bb * 128:(bb + 1) * 128],
                            wb[2][:, kc, nh2 * 512:(nh2 + 1) * 512],
                            start=(kc == 0), stop=False),
                           reads=[B_h2[sl], B_wb[2]], writes=[bg])
                    EM("pe", lambda e, nh2=nh2: e.matmul(
                        ps_big[t % 2][:, nh2 * 512:(nh2 + 1) * 512], ones_bf[:],
                        ffnb_row[0:1, nh2 * 512:(nh2 + 1) * 512], start=False, stop=True),
                       reads=[B_const], writes=[bg])
                EM("act", lambda e, t=t: e.activation(tsb[t % 2][:], ps_big[t % 2][:], AF.Tanh),
                   reads=[bg], writes=[B_t[t % 2]])
                EM("dve", lambda e, t=t: e.tensor_tensor(resid3[:, t, :], resid3[:, t, :],
                                                         tsb[t % 2][:], op=OP.add),
                   reads=[B_resid[t], B_t[t % 2]], writes=[B_resid[t]])
        fctx.close()
        p.fence()

    def attn_b():
        bctx = ExitStack()
        wk = p.sb("wk", [128, DC, D], BF16, bctx)
        wv = p.sb("wv", [128, DC, D], BF16, bctx)
        B_wk, B_wv = p.buf("wk"), p.buf("wv")
        ksb = [p.sb(f"ksb{i}", [128, DC, 512], BF16, bctx) for i in range(2)]
        kfe = [p.sb(f"kfe{i}", [128, D], BF16, bctx) for i in range(2)]
        kfo = [p.sb(f"kfo{i}", [128, D], BF16, bctx) for i in range(2)]
        vev = [p.sb(f"vev{i}", [128, D], BF16, bctx) for i in range(2)]
        vod = [p.sb(f"vod{i}", [128, D], BF16, bctx) for i in range(2)]
        kv_all = p.sb("kv_all", [128, 8 * 130], F32, bctx)
        B_ksb = [p.buf(f"ksb{i}") for i in range(2)]
        B_kfe = [p.buf(f"kfe{i}") for i in range(2)]
        B_kfo = [p.buf(f"kfo{i}") for i in range(2)]
        B_vev = [p.buf(f"vev{i}") for i in range(2)]
        B_vod = [p.buf(f"vod{i}") for i in range(2)]
        B_kv_all = p.buf("kv_all")
        B_ccin, B_ccout = p.buf("cc_in"), p.buf("cc_out")
        for i in range(2):
            for bufT, bufB in ((kfe[i], B_kfe[i]), (kfo[i], B_kfo[i]),
                               (vev[i], B_vev[i]), (vod[i], B_vod[i])):
                EM("dve", lambda e, bufT=bufT: e.memset(bufT[:], 0.0), writes=[bufB])
        pview = lambda pst: pst.ap().rearrange("q (pr j) -> q pr j", pr=8)
        ev3 = lambda sbT: sbT.ap().rearrange("q (pr j) -> q pr j", pr=8)

        dma("pool", wk[:], nc_in["L0_attn_b_Wk"].rearrange("(c q) n -> q c n", q=128),
            writes=[B_wk])
        dma("pool", wv[:], nc_in["L0_attn_b_Wv"].rearrange("(c q) n -> q c n", q=128),
            writes=[B_wv])

        ln_pass()
        if STOP_AFTER == "bln":
            bctx.close(); p.fence(); return

        kv3 = kv_all.ap().rearrange("q (pr j) -> q pr j", pr=8)
        EM("dve", lambda e: e.memset(kv_all[:], 0.0), writes=[B_kv_all])
        B_ksump = B_aux[1]
        for g in range(NG):
            sl = g % 2
            for oc in range(DC):
                mb = B_med[oc % 2]
                for kc in range(DC):
                    EM("pe", lambda e, oc=oc, kc=kc, g=g: e.matmul(
                        ps_med[oc % 2][:], wk[:, kc, oc * 128:(oc + 1) * 128],
                        xT3[:, kc, g * 512:(g + 1) * 512],
                        start=(kc == 0), stop=(kc == DC - 1)),
                       reads=[B_wk] + [B_xT[4 * g + j] for j in range(4)], writes=[mb])
                EM("act", lambda e, oc=oc: e.activation(ksb[sl][:, oc, :], ps_med[oc % 2][:],
                                                        AF.Copy),
                   reads=[mb], writes=[B_ksb[sl]])
            for bb in range(4):
                t = 4 * g + bb
                ts = t % 2
                vb = B_big[ts]
                for nh2 in range(2):
                    for kc in range(DC):
                        EM("pe", lambda e, kc=kc, nh2=nh2, t=t: e.matmul(
                            ps_big[ts][:, nh2 * 512:(nh2 + 1) * 512],
                            xT3[:, kc, t * 128:(t + 1) * 128],
                            wv[:, kc, nh2 * 512:(nh2 + 1) * 512],
                            start=(kc == 0), stop=(kc == DC - 1)),
                           reads=[B_xT[t], B_wv], writes=[vb])
                EM("act", lambda e, ts=ts: e.activation(
                    ev3(vev[ts])[:, :, 0:64], pview(ps_big[ts])[:, :, 0:64], AF.Copy),
                   reads=[vb], writes=[B_vev[ts]])
                EM("act", lambda e, ts=ts: e.activation(
                    ev3(vod[ts])[:, :, 64:128], pview(ps_big[ts])[:, :, 64:128], AF.Copy),
                   reads=[vb], writes=[B_vod[ts]])
                if PASSA_LEVEL < 2:
                    continue
                for pr in range(8):
                    for par in range(2):
                        wf_p = wfpad_e if par == 0 else wfpad_o
                        EM("pe", lambda e, pr=pr, par=par, bb=bb, sl=sl, wf_p=wf_p: e.matmul(
                            ps_big[ts][:, pr * 128 + par * 64: pr * 128 + par * 64 + 64],
                            ksb[sl][:, pr, bb * 128:(bb + 1) * 128],
                            wf_p[:], start=((pr == 0 or pr == 4) and par == 0),
                            stop=False, skip_group_check=True),
                           reads=[B_ksb[sl], B_const], writes=[vb])
                for nh2 in range(2):
                    EM("pe", lambda e, nh2=nh2, ts=ts: e.matmul(
                        ps_big[ts][:, nh2 * 512:(nh2 + 1) * 512], ones_bf[:],
                        bf_row16[0:1, nh2 * 512:(nh2 + 1) * 512], start=False, stop=True,
                        skip_group_check=True),
                       reads=[B_const], writes=[vb])
                EM("act", lambda e, ts=ts: e.activation(
                    ev3(kfe[ts])[:, :, 0:64], pview(ps_big[ts])[:, :, 0:64], AF.Relu),
                   reads=[vb], writes=[B_kfe[ts]])
                EM("act", lambda e, ts=ts: e.activation(
                    ev3(kfo[ts])[:, :, 64:128], pview(ps_big[ts])[:, :, 64:128], AF.Relu),
                   reads=[vb], writes=[B_kfo[ts]])
                if PASSA_LEVEL < 3:
                    continue
                kb = (ts + 1) % 2
                B_kvp = B_big[kb]
                for pr in range(8):
                    EM("pe", lambda e, pr=pr, ts=ts, kb=kb: e.matmul(
                        ps_big[kb][:, pr * 128:(pr + 1) * 128],
                        kfe[ts][:, pr * 128:(pr + 1) * 128],
                        vev[ts][:, pr * 128:(pr + 1) * 128],
                        start=(pr == 0 or pr == 4), stop=False, skip_group_check=True),
                       reads=[B_kfe[ts], B_vev[ts]], writes=[B_kvp])
                    EM("pe", lambda e, pr=pr, ts=ts, kb=kb: e.matmul(
                        ps_big[kb][:, pr * 128:(pr + 1) * 128],
                        kfo[ts][:, pr * 128:(pr + 1) * 128],
                        vod[ts][:, pr * 128:(pr + 1) * 128],
                        start=False, stop=(pr == 7), skip_group_check=True),
                       reads=[B_kfo[ts], B_vod[ts]], writes=[B_kvp])
                    EM("pe", lambda e, pr=pr, ts=ts: e.matmul(
                        ps_aux[1][:, pr * 2: pr * 2 + 1],
                        kfe[ts][:, pr * 128:(pr + 1) * 128], ones_col_bf[:],
                        start=(pr == 0), stop=False, skip_group_check=True),
                       reads=[B_kfe[ts], B_const], writes=[B_ksump])
                    EM("pe", lambda e, pr=pr, ts=ts: e.matmul(
                        ps_aux[1][:, pr * 2 + 1: pr * 2 + 2],
                        kfo[ts][:, pr * 128:(pr + 1) * 128], ones_col_bf[:],
                        start=False, stop=(pr == 7), skip_group_check=True),
                       reads=[B_kfo[ts], B_const], writes=[B_ksump])
                EM("dve", lambda e, kb=kb: e.tensor_tensor(
                    kv3[:, :, 0:128], kv3[:, :, 0:128],
                    ps_big[kb].ap().rearrange("q (pr j) -> q pr j", pr=8), op=OP.add),
                   reads=[B_kvp, B_kv_all], writes=[B_kv_all])
                EM("dve", lambda e: e.tensor_tensor(
                    kv3[:, :, 128:130], kv3[:, :, 128:130],
                    ps_aux[1].ap()[:, 0:16].rearrange("q (pr j) -> q pr j", pr=8), op=OP.add),
                   reads=[B_ksump, B_kv_all], writes=[B_kv_all])

        if STOP_AFTER == "kvacc":
            dbg_extra.append(("dbg_kv", kv_all, [128, 8 * 130], F32))
            return
        dma("sp", cc_in, kv_all[:], reads=[B_kv_all], writes=[B_ccin])
        EM("pool", lambda e: e.collective_compute(
            "AllReduce", OP.add, replica_groups=[[0, 1], [2, 3], [4, 5], [6, 7]],
            ins=[cc_in.opt()], outs=[cc_out.opt()]),
           reads=[B_ccin], writes=[B_ccout], asyncwb=True, hard_in=True, cc=True)

        bctx.close()
        p.fence()
        bctx = ExitStack()
        wq = p.sb("wq", [128, DC, D], BF16, bctx)
        wo = p.sb("wo", [128, DC, D], BF16, bctx)
        B_wq, B_wo = p.buf("wq"), p.buf("wo")
        dma("pool", wq[:], nc_in["L0_attn_b_Wq"].rearrange("(c q) n -> q c n", q=128),
            writes=[B_wq])
        dma("pool", wo[:], nc_in["L0_attn_b_Wo"].rearrange("(c q) n -> q c n", q=128),
            writes=[B_wo])

        kvt = p.sb("kvt", [128, 8 * 130], F32, bctx)
        kvt_bf = p.sb("kvt_bf", [128, 8 * 130], BF16, bctx)
        B_kvt, B_kvtbf = p.buf("kvt"), p.buf("kvt_bf")
        dma("sp", kvt[:], cc_out, reads=[B_ccout], writes=[B_kvt])
        EM("dve", lambda e: e.tensor_copy(kvt_bf[:], kvt[:]), reads=[B_kvt], writes=[B_kvtbf])
        kvt3 = kvt_bf.ap().rearrange("q (pr j) -> q pr j", pr=8)
        if STOP_AFTER == "kv":
            dbg_extra.append(("dbg_kv", kvt, [128, 8 * 130], F32))
            return

        qsb = [p.sb(f"qsb{i}", [128, 512], BF16, bctx) for i in range(2)]
        qfsb = [p.sb(f"qfsb{i}", [128, DC, 512], BF16, bctx) for i in range(2)]
        nrm = [p.sb(f"nrm{i}", [128, NH], F32, bctx) for i in range(2)]
        aob = [p.sb(f"aob{i}", [128, D], BF16, bctx) for i in range(2)]
        aoT = [p.sb(f"aoT{i}", [128, D], BF16, bctx) for i in range(2)]
        B_qsb = [p.buf(f"qsb{i}") for i in range(2)]
        B_qf = [p.buf(f"qfsb{i}") for i in range(2)]
        B_nrm = [p.buf(f"nrm{i}") for i in range(2)]
        B_ao = [p.buf(f"aob{i}") for i in range(2)]
        B_aoT = [p.buf(f"aoT{i}") for i in range(2)]

        for g in range(NG):
            sl = g % 2
            for oc in range(DC):
                mb = B_med[oc % 2]
                qb = B_qsb[oc % 2]
                for kc in range(DC):
                    EM("pe", lambda e, oc=oc, kc=kc, g=g: e.matmul(
                        ps_med[oc % 2][:], wq[:, kc, oc * 128:(oc + 1) * 128],
                        xT3[:, kc, g * 512:(g + 1) * 512],
                        start=(kc == 0), stop=(kc == DC - 1)),
                       reads=[B_wq] + [B_xT[4 * g + j] for j in range(4)], writes=[mb])
                EM("act", lambda e, oc=oc: e.activation(qsb[oc % 2][:], ps_med[oc % 2][:],
                                                        AF.Copy),
                   reads=[mb], writes=[qb])
                ax = B_aux[oc % 2]
                EM("pe", lambda e, oc=oc: e.matmul(
                    ps_aux[oc % 2][:], wfpair[:], qsb[oc % 2][:],
                    start=True, stop=True),
                   reads=[qb, B_const], writes=[ax])
                EM("act", lambda e, oc=oc, sl=sl: e.activation(
                    qfsb[sl][:, oc, :], ps_aux[oc % 2][:], AF.Relu, bias=bf_col[:, 0:1],
                    scale=1.0),
                   reads=[ax], hard_reads=[B_const], writes=[B_qf[sl]])
            for bb in range(4):
                t = 4 * g + bb
                ts = t % 2
                bg = B_big[ts]
                ax = B_aux[ts]
                for pr in range(8):
                    EM("pe", lambda e, pr=pr, bb=bb, sl=sl, ts=ts: e.matmul(
                        ps_big[ts][:, pr * 128:(pr + 1) * 128],
                        qfsb[sl][:, pr, bb * 128:(bb + 1) * 128],
                        kvt3[:, pr, 0:128], start=True, stop=True,
                        skip_group_check=True),
                       reads=[B_qf[sl], B_kvtbf], writes=[bg])
                    EM("pe", lambda e, pr=pr, bb=bb, sl=sl, ts=ts: e.matmul(
                        ps_aux[ts][:, pr * 2: pr * 2 + 2],
                        qfsb[sl][:, pr, bb * 128:(bb + 1) * 128],
                        kvt3[:, pr, 128:130], start=True, stop=True,
                        skip_group_check=True),
                       reads=[B_qf[sl], B_kvtbf], writes=[ax])
                EM("dve", lambda e, ts=ts: e.tensor_scalar(nrm[ts][:], ps_aux[ts][:, 0:NH],
                                                           1e-6, None, op0=OP.max),
                   reads=[ax], writes=[B_nrm[ts]])
                EM("dve", lambda e, ts=ts: e.reciprocal(nrm[ts][:], nrm[ts][:]),
                   reads=[B_nrm[ts]], writes=[B_nrm[ts]], asyncwb=True, hard_in=True)
                for hh in range(NH):
                    EM("dve", lambda e, hh=hh, ts=ts: e.tensor_scalar(
                        aob[ts][:, hh * 64:(hh + 1) * 64], ps_big[ts][:, hh * 64:(hh + 1) * 64],
                        nrm[ts][:, hh:hh + 1], None, op0=OP.mult),
                       reads=[bg], hard_reads=[B_nrm[ts]], writes=[B_ao[ts]])
                for c in range(DC):
                    EM("pe", lambda e, c=c, ts=ts: e.matmul(
                        ps_big[ts][:, c * 128:(c + 1) * 128], aob[ts][:, c * 128:(c + 1) * 128],
                        id_bf[:], start=True, stop=True),
                       reads=[B_ao[ts], B_const], writes=[bg])
                EM("act", lambda e, ts=ts: e.activation(aoT[ts][:], ps_big[ts][:], AF.Copy),
                   reads=[bg], writes=[B_aoT[ts]])
                for nh2 in range(2):
                    for kc in range(DC):
                        EM("pe", lambda e, kc=kc, nh2=nh2, ts=ts: e.matmul(
                            ps_big[ts][:, nh2 * 512:(nh2 + 1) * 512],
                            aoT[ts][:, kc * 128:(kc + 1) * 128],
                            wo[:, kc, nh2 * 512:(nh2 + 1) * 512],
                            start=(kc == 0), stop=False),
                           reads=[B_aoT[ts], B_wo], writes=[bg])
                    EM("pe", lambda e, nh2=nh2, ts=ts: e.matmul(
                        ps_big[ts][:, nh2 * 512:(nh2 + 1) * 512], ones_bf[:],
                        bo_row[0:1, nh2 * 512:(nh2 + 1) * 512], start=False, stop=True),
                       reads=[B_const], writes=[bg])
                EM("dve", lambda e, t=t, ts=ts: e.tensor_tensor(
                    resid3[:, t, :], resid3[:, t, :], ps_big[ts][:], op=OP.add),
                   reads=[B_resid[t], bg], writes=[B_resid[t]])
        bctx.close()
        p.fence()


    subs = [attn_a, ffn, attn_b, ffn]
    for i in range(n_sub):
        subs[i]()

    B_y = p.buf("y")
    dma("sp", y_out.rearrange("(t q) d -> q t d", q=128), resid3,
        reads=B_resid, writes=[B_y])
    if DEBUG_DUMP:
        p.fence()
        for name, sb_t, shape, dt in [
            ("dbg_logits", logits_all, [128, 6 * NT], F32),
            ("dbg_probs", probs_all, [128, 6 * NT], F32),
            ("dbg_mv", mv_all, [128, 2 * NT], F32),
            ("dbg_rstd", rstd_all, [128, NT], F32),
            ("dbg_xT", xT, [128, DC * TOK], BF16),
            ("dbg_probsT", probsT, [6, TOK], BF16),
            ("dbg_sm_e", sm_e[1], [128, 6], F32),
            ("dbg_sm_c", sm_c[1], [128, 6], F32),
            ("dbg_sm_m", sm_m[1], [128, 1], F32),
            ("dbg_sm_m2", sm_m2[1], [128, 1], F32),
            ("dbg_sm_s", sm_s[1], [128, 1], F32),
        ] + dbg_extra:
            o = nc.dram_tensor(name, shape, dt, kind="ExternalOutput").ap()
            dma("sp", o, sb_t[:], reads=[], writes=[p.buf(name)])
    p.final_wait("sp")
    return nc


def kernel(**inputs):
    x = np.ascontiguousarray(np.asarray(inputs["L0_attn_a_x"], dtype=np.float32)
                             .reshape(B * L, D))
    common = {n: np.ascontiguousarray(np.asarray(inputs[n], dtype=np.float32))
              for n in WEIGHT_NAMES}
    ident = np.eye(128, dtype=np.float32)
    nc = build_program(4)
    in_maps = []
    for c in range(NCORES):
        m = dict(common)
        m["x_shard"] = np.ascontiguousarray(x[c * TOK:(c + 1) * TOK])
        m["ident"] = ident
        in_maps.append(m)
    res = run_bass_kernel_spmd(nc, in_maps, core_ids=list(range(NCORES)))
    y = np.concatenate([res.results[c]["y"] for c in range(NCORES)], axis=0)
    return y.reshape(B, L, D).astype(np.float32)


# version 04
# speedup vs baseline: 1.0294x; 1.0063x over previous
import numpy as np
from contextlib import ExitStack

import concourse.bass as bass
import concourse.mybir as mybir
from concourse.alu_op_type import AluOpType as OP
from concourse.bass_utils import run_bass_kernel_spmd

F32 = mybir.dt.float32
BF16 = mybir.dt.bfloat16
AF = mybir.ActivationFunctionType
AX = mybir.AxisListType

NCORES = 8
B, L, D = 4, 4096, 1024
TOK = (B * L) // NCORES
NT = TOK // 128
NG = TOK // 512
DC = D // 128
H = 512
E, FF = 6, 1536
FH = FF // 2
NH, HD, FD = 16, 64, 64
EPS = 1e-5


class Buf:
    __slots__ = ("name", "lw", "readers", "fence", "first")

    def __init__(self, name, fence):
        self.name = name
        self.lw = None
        self.readers = {}
        self.fence = fence
        self.first = True


class Prog:
    SEMS = ["pe", "act", "dve", "pool", "sp", "poold", "cc"]

    def __init__(self, nc):
        self.nc = nc
        self.eng = {"pe": nc.tensor, "act": nc.scalar, "dve": nc.vector,
                    "pool": nc.gpsimd, "sp": nc.sync}
        self.ctx = ExitStack()
        self.sems = {s: self.ctx.enter_context(nc.semaphore("s_" + s)) for s in self.SEMS}
        self.cnt = {s: 0 for s in self.SEMS}
        self.waited = {s: {} for s in self.eng}
        self.fence_counts = {s: 0 for s in self.SEMS}

    def buf(self, name):
        return Buf(name, dict(self.fence_counts))

    def fence(self):
        self.fence_counts = dict(self.cnt)

    def sb(self, name, shape, dt, ctx=None):
        side = None if ctx is None else "right"
        return (ctx or self.ctx).enter_context(
            self.nc.sbuf_tensor(name, shape, dt, side=side))

    def psum(self, name, shape, dt, ctx=None):
        return (ctx or self.ctx).enter_context(self.nc.psum_tensor(name, shape, dt))

    def emit(self, stream, fn, reads=(), hard_reads=(), writes=(), dma=False, asyncwb=False,
             hard_in=False, cc=False):
        hard_reads = list(hard_reads) + list(reads)
        reads = ()
        if cc:
            semname_own = "cc"
        elif dma and stream == "pool":
            semname_own = "poold"
        else:
            semname_own = stream
        need = {}

        def add(sem, val):
            if val > need.get(sem, 0):
                need[sem] = val

        for b in hard_reads:
            if b.lw is not None:
                s2, v2, _ = b.lw
                add(s2, v2)
        for b in writes:
            if b.first:
                for s2, v2 in b.fence.items():
                    if s2 != semname_own and v2 > 0:
                        add(s2, v2)
            if b.lw is not None:
                s2, v2, hard = b.lw
                if s2 != semname_own or hard:
                    add(s2, v2)
            for s2, v2 in b.readers.items():
                if s2 != semname_own:
                    add(s2, v2)
        eng = self.eng[stream]
        for semname, val in sorted(need.items()):
            if self.waited[stream].get(semname, 0) < val:
                eng.wait_ge(self.sems[semname], val)
                self.waited[stream][semname] = val
        inst = fn(eng)
        inc = 16 if dma else 1
        inst.then_inc(self.sems[semname_own], inc)
        self.cnt[semname_own] += inc
        v = self.cnt[semname_own]
        for b in writes:
            b.lw = (semname_own, v, asyncwb)
            b.readers = {}
            b.first = False
        for b in list(reads) + list(hard_reads):
            if b.readers.get(semname_own, 0) < v:
                b.readers[semname_own] = v

    def final_wait(self, stream="sp"):
        for semname in self.SEMS:
            v = self.cnt[semname]
            if v > 0 and self.waited[stream].get(semname, 0) < v:
                self.eng[stream].wait_ge(self.sems[semname], v)
                self.waited[stream][semname] = v


DEBUG_DUMP = False
STOP_AFTER = None
PASSA_LEVEL = 4

WEIGHT_NAMES = [
    "L0_attn_a_r1_w", "L0_attn_a_r1_b", "L0_attn_a_ln_g", "L0_attn_a_ln_b",
    "L0_attn_a_r2_w", "L0_attn_a_r2_b", "L0_attn_a_temp", "L0_attn_a_route_bias",
    "L0_attn_a_W1", "L0_attn_a_b1", "L0_attn_a_W2", "L0_attn_a_b2",
    "L0_attn_b_Wq", "L0_attn_b_Wk", "L0_attn_b_Wv", "L0_attn_b_Wo",
    "L0_attn_b_bo", "L0_attn_b_Wf", "L0_attn_b_bf",
    "L0_ffn_weight", "L0_ffn_aux_weight1", "L0_ffn_aux_weight2", "L0_ffn_bias",
]


def build_program(n_sub=4, debug_out=None):
    nc = bass.Bass("TRN2", target_bir_lowering=False, debug=False)
    p = Prog(nc)
    dbg_extra = []
    nc_in = {}
    shapes = {
        "L0_attn_a_r1_w": [D, H], "L0_attn_a_r1_b": [H], "L0_attn_a_ln_g": [H],
        "L0_attn_a_ln_b": [H], "L0_attn_a_r2_w": [H, E], "L0_attn_a_r2_b": [E],
        "L0_attn_a_temp": [1], "L0_attn_a_route_bias": [E],
        "L0_attn_a_W1": [E, D, FF], "L0_attn_a_b1": [E, FF],
        "L0_attn_a_W2": [E, FF, D], "L0_attn_a_b2": [E, D],
        "L0_attn_b_Wq": [D, D], "L0_attn_b_Wk": [D, D], "L0_attn_b_Wv": [D, D],
        "L0_attn_b_Wo": [D, D], "L0_attn_b_bo": [D], "L0_attn_b_Wf": [HD, FD],
        "L0_attn_b_bf": [FD], "L0_ffn_weight": [D, D], "L0_ffn_aux_weight1": [D, D],
        "L0_ffn_aux_weight2": [D, D], "L0_ffn_bias": [D],
    }
    for n in WEIGHT_NAMES:
        nc_in[n] = nc.dram_tensor(n, shapes[n], F32, kind="ExternalInput").ap()
    x_in = nc.dram_tensor("x_shard", [TOK, D], F32, kind="ExternalInput").ap()
    ident_in = nc.dram_tensor("ident", [128, 128], F32, kind="ExternalInput").ap()
    y_out = nc.dram_tensor("y", [TOK, D], F32, kind="ExternalOutput").ap()
    cc_in = nc.dram_tensor("cc_in", [128, 8 * 130], F32, kind="Internal").ap()
    cc_out = nc.dram_tensor("cc_out", [128, 8 * 130], F32, kind="Internal").ap()

    resid = p.sb("resid", [128, NT * D], F32)
    resid3 = resid.ap().rearrange("q (t d) -> q t d", t=NT)
    xT = p.sb("xT", [128, DC * TOK], BF16)
    xT3 = xT.ap().rearrange("q (c n) -> q c n", c=DC)
    id_f32 = p.sb("id_f32", [128, 128], F32)
    id_bf = p.sb("id_bf", [128, 128], BF16)
    ones_bf = p.sb("ones_bf", [1, 128], BF16)
    ones_f32 = p.sb("ones_f32", [1, 128], F32)
    ones_col_bf = p.sb("ones_col_bf", [128, 1], BF16)
    pow_exp = p.sb("pow_exp", [128, 1], F32)
    g_bc = p.sb("g_bc", [128, H], F32)
    b_bc = p.sb("b_bc", [128, H], F32)
    rb_bc = p.sb("rb_bc", [128, E], F32)
    rt_col = p.sb("rt_col", [128, 1], F32)
    r1b_row = p.sb("r1b_row", [1, H], BF16)
    r2b_row = p.sb("r2b_row", [1, E], BF16)
    ffnb_row = p.sb("ffnb_row", [1, D], BF16)
    bo_row = p.sb("bo_row", [1, D], BF16)
    bf_col = p.sb("bf_col", [128, 1], F32)
    bf_row16 = p.sb("bf_row16", [1, D], BF16)
    wfpad_e = p.sb("wfpad_e", [128, FD], BF16)
    wfpad_o = p.sb("wfpad_o", [128, FD], BF16)
    wfpair = p.sb("wfpair", [128, 128], BF16)
    b1_cols = p.sb("b1_cols", [128, 72], F32)
    b2_sb = p.sb("b2_sb", [6, D], BF16)
    sctx = ExitStack()
    b1_72 = p.sb("b1_72", [72, 128], F32, sctx)
    temp_sb = p.sb("temp_sb", [1, 1], F32, sctx)
    rt_row = p.sb("rt_row", [1, 1], F32, sctx)
    grow = p.sb("grow", [1, H], F32, sctx)
    brow = p.sb("brow", [1, H], F32, sctx)
    rbrow = p.sb("rbrow", [1, E], F32, sctx)

    st6 = [p.sb(f"st6_{i}", [128, 12], F32) for i in range(2)]
    mv_all = p.sb("mv_all", [128, 2 * NT], F32)
    veps_all = p.sb("veps_all", [128, NT], F32)
    rstd_all = p.sb("rstd_all", [128, NT], F32)
    xln = [p.sb(f"xln_{i}", [128, D], BF16) for i in range(2)]
    logits_all = p.sb("logits_all", [128, 6 * NT], F32)
    probs_all = p.sb("probs_all", [128, 6 * NT], F32)
    probsT = p.sb("probsT", [6, TOK], BF16)
    mv2 = [p.sb(f"mv2_{i}", [128, 2], F32) for i in range(2)]
    veps2 = [p.sb(f"veps2_{i}", [128, 1], F32) for i in range(2)]
    rstd2 = [p.sb(f"rstd2_{i}", [128, 1], F32) for i in range(2)]
    t1 = [p.sb(f"t1_{i}", [128, H], F32) for i in range(2)]
    h2b = [p.sb(f"h2b_{i}", [128, H], BF16) for i in range(2)]
    h2T = [p.sb(f"h2T_{i}", [128, H], BF16) for i in range(2)]
    sm_e = [p.sb(f"sm_e_{i}", [128, 6], F32) for i in range(2)]
    sm_c = [p.sb(f"sm_c_{i}", [128, 6], F32) for i in range(2)]
    sm_m = [p.sb(f"sm_m_{i}", [128, 1], F32) for i in range(2)]
    sm_m2 = [p.sb(f"sm_m2_{i}", [128, 1], F32) for i in range(2)]
    sm_s = [p.sb(f"sm_s_{i}", [128, 1], F32) for i in range(2)]
    probs_bf = [p.sb(f"probs_bf_{i}", [128, 6], BF16) for i in range(2)]

    ps_big = [p.psum(f"ps_big{i}", [128, 1024], F32) for i in range(2)]
    ps_med = [p.psum(f"ps_med{i}", [128, 512], F32) for i in range(2)]
    ps_aux = [p.psum(f"ps_aux{i}", [128, 512], F32) for i in range(2)]

    B_resid = [p.buf(f"resid{t}") for t in range(NT)]
    B_xT = [p.buf(f"xT{t}") for t in range(NT)]
    B_const = p.buf("const")
    B_st6 = [p.buf(f"st6_{i}") for i in range(2)]
    B_mv = [p.buf(f"mv{t}") for t in range(NT)]
    B_veps = [p.buf(f"veps{t}") for t in range(NT)]
    B_rstd = [p.buf(f"rstd{t}") for t in range(NT)]
    B_xln = [p.buf(f"xln{i}") for i in range(2)]
    B_logits = [p.buf(f"logits{t}") for t in range(NT)]
    B_probs = [p.buf(f"probs{t}") for t in range(NT)]
    B_probsT = [p.buf(f"probsT{t}") for t in range(NT)]
    B_mv2 = [p.buf(f"mv2_{i}") for i in range(2)]
    B_veps2 = [p.buf(f"veps2_{i}") for i in range(2)]
    B_rstd2 = [p.buf(f"rstd2_{i}") for i in range(2)]
    B_t1 = [p.buf(f"t1_{i}") for i in range(2)]
    B_h2b = [p.buf(f"h2b_{i}") for i in range(2)]
    B_h2T = [p.buf(f"h2T_{i}") for i in range(2)]
    B_sm = [p.buf(f"sm{i}") for i in range(2)]
    B_pbf = [p.buf(f"pbf{i}") for i in range(2)]
    B_big = [p.buf(f"big{i}") for i in range(2)]
    B_med = [p.buf(f"med{i}") for i in range(2)]
    B_aux = [p.buf(f"aux{i}") for i in range(2)]

    EM = p.emit

    def dma(stream, out_ap, in_ap, writes, reads=(), **kw):
        EM(stream, lambda e: e.dma_start(out=out_ap, in_=in_ap, **kw),
           reads=reads, writes=writes, dma=True)

    dma("sp", resid3, x_in.rearrange("(t q) d -> q t d", q=128), writes=B_resid)
    dma("sp", id_f32[:], ident_in, writes=[B_const])
    dma("pool", id_bf[:], ident_in, writes=[B_const])
    dma("pool", r1b_row[:], nc_in["L0_attn_a_r1_b"][None, :], writes=[B_const])
    dma("pool", r2b_row[:], nc_in["L0_attn_a_r2_b"][None, :], writes=[B_const])
    dma("pool", ffnb_row[:], nc_in["L0_ffn_bias"][None, :], writes=[B_const])
    dma("pool", bo_row[:], nc_in["L0_attn_b_bo"][None, :], writes=[B_const])
    dma("pool", b2_sb[:], nc_in["L0_attn_a_b2"], writes=[B_const])
    EM("dve", lambda e: e.memset(wfpad_e[:], 0.0), writes=[B_const])
    EM("dve", lambda e: e.memset(wfpad_o[:], 0.0), writes=[B_const])
    EM("dve", lambda e: e.memset(wfpair[:], 0.0), writes=[B_const])
    dma("pool", wfpad_e[0:64, :], nc_in["L0_attn_b_Wf"], reads=[B_const], writes=[B_const])
    dma("pool", wfpad_o[64:128, :], nc_in["L0_attn_b_Wf"], reads=[B_const], writes=[B_const])
    dma("pool", wfpair[0:64, 0:64], nc_in["L0_attn_b_Wf"], reads=[B_const], writes=[B_const])
    dma("pool", wfpair[64:128, 64:128], nc_in["L0_attn_b_Wf"], reads=[B_const], writes=[B_const])
    for hh in range(NH):
        dma("pool", bf_row16[0:1, hh * FD:(hh + 1) * FD], nc_in["L0_attn_b_bf"][None, :],
            writes=[B_const])
    dma("sp", bf_col[0:64, :], nc_in["L0_attn_b_bf"][:, None], writes=[B_const])
    dma("sp", bf_col[64:128, :], nc_in["L0_attn_b_bf"][:, None], writes=[B_const])
    dma("sp", b1_72[:], nc_in["L0_attn_a_b1"].rearrange("e (c q) -> (e c) q", q=128),
        writes=[B_const])
    dma("sp", temp_sb[:], nc_in["L0_attn_a_temp"][None, :], writes=[B_const])
    dma("sp", grow[:], nc_in["L0_attn_a_ln_g"][None, :], writes=[B_const])
    dma("sp", brow[:], nc_in["L0_attn_a_ln_b"][None, :], writes=[B_const])
    dma("sp", rbrow[:], nc_in["L0_attn_a_route_bias"][None, :], writes=[B_const])

    EM("dve", lambda e: e.memset(ones_bf[:], 1.0), writes=[B_const])
    EM("dve", lambda e: e.memset(ones_f32[:], 1.0), writes=[B_const])
    EM("dve", lambda e: e.memset(ones_col_bf[:], 1.0), writes=[B_const])
    EM("dve", lambda e: e.memset(pow_exp[:], -0.5), writes=[B_const])
    EM("dve", lambda e: e.reciprocal(rt_row[:], temp_sb[:]), reads=[B_const], writes=[B_const])

    def bcast(row_ap, ncols, dst_ap):
        EM("pe", lambda e: e.matmul(ps_med[0][:, 0:ncols], ones_f32[:], row_ap,
                                    start=True, stop=True),
           reads=[B_const], writes=[B_med[0]])
        EM("act", lambda e: e.activation(dst_ap, ps_med[0][:, 0:ncols], AF.Copy),
           reads=[B_med[0]], writes=[B_const])

    bcast(grow[:], H, g_bc[:])
    bcast(brow[:], H, b_bc[:])
    bcast(rbrow[:], E, rb_bc[:])
    bcast(rt_row[:], 1, rt_col[:])
    EM("pe", lambda e: e.matmul(ps_med[1][:, 0:72], b1_72[:], id_f32[0:72, 0:72],
                                start=True, stop=True),
       reads=[B_const], writes=[B_med[1]])
    EM("act", lambda e: e.activation(b1_cols[:], ps_med[1][:, 0:72], AF.Copy),
       reads=[B_med[1]], writes=[B_const])
    sctx.close()
    p.fence()

    def ln_stats(t):
        s = t % 2
        EM("dve", lambda e: e.bn_stats(st6[s][:, 0:6], resid3[:, t, 0:512]),
           reads=[B_resid[t]], writes=[B_st6[s]], asyncwb=True, hard_in=True)
        EM("dve", lambda e: e.bn_stats(st6[s][:, 6:12], resid3[:, t, 512:1024]),
           reads=[B_resid[t]], writes=[B_st6[s]], asyncwb=True, hard_in=True)
        EM("dve", lambda e: e.bn_aggr(mv_all[:, 2 * t:2 * t + 2], st6[s][:]),
           reads=[B_st6[s]], writes=[B_mv[t]], asyncwb=True, hard_in=True)
        EM("dve", lambda e: e.tensor_scalar(veps_all[:, t:t + 1], mv_all[:, 2 * t + 1:2 * t + 2],
                                            EPS, None, op0=OP.add),
           reads=[B_mv[t]], writes=[B_veps[t]])
        EM("pool", lambda e: e.tensor_tensor(rstd_all[:, t:t + 1], veps_all[:, t:t + 1],
                                             pow_exp[:], op=OP.pow),
           reads=[B_veps[t], B_const], writes=[B_rstd[t]], asyncwb=True, hard_in=True)

    def ln_apply_transpose(t):
        s = t % 2
        EM("dve", lambda e: e.tensor_scalar(xln[s][:], resid3[:, t, :],
                                            mv_all[:, 2 * t:2 * t + 1],
                                            rstd_all[:, t:t + 1],
                                            op0=OP.subtract, op1=OP.mult),
           reads=[B_resid[t]], hard_reads=[B_mv[t], B_rstd[t]], writes=[B_xln[s]])
        bp = B_big[t % 2]
        for c in range(DC):
            EM("pe", lambda e, c=c: e.matmul(ps_big[t % 2][:, c * 128:(c + 1) * 128],
                                             xln[s][:, c * 128:(c + 1) * 128], id_bf[:],
                                             start=True, stop=True),
               reads=[B_xln[s], B_const], writes=[bp])
        EM("act", lambda e: e.activation(
            xT3[:, :, t * 128:(t + 1) * 128],
            ps_big[t % 2].ap().rearrange("q (c n) -> q c n", c=8), AF.Copy),
           reads=[bp], writes=[B_xT[t]])

    def ln_pass():
        if STOP_AFTER == "stats":
            for t in range(NT):
                ln_stats(t)
            return
        for t in range(NT):
            ln_stats(t)
            if t >= 2:
                ln_apply_transpose(t - 2)
        ln_apply_transpose(NT - 2)
        ln_apply_transpose(NT - 1)

    def attn_a():
        actx = ExitStack()
        r1w = p.sb("r1w", [128, DC, H], BF16, actx)
        r2w = p.sb("r2w", [128, 4, E], BF16, actx)
        B_r1w, B_r2w = p.buf("r1w"), p.buf("r2w")
        dma("pool", r1w[:], nc_in["L0_attn_a_r1_w"].rearrange("(c q) h -> q c h", q=128),
            writes=[B_r1w])
        dma("pool", r2w[:], nc_in["L0_attn_a_r2_w"].rearrange("(c q) s -> q c s", q=128),
            writes=[B_r2w])

        ln_pass()
        if STOP_AFTER in ("ln", "stats"):
            actx.close(); p.fence(); return

        for t in range(NT):
            s = t % 2
            hb = B_med[s]
            for c in range(DC):
                EM("pe", lambda e, c=c: e.matmul(ps_med[s][:], xT3[:, c, t * 128:(t + 1) * 128],
                                                 r1w[:, c, :], start=(c == 0), stop=False),
                   reads=[B_xT[t], B_r1w], writes=[hb])
            EM("pe", lambda e: e.matmul(ps_med[s][:], ones_bf[:], r1b_row[:],
                                        start=False, stop=True),
               reads=[B_const], writes=[hb])
            EM("dve", lambda e: e.bn_stats(st6[s][:, 0:6], ps_med[s][:]),
               reads=[hb], writes=[B_st6[s]], asyncwb=True, hard_in=True)
            EM("dve", lambda e: e.bn_aggr(mv2[s][:], st6[s][:, 0:6]),
               reads=[B_st6[s]], writes=[B_mv2[s]], asyncwb=True, hard_in=True)
            EM("dve", lambda e: e.tensor_scalar(veps2[s][:], mv2[s][:, 1:2], EPS, None, op0=OP.add),
               reads=[B_mv2[s]], writes=[B_veps2[s]])
            EM("pool", lambda e: e.tensor_tensor(rstd2[s][:], veps2[s][:], pow_exp[:], op=OP.pow),
               reads=[B_veps2[s], B_const], writes=[B_rstd2[s]], asyncwb=True, hard_in=True)
            EM("dve", lambda e: e.tensor_scalar(t1[s][:], ps_med[s][:], mv2[s][:, 0:1],
                                                rstd2[s][:], op0=OP.subtract, op1=OP.mult),
               reads=[hb], hard_reads=[B_mv2[s], B_rstd2[s]], writes=[B_t1[s]])
            EM("dve", lambda e: e.tensor_tensor(t1[s][:], t1[s][:], g_bc[:], op=OP.mult),
               reads=[B_t1[s], B_const], writes=[B_t1[s]])
            EM("dve", lambda e: e.tensor_tensor(t1[s][:], t1[s][:], b_bc[:], op=OP.add),
               reads=[B_t1[s], B_const], writes=[B_t1[s]])
            EM("act", lambda e: e.activation(h2b[s][:], t1[s][:], AF.Gelu),
               reads=[B_t1[s]], writes=[B_h2b[s]])
            bg = B_big[s]
            for c in range(4):
                EM("pe", lambda e, c=c: e.matmul(ps_big[s][:, c * 128:(c + 1) * 128],
                                                 h2b[s][:, c * 128:(c + 1) * 128], id_bf[:],
                                                 start=True, stop=True),
                   reads=[B_h2b[s], B_const], writes=[bg])
            EM("act", lambda e: e.activation(h2T[s][:], ps_big[s][:, 0:512], AF.Copy),
               reads=[bg], writes=[B_h2T[s]])
            ab = B_aux[s]
            for c in range(4):
                EM("pe", lambda e, c=c: e.matmul(ps_aux[s][:, 0:E],
                                                 h2T[s][:, c * 128:(c + 1) * 128],
                                                 r2w[:, c, :], start=(c == 0), stop=False),
                   reads=[B_h2T[s], B_r2w], writes=[ab])
            EM("pe", lambda e: e.matmul(ps_aux[s][:, 0:E], ones_bf[:], r2b_row[:],
                                        start=False, stop=True),
               reads=[B_const], writes=[ab])
            EM("dve", lambda e: e.scalar_tensor_tensor(
                logits_all[:, 6 * t:6 * t + 6], ps_aux[s][:, 0:E], rt_col[:, 0:1], rb_bc[:],
                op0=OP.mult, op1=OP.add),
               reads=[ab, B_const], hard_reads=[B_const], writes=[B_logits[t]])

        if STOP_AFTER == "router":
            actx.close(); p.fence(); return
        for t in range(NT):
            s = t % 2
            lsl = logits_all[:, 6 * t:6 * t + 6]
            psl = probs_all[:, 6 * t:6 * t + 6]
            EM("dve", lambda e: e.reduce_max(sm_m[s][:], lsl, axis=AX.X),
               reads=[B_logits[t]], writes=[B_sm[s]], asyncwb=True, hard_in=True)
            EM("dve", lambda e: e.tensor_scalar(sm_m[s][:], sm_m[s][:], -1.0, None, op0=OP.mult),
               reads=[B_sm[s]], writes=[B_sm[s]])
            EM("act", lambda e: e.activation(sm_e[s][:], lsl, AF.Exp, bias=sm_m[s][:, 0:1],
                                             scale=1.0),
               reads=[B_logits[t]], hard_reads=[B_sm[s]], writes=[B_sm[s]])
            EM("dve", lambda e: e.reduce_sum(sm_s[s][:], sm_e[s][:], axis=AX.X),
               reads=[B_sm[s]], writes=[B_sm[s]], asyncwb=True, hard_in=True)
            EM("dve", lambda e: e.reciprocal(sm_s[s][:], sm_s[s][:]),
               reads=[B_sm[s]], writes=[B_sm[s]])
            EM("dve", lambda e: e.tensor_scalar(psl, sm_e[s][:], sm_s[s][:, 0:1], None,
                                                op0=OP.mult),
               reads=[B_sm[s]], hard_reads=[B_sm[s]], writes=[B_probs[t]])
            EM("dve", lambda e: e.reduce_max(sm_m[s][:], psl, axis=AX.X),
               reads=[B_probs[t]], writes=[B_sm[s]], asyncwb=True, hard_in=True)
            EM("dve", lambda e: e.tensor_scalar(sm_c[s][:], psl, sm_m[s][:, 0:1], None,
                                                op0=OP.is_equal),
               reads=[B_probs[t]], hard_reads=[B_sm[s]], writes=[B_sm[s]])
            EM("dve", lambda e: e.scalar_tensor_tensor(sm_e[s][:], sm_c[s][:], -1e30, psl,
                                                       op0=OP.mult, op1=OP.add),
               reads=[B_sm[s], B_probs[t]], writes=[B_sm[s]])
            EM("dve", lambda e: e.reduce_max(sm_m2[s][:], sm_e[s][:], axis=AX.X),
               reads=[B_sm[s]], writes=[B_sm[s]], asyncwb=True, hard_in=True)
            EM("dve", lambda e: e.tensor_scalar(sm_c[s][:], psl, sm_m2[s][:, 0:1], None,
                                                op0=OP.is_ge),
               reads=[B_probs[t]], hard_reads=[B_sm[s]], writes=[B_sm[s]])
            EM("dve", lambda e: e.tensor_tensor(psl, psl, sm_c[s][:], op=OP.mult),
               reads=[B_probs[t], B_sm[s]], writes=[B_probs[t]])
            EM("dve", lambda e: e.tensor_copy(probs_bf[s][:], psl),
               reads=[B_probs[t]], writes=[B_pbf[s]])
            EM("pe", lambda e: e.matmul(ps_aux[s][0:6, 0:128], probs_bf[s][:], id_bf[:],
                                        start=True, stop=True),
               reads=[B_pbf[s], B_const], writes=[B_aux[s]])
            EM("act", lambda e: e.activation(probsT.ap()[:, t * 128:(t + 1) * 128],
                                             ps_aux[s][0:6, 0:128], AF.Copy),
               reads=[B_aux[s]], writes=[B_probsT[t]])

        if STOP_AFTER == "softmax":
            actx.close(); p.fence(); return
        for t in range(NT):
            s = t % 2
            bg = B_big[s]
            for nh2 in range(2):
                EM("pe", lambda e, nh2=nh2: e.matmul(
                    ps_big[s][:, nh2 * 512:(nh2 + 1) * 512],
                    probsT.ap()[:, t * 128:(t + 1) * 128],
                    b2_sb[:, nh2 * 512:(nh2 + 1) * 512], start=True, stop=True),
                   reads=[B_probsT[t], B_const], writes=[bg])
            EM("dve", lambda e: e.tensor_tensor(resid3[:, t, :], resid3[:, t, :],
                                                ps_big[s][:], op=OP.add),
               reads=[B_resid[t], bg], writes=[B_resid[t]])

        if STOP_AFTER == "b2":
            actx.close(); p.fence(); return
        w1b = [p.sb(f"w1b{i}", [128, DC, FH], BF16, actx) for i in range(2)]
        w2b = [p.sb(f"w2b{i}", [128, 6, D], BF16, actx) for i in range(2)]
        actb = [p.sb(f"actb{i}", [128, 6, 512], BF16, actx) for i in range(2)]
        B_w1 = [p.buf(f"w1b{i}") for i in range(2)]
        B_w2 = [p.buf(f"w2b{i}") for i in range(2)]
        B_act = [p.buf(f"actb{i}") for i in range(2)]

        order = [0, 2, 4, 1, 3, 5]
        steps = [(eo, fh) for eo in order for fh in range(2)]

        def prefetch(i):
            ex, fh = steps[i]
            sl = i % 2
            dma("pool", w1b[sl][:],
                nc_in["L0_attn_a_W1"][ex, :, fh * FH:(fh + 1) * FH]
                .rearrange("(c q) f -> q c f", q=128), writes=[B_w1[sl]])
            dma("pool", w2b[sl][:],
                nc_in["L0_attn_a_W2"][ex, fh * FH:(fh + 1) * FH, :]
                .rearrange("(c q) n -> q c n", q=128), writes=[B_w2[sl]])

        def w1_group(i, g):
            ex, fh = steps[i]
            sl = i % 2
            func = AF.Gelu if ex % 2 == 0 else AF.Silu
            asl = g % 2
            for c in range(6):
                mb = B_med[c % 2]
                for kc in range(DC):
                    EM("pe", lambda e, c=c, kc=kc, g=g: e.matmul(
                        ps_med[c % 2][:], w1b[sl][:, kc, c * 128:(c + 1) * 128],
                        xT3[:, kc, g * 512:(g + 1) * 512],
                        start=(kc == 0), stop=(kc == DC - 1)),
                       reads=[B_w1[sl]] + [B_xT[4 * g + j] for j in range(4)],
                       writes=[mb])
                bcol = b1_cols[:, ex * 12 + fh * 6 + c: ex * 12 + fh * 6 + c + 1]
                EM("act", lambda e, c=c, g=g, func=func, bcol=bcol: e.activation(
                    actb[asl][:, c, :], ps_med[c % 2][:], func, bias=bcol, scale=1.0),
                   reads=[mb], hard_reads=[B_const], writes=[B_act[asl]])

        def eo_group(i, g):
            ex, fh = steps[i]
            sl = i % 2
            asl = g % 2
            for bb in range(4):
                t = 4 * g + bb
                bg = B_big[t % 2]
                for nh2 in range(2):
                    for c in range(6):
                        EM("pe", lambda e, c=c, nh2=nh2, bb=bb, g=g: e.matmul(
                            ps_big[t % 2][:, nh2 * 512:(nh2 + 1) * 512],
                            actb[asl][:, c, bb * 128:(bb + 1) * 128],
                            w2b[sl][:, c, nh2 * 512:(nh2 + 1) * 512],
                            start=(c == 0), stop=(c == 5)),
                           reads=[B_act[asl], B_w2[sl]], writes=[bg])
                EM("dve", lambda e, t=t: e.scalar_tensor_tensor(
                    resid3[:, t, :], ps_big[t % 2][:],
                    probs_all[:, 6 * t + ex: 6 * t + ex + 1], resid3[:, t, :],
                    op0=OP.mult, op1=OP.add),
                   reads=[bg, B_resid[t]], hard_reads=[B_probs[t]],
                   writes=[B_resid[t]])

        prefetch(0)
        stages = [(i, g) for i in range(len(steps)) for g in range(NG)]
        prefetched = 0
        for j, (i, g) in enumerate(stages):
            if g == 1 and i + 1 < len(steps) and prefetched <= i:
                prefetch(i + 1)
                prefetched = i + 1
            w1_group(i, g)
            if j >= 1:
                eo_group(*stages[j - 1])
        eo_group(*stages[-1])
        actx.close()
        p.fence()

    ffn_calls = [0]

    def ffn():
        tag = f"f{ffn_calls[0]}"
        ffn_calls[0] += 1
        fctx = ExitStack()
        wb = [p.sb(f"{tag}w{i}", [128, DC, D], BF16, fctx) for i in range(3)]
        B_wb = [p.buf(f"{tag}w{i}") for i in range(3)]
        h1sb = [p.sb(f"{tag}h1sb{i}", [128, DC, 512], BF16, fctx) for i in range(2)]
        h2sb = [p.sb(f"{tag}h2sb0", [128, DC, 512], BF16, fctx)] * 2
        tsb = [p.sb(f"{tag}tsb0", [128, D], F32, fctx)] * 2
        B_h1 = [p.buf(f"{tag}h1sb{i}") for i in range(2)]
        B_h2 = [p.buf(f"{tag}h2sb0")] * 2
        B_t = [p.buf(f"{tag}tsb0")] * 2
        for wi, wn in enumerate(["L0_ffn_weight", "L0_ffn_aux_weight1", "L0_ffn_aux_weight2"]):
            dma("pool", wb[wi][:], nc_in[wn].rearrange("(c q) n -> q c n", q=128),
                writes=[B_wb[wi]])

        ln_pass()

        def mm_fm(dst_sb, dst_buf, src_fn, src_bufs, w_i):
            for oc in range(DC):
                mb = B_med[oc % 2]
                for kc in range(DC):
                    EM("pe", lambda e, oc=oc, kc=kc: e.matmul(
                        ps_med[oc % 2][:], wb[w_i][:, kc, oc * 128:(oc + 1) * 128],
                        src_fn(kc), start=(kc == 0), stop=(kc == DC - 1)),
                       reads=[B_wb[w_i]] + src_bufs, writes=[mb])
                EM("act", lambda e, oc=oc: e.activation(dst_sb[:, oc, :], ps_med[oc % 2][:],
                                                        AF.Copy),
                   reads=[mb], writes=[dst_buf])

        for g in range(NG):
            sl = g % 2
            src_bufs = [B_xT[4 * g + j] for j in range(4)]
            mm_fm(h1sb[sl], B_h1[sl],
                  lambda kc, g=g: xT3[:, kc, g * 512:(g + 1) * 512], src_bufs, 0)
            mm_fm(h2sb[sl], B_h2[sl],
                  lambda kc, sl=sl: h1sb[sl][:, kc, :], [B_h1[sl]], 1)
            for bb in range(4):
                t = 4 * g + bb
                bg = B_big[t % 2]
                for nh2 in range(2):
                    for kc in range(DC):
                        EM("pe", lambda e, kc=kc, nh2=nh2, bb=bb: e.matmul(
                            ps_big[t % 2][:, nh2 * 512:(nh2 + 1) * 512],
                            h2sb[sl][:, kc, bb * 128:(bb + 1) * 128],
                            wb[2][:, kc, nh2 * 512:(nh2 + 1) * 512],
                            start=(kc == 0), stop=False),
                           reads=[B_h2[sl], B_wb[2]], writes=[bg])
                    EM("pe", lambda e, nh2=nh2: e.matmul(
                        ps_big[t % 2][:, nh2 * 512:(nh2 + 1) * 512], ones_bf[:],
                        ffnb_row[0:1, nh2 * 512:(nh2 + 1) * 512], start=False, stop=True),
                       reads=[B_const], writes=[bg])
                EM("act", lambda e, t=t: e.activation(tsb[t % 2][:], ps_big[t % 2][:], AF.Tanh),
                   reads=[bg], writes=[B_t[t % 2]])
                EM("dve", lambda e, t=t: e.tensor_tensor(resid3[:, t, :], resid3[:, t, :],
                                                         tsb[t % 2][:], op=OP.add),
                   reads=[B_resid[t], B_t[t % 2]], writes=[B_resid[t]])
        fctx.close()
        p.fence()

    def attn_b():
        bctx = ExitStack()
        wk = p.sb("wk", [128, DC, D], BF16, bctx)
        wv = p.sb("wv", [128, DC, D], BF16, bctx)
        B_wk, B_wv = p.buf("wk"), p.buf("wv")
        ksb = [p.sb(f"ksb{i}", [128, DC, 512], BF16, bctx) for i in range(2)]
        kfe = [p.sb(f"kfe{i}", [128, D], BF16, bctx) for i in range(2)]
        kfo = [p.sb(f"kfo{i}", [128, D], BF16, bctx) for i in range(2)]
        vev = [p.sb(f"vev{i}", [128, D], BF16, bctx) for i in range(2)]
        vod = [p.sb(f"vod{i}", [128, D], BF16, bctx) for i in range(2)]
        kv_all = p.sb("kv_all", [128, 8 * 130], F32, bctx)
        B_ksb = [p.buf(f"ksb{i}") for i in range(2)]
        B_kfe = [p.buf(f"kfe{i}") for i in range(2)]
        B_kfo = [p.buf(f"kfo{i}") for i in range(2)]
        B_vev = [p.buf(f"vev{i}") for i in range(2)]
        B_vod = [p.buf(f"vod{i}") for i in range(2)]
        B_kv_all = p.buf("kv_all")
        B_ccin, B_ccout = p.buf("cc_in"), p.buf("cc_out")
        for i in range(2):
            for bufT, bufB in ((kfe[i], B_kfe[i]), (kfo[i], B_kfo[i]),
                               (vev[i], B_vev[i]), (vod[i], B_vod[i])):
                EM("dve", lambda e, bufT=bufT: e.memset(bufT[:], 0.0), writes=[bufB])
        pview = lambda pst: pst.ap().rearrange("q (pr j) -> q pr j", pr=8)
        ev3 = lambda sbT: sbT.ap().rearrange("q (pr j) -> q pr j", pr=8)

        dma("pool", wk[:], nc_in["L0_attn_b_Wk"].rearrange("(c q) n -> q c n", q=128),
            writes=[B_wk])
        dma("pool", wv[:], nc_in["L0_attn_b_Wv"].rearrange("(c q) n -> q c n", q=128),
            writes=[B_wv])

        ln_pass()
        if STOP_AFTER == "bln":
            bctx.close(); p.fence(); return

        kv3 = kv_all.ap().rearrange("q (pr j) -> q pr j", pr=8)
        EM("dve", lambda e: e.memset(kv_all[:], 0.0), writes=[B_kv_all])
        B_ksump = B_aux[1]
        for g in range(NG):
            sl = g % 2
            for oc in range(DC):
                mb = B_med[oc % 2]
                for kc in range(DC):
                    EM("pe", lambda e, oc=oc, kc=kc, g=g: e.matmul(
                        ps_med[oc % 2][:], wk[:, kc, oc * 128:(oc + 1) * 128],
                        xT3[:, kc, g * 512:(g + 1) * 512],
                        start=(kc == 0), stop=(kc == DC - 1)),
                       reads=[B_wk] + [B_xT[4 * g + j] for j in range(4)], writes=[mb])
                EM("act", lambda e, oc=oc: e.activation(ksb[sl][:, oc, :], ps_med[oc % 2][:],
                                                        AF.Copy),
                   reads=[mb], writes=[B_ksb[sl]])
            for bb in range(4):
                t = 4 * g + bb
                ts = t % 2
                vb = B_big[ts]
                for nh2 in range(2):
                    for kc in range(DC):
                        EM("pe", lambda e, kc=kc, nh2=nh2, t=t: e.matmul(
                            ps_big[ts][:, nh2 * 512:(nh2 + 1) * 512],
                            xT3[:, kc, t * 128:(t + 1) * 128],
                            wv[:, kc, nh2 * 512:(nh2 + 1) * 512],
                            start=(kc == 0), stop=(kc == DC - 1)),
                           reads=[B_xT[t], B_wv], writes=[vb])
                EM("act", lambda e, ts=ts: e.activation(
                    ev3(vev[ts])[:, :, 0:64], pview(ps_big[ts])[:, :, 0:64], AF.Copy),
                   reads=[vb], writes=[B_vev[ts]])
                EM("act", lambda e, ts=ts: e.activation(
                    ev3(vod[ts])[:, :, 64:128], pview(ps_big[ts])[:, :, 64:128], AF.Copy),
                   reads=[vb], writes=[B_vod[ts]])
                if PASSA_LEVEL < 2:
                    continue
                for pr in range(8):
                    for par in range(2):
                        wf_p = wfpad_e if par == 0 else wfpad_o
                        EM("pe", lambda e, pr=pr, par=par, bb=bb, sl=sl, wf_p=wf_p: e.matmul(
                            ps_big[ts][:, pr * 128 + par * 64: pr * 128 + par * 64 + 64],
                            ksb[sl][:, pr, bb * 128:(bb + 1) * 128],
                            wf_p[:], start=((pr == 0 or pr == 4) and par == 0),
                            stop=False, skip_group_check=True),
                           reads=[B_ksb[sl], B_const], writes=[vb])
                for nh2 in range(2):
                    EM("pe", lambda e, nh2=nh2, ts=ts: e.matmul(
                        ps_big[ts][:, nh2 * 512:(nh2 + 1) * 512], ones_bf[:],
                        bf_row16[0:1, nh2 * 512:(nh2 + 1) * 512], start=False, stop=True,
                        skip_group_check=True),
                       reads=[B_const], writes=[vb])
                EM("act", lambda e, ts=ts: e.activation(
                    ev3(kfe[ts])[:, :, 0:64], pview(ps_big[ts])[:, :, 0:64], AF.Relu),
                   reads=[vb], writes=[B_kfe[ts]])
                EM("act", lambda e, ts=ts: e.activation(
                    ev3(kfo[ts])[:, :, 64:128], pview(ps_big[ts])[:, :, 64:128], AF.Relu),
                   reads=[vb], writes=[B_kfo[ts]])
                if PASSA_LEVEL < 3:
                    continue
                for pr in range(8):
                    mi = pr // 4
                    mcol = (pr % 4) * 128
                    EM("pe", lambda e, pr=pr, ts=ts, mi=mi, mcol=mcol: e.matmul(
                        ps_med[mi][:, mcol:mcol + 128],
                        kfe[ts][:, pr * 128:(pr + 1) * 128],
                        vev[ts][:, pr * 128:(pr + 1) * 128],
                        start=(pr == 0 or pr == 4), stop=False, skip_group_check=True),
                       reads=[B_kfe[ts], B_vev[ts]], writes=[B_med[mi]])
                    EM("pe", lambda e, pr=pr, ts=ts, mi=mi, mcol=mcol: e.matmul(
                        ps_med[mi][:, mcol:mcol + 128],
                        kfo[ts][:, pr * 128:(pr + 1) * 128],
                        vod[ts][:, pr * 128:(pr + 1) * 128],
                        start=False, stop=(pr == 3 or pr == 7), skip_group_check=True),
                       reads=[B_kfo[ts], B_vod[ts]], writes=[B_med[mi]])
                    EM("pe", lambda e, pr=pr, ts=ts: e.matmul(
                        ps_aux[1][:, pr * 2: pr * 2 + 1],
                        kfe[ts][:, pr * 128:(pr + 1) * 128], ones_col_bf[:],
                        start=(pr == 0), stop=False, skip_group_check=True),
                       reads=[B_kfe[ts], B_const], writes=[B_ksump])
                    EM("pe", lambda e, pr=pr, ts=ts: e.matmul(
                        ps_aux[1][:, pr * 2 + 1: pr * 2 + 2],
                        kfo[ts][:, pr * 128:(pr + 1) * 128], ones_col_bf[:],
                        start=False, stop=(pr == 7), skip_group_check=True),
                       reads=[B_kfo[ts], B_const], writes=[B_ksump])
                EM("dve", lambda e: e.tensor_tensor(
                    kv3[:, 0:4, 0:128], kv3[:, 0:4, 0:128],
                    ps_med[0].ap().rearrange("q (pr j) -> q pr j", pr=4), op=OP.add),
                   reads=[B_med[0], B_kv_all], writes=[B_kv_all])
                EM("dve", lambda e: e.tensor_tensor(
                    kv3[:, 4:8, 0:128], kv3[:, 4:8, 0:128],
                    ps_med[1].ap().rearrange("q (pr j) -> q pr j", pr=4), op=OP.add),
                   reads=[B_med[1], B_kv_all], writes=[B_kv_all])
                EM("dve", lambda e: e.tensor_tensor(
                    kv3[:, :, 128:130], kv3[:, :, 128:130],
                    ps_aux[1].ap()[:, 0:16].rearrange("q (pr j) -> q pr j", pr=8), op=OP.add),
                   reads=[B_ksump, B_kv_all], writes=[B_kv_all])

        if STOP_AFTER == "kvacc":
            dbg_extra.append(("dbg_kv", kv_all, [128, 8 * 130], F32))
            return
        dma("sp", cc_in, kv_all[:], reads=[B_kv_all], writes=[B_ccin])
        EM("pool", lambda e: e.collective_compute(
            "AllReduce", OP.add, replica_groups=[[0, 1], [2, 3], [4, 5], [6, 7]],
            ins=[cc_in.opt()], outs=[cc_out.opt()]),
           reads=[B_ccin], writes=[B_ccout], asyncwb=True, hard_in=True, cc=True)

        bctx.close()
        p.fence()
        bctx = ExitStack()
        wq = p.sb("wq", [128, DC, D], BF16, bctx)
        wo = p.sb("wo", [128, DC, D], BF16, bctx)
        B_wq, B_wo = p.buf("wq"), p.buf("wo")
        dma("pool", wq[:], nc_in["L0_attn_b_Wq"].rearrange("(c q) n -> q c n", q=128),
            writes=[B_wq])
        dma("pool", wo[:], nc_in["L0_attn_b_Wo"].rearrange("(c q) n -> q c n", q=128),
            writes=[B_wo])

        kvt = p.sb("kvt", [128, 8 * 130], F32, bctx)
        kvt_bf = p.sb("kvt_bf", [128, 8 * 130], BF16, bctx)
        B_kvt, B_kvtbf = p.buf("kvt"), p.buf("kvt_bf")
        dma("sp", kvt[:], cc_out, reads=[B_ccout], writes=[B_kvt])
        EM("dve", lambda e: e.tensor_copy(kvt_bf[:], kvt[:]), reads=[B_kvt], writes=[B_kvtbf])
        kvt3 = kvt_bf.ap().rearrange("q (pr j) -> q pr j", pr=8)
        if STOP_AFTER == "kv":
            dbg_extra.append(("dbg_kv", kvt, [128, 8 * 130], F32))
            return

        qsb = [p.sb(f"qsb{i}", [128, 512], BF16, bctx) for i in range(2)]
        qfsb = [p.sb(f"qfsb{i}", [128, DC, 512], BF16, bctx) for i in range(2)]
        nrm = [p.sb(f"nrm{i}", [128, NH], F32, bctx) for i in range(2)]
        aob = [p.sb(f"aob{i}", [128, D], BF16, bctx) for i in range(2)]
        aoT = [p.sb(f"aoT{i}", [128, D], BF16, bctx) for i in range(2)]
        B_qsb = [p.buf(f"qsb{i}") for i in range(2)]
        B_qf = [p.buf(f"qfsb{i}") for i in range(2)]
        B_nrm = [p.buf(f"nrm{i}") for i in range(2)]
        B_ao = [p.buf(f"aob{i}") for i in range(2)]
        B_aoT = [p.buf(f"aoT{i}") for i in range(2)]

        for g in range(NG):
            sl = g % 2
            for oc in range(DC):
                mb = B_med[oc % 2]
                qb = B_qsb[oc % 2]
                for kc in range(DC):
                    EM("pe", lambda e, oc=oc, kc=kc, g=g: e.matmul(
                        ps_med[oc % 2][:], wq[:, kc, oc * 128:(oc + 1) * 128],
                        xT3[:, kc, g * 512:(g + 1) * 512],
                        start=(kc == 0), stop=(kc == DC - 1)),
                       reads=[B_wq] + [B_xT[4 * g + j] for j in range(4)], writes=[mb])
                EM("act", lambda e, oc=oc: e.activation(qsb[oc % 2][:], ps_med[oc % 2][:],
                                                        AF.Copy),
                   reads=[mb], writes=[qb])
                ax = B_aux[oc % 2]
                EM("pe", lambda e, oc=oc: e.matmul(
                    ps_aux[oc % 2][:], wfpair[:], qsb[oc % 2][:],
                    start=True, stop=True),
                   reads=[qb, B_const], writes=[ax])
                EM("act", lambda e, oc=oc, sl=sl: e.activation(
                    qfsb[sl][:, oc, :], ps_aux[oc % 2][:], AF.Relu, bias=bf_col[:, 0:1],
                    scale=1.0),
                   reads=[ax], hard_reads=[B_const], writes=[B_qf[sl]])
            for bb in range(4):
                t = 4 * g + bb
                ts = t % 2
                bg = B_big[ts]
                ax = B_aux[ts]
                for pr in range(8):
                    EM("pe", lambda e, pr=pr, bb=bb, sl=sl, ts=ts: e.matmul(
                        ps_big[ts][:, pr * 128:(pr + 1) * 128],
                        qfsb[sl][:, pr, bb * 128:(bb + 1) * 128],
                        kvt3[:, pr, 0:128], start=True, stop=True,
                        skip_group_check=True),
                       reads=[B_qf[sl], B_kvtbf], writes=[bg])
                    EM("pe", lambda e, pr=pr, bb=bb, sl=sl, ts=ts: e.matmul(
                        ps_aux[ts][:, pr * 2: pr * 2 + 2],
                        qfsb[sl][:, pr, bb * 128:(bb + 1) * 128],
                        kvt3[:, pr, 128:130], start=True, stop=True,
                        skip_group_check=True),
                       reads=[B_qf[sl], B_kvtbf], writes=[ax])
                EM("dve", lambda e, ts=ts: e.tensor_scalar(nrm[ts][:], ps_aux[ts][:, 0:NH],
                                                           1e-6, None, op0=OP.max),
                   reads=[ax], writes=[B_nrm[ts]])
                EM("dve", lambda e, ts=ts: e.reciprocal(nrm[ts][:], nrm[ts][:]),
                   reads=[B_nrm[ts]], writes=[B_nrm[ts]], asyncwb=True, hard_in=True)
                for hh in range(NH):
                    EM("dve", lambda e, hh=hh, ts=ts: e.tensor_scalar(
                        aob[ts][:, hh * 64:(hh + 1) * 64], ps_big[ts][:, hh * 64:(hh + 1) * 64],
                        nrm[ts][:, hh:hh + 1], None, op0=OP.mult),
                       reads=[bg], hard_reads=[B_nrm[ts]], writes=[B_ao[ts]])
                for c in range(DC):
                    EM("pe", lambda e, c=c, ts=ts: e.matmul(
                        ps_big[ts][:, c * 128:(c + 1) * 128], aob[ts][:, c * 128:(c + 1) * 128],
                        id_bf[:], start=True, stop=True),
                       reads=[B_ao[ts], B_const], writes=[bg])
                EM("act", lambda e, ts=ts: e.activation(aoT[ts][:], ps_big[ts][:], AF.Copy),
                   reads=[bg], writes=[B_aoT[ts]])
                for nh2 in range(2):
                    for kc in range(DC):
                        EM("pe", lambda e, kc=kc, nh2=nh2, ts=ts: e.matmul(
                            ps_big[ts][:, nh2 * 512:(nh2 + 1) * 512],
                            aoT[ts][:, kc * 128:(kc + 1) * 128],
                            wo[:, kc, nh2 * 512:(nh2 + 1) * 512],
                            start=(kc == 0), stop=False),
                           reads=[B_aoT[ts], B_wo], writes=[bg])
                    EM("pe", lambda e, nh2=nh2, ts=ts: e.matmul(
                        ps_big[ts][:, nh2 * 512:(nh2 + 1) * 512], ones_bf[:],
                        bo_row[0:1, nh2 * 512:(nh2 + 1) * 512], start=False, stop=True),
                       reads=[B_const], writes=[bg])
                EM("dve", lambda e, t=t, ts=ts: e.tensor_tensor(
                    resid3[:, t, :], resid3[:, t, :], ps_big[ts][:], op=OP.add),
                   reads=[B_resid[t], bg], writes=[B_resid[t]])
        bctx.close()
        p.fence()


    subs = [attn_a, ffn, attn_b, ffn]
    for i in range(n_sub):
        subs[i]()

    B_y = p.buf("y")
    dma("sp", y_out.rearrange("(t q) d -> q t d", q=128), resid3,
        reads=B_resid, writes=[B_y])
    if DEBUG_DUMP:
        p.fence()
        for name, sb_t, shape, dt in [
            ("dbg_logits", logits_all, [128, 6 * NT], F32),
            ("dbg_probs", probs_all, [128, 6 * NT], F32),
            ("dbg_mv", mv_all, [128, 2 * NT], F32),
            ("dbg_rstd", rstd_all, [128, NT], F32),
            ("dbg_xT", xT, [128, DC * TOK], BF16),
            ("dbg_probsT", probsT, [6, TOK], BF16),
            ("dbg_sm_e", sm_e[1], [128, 6], F32),
            ("dbg_sm_c", sm_c[1], [128, 6], F32),
            ("dbg_sm_m", sm_m[1], [128, 1], F32),
            ("dbg_sm_m2", sm_m2[1], [128, 1], F32),
            ("dbg_sm_s", sm_s[1], [128, 1], F32),
        ] + dbg_extra:
            o = nc.dram_tensor(name, shape, dt, kind="ExternalOutput").ap()
            dma("sp", o, sb_t[:], reads=[], writes=[p.buf(name)])
    p.final_wait("sp")
    return nc


def kernel(**inputs):
    x = np.ascontiguousarray(np.asarray(inputs["L0_attn_a_x"], dtype=np.float32)
                             .reshape(B * L, D))
    common = {n: np.ascontiguousarray(np.asarray(inputs[n], dtype=np.float32))
              for n in WEIGHT_NAMES}
    ident = np.eye(128, dtype=np.float32)
    nc = build_program(4)
    in_maps = []
    for c in range(NCORES):
        m = dict(common)
        m["x_shard"] = np.ascontiguousarray(x[c * TOK:(c + 1) * TOK])
        m["ident"] = ident
        in_maps.append(m)
    res = run_bass_kernel_spmd(nc, in_maps, core_ids=list(range(NCORES)))
    y = np.concatenate([res.results[c]["y"] for c in range(NCORES)], axis=0)
    return y.reshape(B, L, D).astype(np.float32)


# version 05
# speedup vs baseline: 1.0590x; 1.0288x over previous
import numpy as np
from contextlib import ExitStack

import concourse.bass as bass
import concourse.mybir as mybir
from concourse.alu_op_type import AluOpType as OP
from concourse.bass_utils import run_bass_kernel_spmd

F32 = mybir.dt.float32
BF16 = mybir.dt.bfloat16
AF = mybir.ActivationFunctionType
AX = mybir.AxisListType

NCORES = 8
B, L, D = 4, 4096, 1024
TOK = (B * L) // NCORES
NT = TOK // 128
NG = TOK // 512
DC = D // 128
H = 512
E, FF = 6, 1536
FH = FF // 2
NH, HD, FD = 16, 64, 64
EPS = 1e-5


class Buf:
    __slots__ = ("name", "lw", "readers", "fence", "first")

    def __init__(self, name, fence):
        self.name = name
        self.lw = None
        self.readers = {}
        self.fence = fence
        self.first = True


class Prog:
    SEMS = ["pe", "act", "dve", "pool", "sp", "poold", "cc"]

    def __init__(self, nc):
        self.nc = nc
        self.eng = {"pe": nc.tensor, "act": nc.scalar, "dve": nc.vector,
                    "pool": nc.gpsimd, "sp": nc.sync}
        self.ctx = ExitStack()
        self.sems = {s: self.ctx.enter_context(nc.semaphore("s_" + s)) for s in self.SEMS}
        self.cnt = {s: 0 for s in self.SEMS}
        self.waited = {s: {} for s in self.eng}
        self.fence_counts = {s: 0 for s in self.SEMS}

    def buf(self, name):
        return Buf(name, dict(self.fence_counts))

    def fence(self):
        self.fence_counts = dict(self.cnt)

    def sb(self, name, shape, dt, ctx=None):
        side = None if ctx is None else "right"
        return (ctx or self.ctx).enter_context(
            self.nc.sbuf_tensor(name, shape, dt, side=side))

    def psum(self, name, shape, dt, ctx=None):
        return (ctx or self.ctx).enter_context(self.nc.psum_tensor(name, shape, dt))

    def emit(self, stream, fn, reads=(), hard_reads=(), writes=(), dma=False, asyncwb=False,
             hard_in=False, cc=False):
        hard_reads = list(hard_reads) + list(reads)
        reads = ()
        if cc:
            semname_own = "cc"
        elif dma and stream == "pool":
            semname_own = "poold"
        else:
            semname_own = stream
        need = {}

        def add(sem, val):
            if val > need.get(sem, 0):
                need[sem] = val

        for b in hard_reads:
            if b.lw is not None:
                s2, v2, _ = b.lw
                add(s2, v2)
        for b in writes:
            if b.first:
                for s2, v2 in b.fence.items():
                    if s2 != semname_own and v2 > 0:
                        add(s2, v2)
            if b.lw is not None:
                s2, v2, hard = b.lw
                if s2 != semname_own or hard:
                    add(s2, v2)
            for s2, v2 in b.readers.items():
                if s2 != semname_own:
                    add(s2, v2)
        eng = self.eng[stream]
        for semname, val in sorted(need.items()):
            if self.waited[stream].get(semname, 0) < val:
                eng.wait_ge(self.sems[semname], val)
                self.waited[stream][semname] = val
        inst = fn(eng)
        inc = 16 if dma else 1
        inst.then_inc(self.sems[semname_own], inc)
        self.cnt[semname_own] += inc
        v = self.cnt[semname_own]
        for b in writes:
            b.lw = (semname_own, v, asyncwb)
            b.readers = {}
            b.first = False
        for b in list(reads) + list(hard_reads):
            if b.readers.get(semname_own, 0) < v:
                b.readers[semname_own] = v

    def final_wait(self, stream="sp"):
        for semname in self.SEMS:
            v = self.cnt[semname]
            if v > 0 and self.waited[stream].get(semname, 0) < v:
                self.eng[stream].wait_ge(self.sems[semname], v)
                self.waited[stream][semname] = v


DEBUG_DUMP = False
STOP_AFTER = None
PASSA_LEVEL = 4

WEIGHT_NAMES = [
    "L0_attn_a_r1_w", "L0_attn_a_r1_b", "L0_attn_a_ln_g", "L0_attn_a_ln_b",
    "L0_attn_a_r2_w", "L0_attn_a_r2_b", "L0_attn_a_temp", "L0_attn_a_route_bias",
    "L0_attn_a_W1", "L0_attn_a_b1", "L0_attn_a_W2", "L0_attn_a_b2",
    "L0_attn_b_Wq", "L0_attn_b_Wk", "L0_attn_b_Wv", "L0_attn_b_Wo",
    "L0_attn_b_bo", "L0_attn_b_Wf", "L0_attn_b_bf",
    "L0_ffn_weight", "L0_ffn_aux_weight1", "L0_ffn_aux_weight2", "L0_ffn_bias",
]


def build_program(n_sub=4, debug_out=None):
    nc = bass.Bass("TRN2", target_bir_lowering=False, debug=False)
    p = Prog(nc)
    dbg_extra = []
    nc_in = {}
    shapes = {
        "L0_attn_a_r1_w": [D, H], "L0_attn_a_r1_b": [H], "L0_attn_a_ln_g": [H],
        "L0_attn_a_ln_b": [H], "L0_attn_a_r2_w": [H, E], "L0_attn_a_r2_b": [E],
        "L0_attn_a_temp": [1], "L0_attn_a_route_bias": [E],
        "L0_attn_a_W1": [E, D, FF], "L0_attn_a_b1": [E, FF],
        "L0_attn_a_W2": [E, FF, D], "L0_attn_a_b2": [E, D],
        "L0_attn_b_Wq": [D, D], "L0_attn_b_Wk": [D, D], "L0_attn_b_Wv": [D, D],
        "L0_attn_b_Wo": [D, D], "L0_attn_b_bo": [D], "L0_attn_b_Wf": [HD, FD],
        "L0_attn_b_bf": [FD], "L0_ffn_weight": [D, D], "L0_ffn_aux_weight1": [D, D],
        "L0_ffn_aux_weight2": [D, D], "L0_ffn_bias": [D],
    }
    for n in WEIGHT_NAMES:
        nc_in[n] = nc.dram_tensor(n, shapes[n], F32, kind="ExternalInput").ap()
    x_in = nc.dram_tensor("x_shard", [TOK, D], F32, kind="ExternalInput").ap()
    ident_in = nc.dram_tensor("ident", [128, 128], F32, kind="ExternalInput").ap()
    y_out = nc.dram_tensor("y", [TOK, D], F32, kind="ExternalOutput").ap()
    cc_in = nc.dram_tensor("cc_in", [128, 8 * 130], F32, kind="Internal").ap()
    cc_out = nc.dram_tensor("cc_out", [128, 8 * 130], F32, kind="Internal").ap()

    resid = p.sb("resid", [128, NT * D], F32)
    resid3 = resid.ap().rearrange("q (t d) -> q t d", t=NT)
    xT = p.sb("xT", [128, DC * TOK], BF16)
    xT3 = xT.ap().rearrange("q (c n) -> q c n", c=DC)
    id_f32 = p.sb("id_f32", [128, 128], F32)
    id_bf = p.sb("id_bf", [128, 128], BF16)
    ones_bf = p.sb("ones_bf", [1, 128], BF16)
    ones_f32 = p.sb("ones_f32", [1, 128], F32)
    ones_col_bf = p.sb("ones_col_bf", [128, 1], BF16)
    pow_exp = p.sb("pow_exp", [128, 1], F32)
    g_bc = p.sb("g_bc", [128, H], F32)
    b_bc = p.sb("b_bc", [128, H], F32)
    rb_bc = p.sb("rb_bc", [128, E], F32)
    rt_col = p.sb("rt_col", [128, 1], F32)
    r1b_row = p.sb("r1b_row", [1, H], BF16)
    r2b_row = p.sb("r2b_row", [1, E], BF16)
    ffnb_row = p.sb("ffnb_row", [1, D], BF16)
    bo_row = p.sb("bo_row", [1, D], BF16)
    bf_col = p.sb("bf_col", [128, 1], F32)
    bf_row16 = p.sb("bf_row16", [1, D], BF16)
    wfpad_e = p.sb("wfpad_e", [128, FD], BF16)
    wfpad_o = p.sb("wfpad_o", [128, FD], BF16)
    wfpair = p.sb("wfpair", [128, 128], BF16)
    b1_cols = p.sb("b1_cols", [128, 72], F32)
    b2_sb = p.sb("b2_sb", [6, D], BF16)
    sctx = ExitStack()
    b1_72 = p.sb("b1_72", [72, 128], F32, sctx)
    temp_sb = p.sb("temp_sb", [1, 1], F32, sctx)
    rt_row = p.sb("rt_row", [1, 1], F32, sctx)
    grow = p.sb("grow", [1, H], F32, sctx)
    brow = p.sb("brow", [1, H], F32, sctx)
    rbrow = p.sb("rbrow", [1, E], F32, sctx)

    st6 = [p.sb(f"st6_{i}", [128, 12], F32) for i in range(2)]
    mv_all = p.sb("mv_all", [128, 2 * NT], F32)
    veps_all = p.sb("veps_all", [128, NT], F32)
    rstd_all = p.sb("rstd_all", [128, NT], F32)
    xln = [p.sb(f"xln_{i}", [128, D], BF16) for i in range(2)]
    logits_all = p.sb("logits_all", [128, 6 * NT], F32)
    probs_all = p.sb("probs_all", [128, 6 * NT], F32)
    probsT = p.sb("probsT", [6, TOK], BF16)
    mv2 = [p.sb(f"mv2_{i}", [128, 2], F32) for i in range(2)]
    veps2 = [p.sb(f"veps2_{i}", [128, 1], F32) for i in range(2)]
    rstd2 = [p.sb(f"rstd2_{i}", [128, 1], F32) for i in range(2)]
    t1 = [p.sb(f"t1_{i}", [128, H], F32) for i in range(2)]
    h2b = [p.sb(f"h2b_{i}", [128, H], BF16) for i in range(2)]
    h2T = [p.sb(f"h2T_{i}", [128, H], BF16) for i in range(2)]
    sm_e = [p.sb(f"sm_e_{i}", [128, 6], F32) for i in range(2)]
    sm_c = [p.sb(f"sm_c_{i}", [128, 6], F32) for i in range(2)]
    sm_m = [p.sb(f"sm_m_{i}", [128, 1], F32) for i in range(2)]
    sm_m2 = [p.sb(f"sm_m2_{i}", [128, 1], F32) for i in range(2)]
    sm_s = [p.sb(f"sm_s_{i}", [128, 1], F32) for i in range(2)]
    probs_bf = [p.sb(f"probs_bf_{i}", [128, 6], BF16) for i in range(2)]

    ps_big = [p.psum(f"ps_big{i}", [128, 1024], F32) for i in range(2)]
    ps_med = [p.psum(f"ps_med{i}", [128, 512], F32) for i in range(2)]
    ps_aux = [p.psum(f"ps_aux{i}", [128, 512], F32) for i in range(2)]

    B_resid = [p.buf(f"resid{t}") for t in range(NT)]
    B_xT = [p.buf(f"xT{t}") for t in range(NT)]
    B_const = p.buf("const")
    B_st6 = [p.buf(f"st6_{i}") for i in range(2)]
    B_mv = [p.buf(f"mv{t}") for t in range(NT)]
    B_veps = [p.buf(f"veps{t}") for t in range(NT)]
    B_rstd = [p.buf(f"rstd{t}") for t in range(NT)]
    B_xln = [p.buf(f"xln{i}") for i in range(2)]
    B_logits = [p.buf(f"logits{t}") for t in range(NT)]
    B_probs = [p.buf(f"probs{t}") for t in range(NT)]
    B_probsT = [p.buf(f"probsT{t}") for t in range(NT)]
    B_mv2 = [p.buf(f"mv2_{i}") for i in range(2)]
    B_veps2 = [p.buf(f"veps2_{i}") for i in range(2)]
    B_rstd2 = [p.buf(f"rstd2_{i}") for i in range(2)]
    B_t1 = [p.buf(f"t1_{i}") for i in range(2)]
    B_h2b = [p.buf(f"h2b_{i}") for i in range(2)]
    B_h2T = [p.buf(f"h2T_{i}") for i in range(2)]
    B_sm = [p.buf(f"sm{i}") for i in range(2)]
    B_pbf = [p.buf(f"pbf{i}") for i in range(2)]
    B_big = [p.buf(f"big{i}") for i in range(2)]
    B_med = [p.buf(f"med{i}") for i in range(2)]
    B_aux = [p.buf(f"aux{i}") for i in range(2)]

    EM = p.emit

    def dma(stream, out_ap, in_ap, writes, reads=(), **kw):
        EM(stream, lambda e: e.dma_start(out=out_ap, in_=in_ap, **kw),
           reads=reads, writes=writes, dma=True)

    dma("sp", resid3, x_in.rearrange("(t q) d -> q t d", q=128), writes=B_resid)
    dma("sp", id_f32[:], ident_in, writes=[B_const])
    dma("pool", id_bf[:], ident_in, writes=[B_const])
    dma("pool", r1b_row[:], nc_in["L0_attn_a_r1_b"][None, :], writes=[B_const])
    dma("pool", r2b_row[:], nc_in["L0_attn_a_r2_b"][None, :], writes=[B_const])
    dma("pool", ffnb_row[:], nc_in["L0_ffn_bias"][None, :], writes=[B_const])
    dma("pool", bo_row[:], nc_in["L0_attn_b_bo"][None, :], writes=[B_const])
    dma("pool", b2_sb[:], nc_in["L0_attn_a_b2"], writes=[B_const])
    EM("dve", lambda e: e.memset(wfpad_e[:], 0.0), writes=[B_const])
    EM("dve", lambda e: e.memset(wfpad_o[:], 0.0), writes=[B_const])
    EM("dve", lambda e: e.memset(wfpair[:], 0.0), writes=[B_const])
    dma("pool", wfpad_e[0:64, :], nc_in["L0_attn_b_Wf"], reads=[B_const], writes=[B_const])
    dma("pool", wfpad_o[64:128, :], nc_in["L0_attn_b_Wf"], reads=[B_const], writes=[B_const])
    dma("pool", wfpair[0:64, 0:64], nc_in["L0_attn_b_Wf"], reads=[B_const], writes=[B_const])
    dma("pool", wfpair[64:128, 64:128], nc_in["L0_attn_b_Wf"], reads=[B_const], writes=[B_const])
    for hh in range(NH):
        dma("pool", bf_row16[0:1, hh * FD:(hh + 1) * FD], nc_in["L0_attn_b_bf"][None, :],
            writes=[B_const])
    dma("sp", bf_col[0:64, :], nc_in["L0_attn_b_bf"][:, None], writes=[B_const])
    dma("sp", bf_col[64:128, :], nc_in["L0_attn_b_bf"][:, None], writes=[B_const])
    dma("sp", b1_72[:], nc_in["L0_attn_a_b1"].rearrange("e (c q) -> (e c) q", q=128),
        writes=[B_const])
    dma("sp", temp_sb[:], nc_in["L0_attn_a_temp"][None, :], writes=[B_const])
    dma("sp", grow[:], nc_in["L0_attn_a_ln_g"][None, :], writes=[B_const])
    dma("sp", brow[:], nc_in["L0_attn_a_ln_b"][None, :], writes=[B_const])
    dma("sp", rbrow[:], nc_in["L0_attn_a_route_bias"][None, :], writes=[B_const])

    EM("dve", lambda e: e.memset(ones_bf[:], 1.0), writes=[B_const])
    EM("dve", lambda e: e.memset(ones_f32[:], 1.0), writes=[B_const])
    EM("dve", lambda e: e.memset(ones_col_bf[:], 1.0), writes=[B_const])
    EM("dve", lambda e: e.memset(pow_exp[:], -0.5), writes=[B_const])
    EM("dve", lambda e: e.reciprocal(rt_row[:], temp_sb[:]), reads=[B_const], writes=[B_const])

    def bcast(row_ap, ncols, dst_ap):
        EM("pe", lambda e: e.matmul(ps_med[0][:, 0:ncols], ones_f32[:], row_ap,
                                    start=True, stop=True),
           reads=[B_const], writes=[B_med[0]])
        EM("act", lambda e: e.activation(dst_ap, ps_med[0][:, 0:ncols], AF.Copy),
           reads=[B_med[0]], writes=[B_const])

    bcast(grow[:], H, g_bc[:])
    bcast(brow[:], H, b_bc[:])
    bcast(rbrow[:], E, rb_bc[:])
    bcast(rt_row[:], 1, rt_col[:])
    EM("pe", lambda e: e.matmul(ps_med[1][:, 0:72], b1_72[:], id_f32[0:72, 0:72],
                                start=True, stop=True),
       reads=[B_const], writes=[B_med[1]])
    EM("act", lambda e: e.activation(b1_cols[:], ps_med[1][:, 0:72], AF.Copy),
       reads=[B_med[1]], writes=[B_const])
    sctx.close()
    p.fence()

    def ln_stats(t):
        s = t % 2
        EM("dve", lambda e: e.bn_stats(st6[s][:, 0:6], resid3[:, t, 0:512]),
           reads=[B_resid[t]], writes=[B_st6[s]], asyncwb=True, hard_in=True)
        EM("dve", lambda e: e.bn_stats(st6[s][:, 6:12], resid3[:, t, 512:1024]),
           reads=[B_resid[t]], writes=[B_st6[s]], asyncwb=True, hard_in=True)
        EM("dve", lambda e: e.bn_aggr(mv_all[:, 2 * t:2 * t + 2], st6[s][:]),
           reads=[B_st6[s]], writes=[B_mv[t]], asyncwb=True, hard_in=True)
        EM("dve", lambda e: e.tensor_scalar(veps_all[:, t:t + 1], mv_all[:, 2 * t + 1:2 * t + 2],
                                            EPS, None, op0=OP.add),
           reads=[B_mv[t]], writes=[B_veps[t]])
        EM("pool", lambda e: e.tensor_tensor(rstd_all[:, t:t + 1], veps_all[:, t:t + 1],
                                             pow_exp[:], op=OP.pow),
           reads=[B_veps[t], B_const], writes=[B_rstd[t]], asyncwb=True, hard_in=True)

    def ln_apply_transpose(t):
        s = t % 2
        EM("dve", lambda e: e.tensor_scalar(xln[s][:], resid3[:, t, :],
                                            mv_all[:, 2 * t:2 * t + 1],
                                            rstd_all[:, t:t + 1],
                                            op0=OP.subtract, op1=OP.mult),
           reads=[B_resid[t]], hard_reads=[B_mv[t], B_rstd[t]], writes=[B_xln[s]])
        bp = B_big[t % 2]
        for c in range(DC):
            EM("pe", lambda e, c=c: e.matmul(ps_big[t % 2][:, c * 128:(c + 1) * 128],
                                             xln[s][:, c * 128:(c + 1) * 128], id_bf[:],
                                             start=True, stop=True),
               reads=[B_xln[s], B_const], writes=[bp])
        EM("act", lambda e: e.activation(
            xT3[:, :, t * 128:(t + 1) * 128],
            ps_big[t % 2].ap().rearrange("q (c n) -> q c n", c=8), AF.Copy),
           reads=[bp], writes=[B_xT[t]])

    def ln_pass():
        if STOP_AFTER == "stats":
            for t in range(NT):
                ln_stats(t)
            return
        for t in range(NT):
            ln_stats(t)
            if t >= 2:
                ln_apply_transpose(t - 2)
        ln_apply_transpose(NT - 2)
        ln_apply_transpose(NT - 1)

    def attn_a():
        actx = ExitStack()
        r1w = p.sb("r1w", [128, DC, H], BF16, actx)
        r2w = p.sb("r2w", [128, 4, E], BF16, actx)
        B_r1w, B_r2w = p.buf("r1w"), p.buf("r2w")
        dma("pool", r1w[:], nc_in["L0_attn_a_r1_w"].rearrange("(c q) h -> q c h", q=128),
            writes=[B_r1w])
        dma("pool", r2w[:], nc_in["L0_attn_a_r2_w"].rearrange("(c q) s -> q c s", q=128),
            writes=[B_r2w])

        ln_pass()
        if STOP_AFTER in ("ln", "stats"):
            actx.close(); p.fence(); return

        def r1_part(t):
            s = t % 2
            hb = B_med[s]
            for c in range(DC):
                EM("pe", lambda e, c=c: e.matmul(ps_med[s][:], xT3[:, c, t * 128:(t + 1) * 128],
                                                 r1w[:, c, :], start=(c == 0), stop=False),
                   reads=[B_xT[t], B_r1w], writes=[hb])
            EM("pe", lambda e: e.matmul(ps_med[s][:], ones_bf[:], r1b_row[:],
                                        start=False, stop=True),
               reads=[B_const], writes=[hb])

        def rest_part(t):
            s = t % 2
            hb = B_med[s]
            EM("dve", lambda e: e.bn_stats(st6[s][:, 0:6], ps_med[s][:]),
               reads=[hb], writes=[B_st6[s]], asyncwb=True, hard_in=True)
            EM("dve", lambda e: e.bn_aggr(mv2[s][:], st6[s][:, 0:6]),
               reads=[B_st6[s]], writes=[B_mv2[s]], asyncwb=True, hard_in=True)
            EM("dve", lambda e: e.tensor_scalar(veps2[s][:], mv2[s][:, 1:2], EPS, None, op0=OP.add),
               reads=[B_mv2[s]], writes=[B_veps2[s]])
            EM("pool", lambda e: e.tensor_tensor(rstd2[s][:], veps2[s][:], pow_exp[:], op=OP.pow),
               reads=[B_veps2[s], B_const], writes=[B_rstd2[s]], asyncwb=True, hard_in=True)
            EM("dve", lambda e: e.tensor_scalar(t1[s][:], ps_med[s][:], mv2[s][:, 0:1],
                                                rstd2[s][:], op0=OP.subtract, op1=OP.mult),
               reads=[hb], hard_reads=[B_mv2[s], B_rstd2[s]], writes=[B_t1[s]])
            EM("dve", lambda e: e.tensor_tensor(t1[s][:], t1[s][:], g_bc[:], op=OP.mult),
               reads=[B_t1[s], B_const], writes=[B_t1[s]])
            EM("dve", lambda e: e.tensor_tensor(t1[s][:], t1[s][:], b_bc[:], op=OP.add),
               reads=[B_t1[s], B_const], writes=[B_t1[s]])
            EM("act", lambda e: e.activation(h2b[s][:], t1[s][:], AF.Gelu),
               reads=[B_t1[s]], writes=[B_h2b[s]])
            bg = B_big[s]
            for c in range(4):
                EM("pe", lambda e, c=c: e.matmul(ps_big[s][:, c * 128:(c + 1) * 128],
                                                 h2b[s][:, c * 128:(c + 1) * 128], id_bf[:],
                                                 start=True, stop=True),
                   reads=[B_h2b[s], B_const], writes=[bg])
            EM("act", lambda e: e.activation(h2T[s][:], ps_big[s][:, 0:512], AF.Copy),
               reads=[bg], writes=[B_h2T[s]])
            ab = B_aux[s]
            for c in range(4):
                EM("pe", lambda e, c=c: e.matmul(ps_aux[s][:, 0:E],
                                                 h2T[s][:, c * 128:(c + 1) * 128],
                                                 r2w[:, c, :], start=(c == 0), stop=False),
                   reads=[B_h2T[s], B_r2w], writes=[ab])
            EM("pe", lambda e: e.matmul(ps_aux[s][:, 0:E], ones_bf[:], r2b_row[:],
                                        start=False, stop=True),
               reads=[B_const], writes=[ab])
            EM("dve", lambda e: e.scalar_tensor_tensor(
                logits_all[:, 6 * t:6 * t + 6], ps_aux[s][:, 0:E], rt_col[:, 0:1], rb_bc[:],
                op0=OP.mult, op1=OP.add),
               reads=[ab, B_const], hard_reads=[B_const], writes=[B_logits[t]])

        for t in range(NT):
            r1_part(t)
            if t >= 1:
                rest_part(t - 1)
        rest_part(NT - 1)

        if STOP_AFTER == "router":
            actx.close(); p.fence(); return
        for t in range(NT):
            s = t % 2
            lsl = logits_all[:, 6 * t:6 * t + 6]
            psl = probs_all[:, 6 * t:6 * t + 6]
            EM("dve", lambda e: e.reduce_max(sm_m[s][:], lsl, axis=AX.X),
               reads=[B_logits[t]], writes=[B_sm[s]], asyncwb=True, hard_in=True)
            EM("dve", lambda e: e.tensor_scalar(sm_m[s][:], sm_m[s][:], -1.0, None, op0=OP.mult),
               reads=[B_sm[s]], writes=[B_sm[s]])
            EM("act", lambda e: e.activation(sm_e[s][:], lsl, AF.Exp, bias=sm_m[s][:, 0:1],
                                             scale=1.0),
               reads=[B_logits[t]], hard_reads=[B_sm[s]], writes=[B_sm[s]])
            EM("dve", lambda e: e.reduce_sum(sm_s[s][:], sm_e[s][:], axis=AX.X),
               reads=[B_sm[s]], writes=[B_sm[s]], asyncwb=True, hard_in=True)
            EM("dve", lambda e: e.reciprocal(sm_s[s][:], sm_s[s][:]),
               reads=[B_sm[s]], writes=[B_sm[s]])
            EM("dve", lambda e: e.tensor_scalar(psl, sm_e[s][:], sm_s[s][:, 0:1], None,
                                                op0=OP.mult),
               reads=[B_sm[s]], hard_reads=[B_sm[s]], writes=[B_probs[t]])
            EM("dve", lambda e: e.reduce_max(sm_m[s][:], psl, axis=AX.X),
               reads=[B_probs[t]], writes=[B_sm[s]], asyncwb=True, hard_in=True)
            EM("dve", lambda e: e.tensor_scalar(sm_c[s][:], psl, sm_m[s][:, 0:1], None,
                                                op0=OP.is_equal),
               reads=[B_probs[t]], hard_reads=[B_sm[s]], writes=[B_sm[s]])
            EM("dve", lambda e: e.scalar_tensor_tensor(sm_e[s][:], sm_c[s][:], -1e30, psl,
                                                       op0=OP.mult, op1=OP.add),
               reads=[B_sm[s], B_probs[t]], writes=[B_sm[s]])
            EM("dve", lambda e: e.reduce_max(sm_m2[s][:], sm_e[s][:], axis=AX.X),
               reads=[B_sm[s]], writes=[B_sm[s]], asyncwb=True, hard_in=True)
            EM("dve", lambda e: e.tensor_scalar(sm_c[s][:], psl, sm_m2[s][:, 0:1], None,
                                                op0=OP.is_ge),
               reads=[B_probs[t]], hard_reads=[B_sm[s]], writes=[B_sm[s]])
            EM("dve", lambda e: e.tensor_tensor(psl, psl, sm_c[s][:], op=OP.mult),
               reads=[B_probs[t], B_sm[s]], writes=[B_probs[t]])
            EM("dve", lambda e: e.tensor_copy(probs_bf[s][:], psl),
               reads=[B_probs[t]], writes=[B_pbf[s]])
            EM("pe", lambda e: e.matmul(ps_aux[s][0:6, 0:128], probs_bf[s][:], id_bf[:],
                                        start=True, stop=True),
               reads=[B_pbf[s], B_const], writes=[B_aux[s]])
            EM("act", lambda e: e.activation(probsT.ap()[:, t * 128:(t + 1) * 128],
                                             ps_aux[s][0:6, 0:128], AF.Copy),
               reads=[B_aux[s]], writes=[B_probsT[t]])

        if STOP_AFTER == "softmax":
            actx.close(); p.fence(); return
        for t in range(NT):
            s = t % 2
            bg = B_big[s]
            for nh2 in range(2):
                EM("pe", lambda e, nh2=nh2: e.matmul(
                    ps_big[s][:, nh2 * 512:(nh2 + 1) * 512],
                    probsT.ap()[:, t * 128:(t + 1) * 128],
                    b2_sb[:, nh2 * 512:(nh2 + 1) * 512], start=True, stop=True),
                   reads=[B_probsT[t], B_const], writes=[bg])
            EM("dve", lambda e: e.tensor_tensor(resid3[:, t, :], resid3[:, t, :],
                                                ps_big[s][:], op=OP.add),
               reads=[B_resid[t], bg], writes=[B_resid[t]])

        if STOP_AFTER == "b2":
            actx.close(); p.fence(); return
        w1b = [p.sb(f"w1b{i}", [128, DC, FH], BF16, actx) for i in range(2)]
        w2b = [p.sb(f"w2b{i}", [128, 6, D], BF16, actx) for i in range(2)]
        actb = [p.sb(f"actb{i}", [128, 6, 512], BF16, actx) for i in range(2)]
        B_w1 = [p.buf(f"w1b{i}") for i in range(2)]
        B_w2 = [p.buf(f"w2b{i}") for i in range(2)]
        B_act = [p.buf(f"actb{i}") for i in range(2)]

        order = [0, 2, 4, 1, 3, 5]
        steps = [(eo, fh) for eo in order for fh in range(2)]

        def prefetch(i):
            ex, fh = steps[i]
            sl = i % 2
            dma("pool", w1b[sl][:],
                nc_in["L0_attn_a_W1"][ex, :, fh * FH:(fh + 1) * FH]
                .rearrange("(c q) f -> q c f", q=128), writes=[B_w1[sl]])
            dma("pool", w2b[sl][:],
                nc_in["L0_attn_a_W2"][ex, fh * FH:(fh + 1) * FH, :]
                .rearrange("(c q) n -> q c n", q=128), writes=[B_w2[sl]])

        def w1_group(i, g):
            ex, fh = steps[i]
            sl = i % 2
            func = AF.Gelu if ex % 2 == 0 else AF.Silu
            asl = g % 2
            for c in range(6):
                mb = B_med[c % 2]
                for kc in range(DC):
                    EM("pe", lambda e, c=c, kc=kc, g=g: e.matmul(
                        ps_med[c % 2][:], w1b[sl][:, kc, c * 128:(c + 1) * 128],
                        xT3[:, kc, g * 512:(g + 1) * 512],
                        start=(kc == 0), stop=(kc == DC - 1)),
                       reads=[B_w1[sl]] + [B_xT[4 * g + j] for j in range(4)],
                       writes=[mb])
                bcol = b1_cols[:, ex * 12 + fh * 6 + c: ex * 12 + fh * 6 + c + 1]
                EM("act", lambda e, c=c, g=g, func=func, bcol=bcol: e.activation(
                    actb[asl][:, c, :], ps_med[c % 2][:], func, bias=bcol, scale=1.0),
                   reads=[mb], hard_reads=[B_const], writes=[B_act[asl]])

        def eo_group(i, g):
            ex, fh = steps[i]
            sl = i % 2
            asl = g % 2
            for bb in range(4):
                t = 4 * g + bb
                bg = B_big[t % 2]
                for nh2 in range(2):
                    for c in range(6):
                        EM("pe", lambda e, c=c, nh2=nh2, bb=bb, g=g: e.matmul(
                            ps_big[t % 2][:, nh2 * 512:(nh2 + 1) * 512],
                            actb[asl][:, c, bb * 128:(bb + 1) * 128],
                            w2b[sl][:, c, nh2 * 512:(nh2 + 1) * 512],
                            start=(c == 0), stop=(c == 5)),
                           reads=[B_act[asl], B_w2[sl]], writes=[bg])
                EM("dve", lambda e, t=t: e.scalar_tensor_tensor(
                    resid3[:, t, :], ps_big[t % 2][:],
                    probs_all[:, 6 * t + ex: 6 * t + ex + 1], resid3[:, t, :],
                    op0=OP.mult, op1=OP.add),
                   reads=[bg, B_resid[t]], hard_reads=[B_probs[t]],
                   writes=[B_resid[t]])

        prefetch(0)
        stages = [(i, g) for i in range(len(steps)) for g in range(NG)]
        prefetched = 0
        for j, (i, g) in enumerate(stages):
            if g == 1 and i + 1 < len(steps) and prefetched <= i:
                prefetch(i + 1)
                prefetched = i + 1
            w1_group(i, g)
            if j >= 1:
                eo_group(*stages[j - 1])
        eo_group(*stages[-1])
        actx.close()
        p.fence()

    ffn_calls = [0]

    def ffn():
        tag = f"f{ffn_calls[0]}"
        ffn_calls[0] += 1
        fctx = ExitStack()
        wb = [p.sb(f"{tag}w{i}", [128, DC, D], BF16, fctx) for i in range(3)]
        B_wb = [p.buf(f"{tag}w{i}") for i in range(3)]
        h1sb = [p.sb(f"{tag}h1sb{i}", [128, DC, 512], BF16, fctx) for i in range(2)]
        h2sb = [p.sb(f"{tag}h2sb0", [128, DC, 512], BF16, fctx)] * 2
        tsb = [p.sb(f"{tag}tsb0", [128, D], F32, fctx)] * 2
        B_h1 = [p.buf(f"{tag}h1sb{i}") for i in range(2)]
        B_h2 = [p.buf(f"{tag}h2sb0")] * 2
        B_t = [p.buf(f"{tag}tsb0")] * 2
        for wi, wn in enumerate(["L0_ffn_weight", "L0_ffn_aux_weight1", "L0_ffn_aux_weight2"]):
            dma("pool", wb[wi][:], nc_in[wn].rearrange("(c q) n -> q c n", q=128),
                writes=[B_wb[wi]])

        ln_pass()

        def mm_fm(dst_sb, dst_buf, src_fn, src_bufs, w_i):
            for oc in range(DC):
                mb = B_med[oc % 2]
                for kc in range(DC):
                    EM("pe", lambda e, oc=oc, kc=kc: e.matmul(
                        ps_med[oc % 2][:], wb[w_i][:, kc, oc * 128:(oc + 1) * 128],
                        src_fn(kc), start=(kc == 0), stop=(kc == DC - 1)),
                       reads=[B_wb[w_i]] + src_bufs, writes=[mb])
                EM("act", lambda e, oc=oc: e.activation(dst_sb[:, oc, :], ps_med[oc % 2][:],
                                                        AF.Copy),
                   reads=[mb], writes=[dst_buf])

        for g in range(NG):
            sl = g % 2
            src_bufs = [B_xT[4 * g + j] for j in range(4)]
            mm_fm(h1sb[sl], B_h1[sl],
                  lambda kc, g=g: xT3[:, kc, g * 512:(g + 1) * 512], src_bufs, 0)
            mm_fm(h2sb[sl], B_h2[sl],
                  lambda kc, sl=sl: h1sb[sl][:, kc, :], [B_h1[sl]], 1)
            for bb in range(4):
                t = 4 * g + bb
                bg = B_big[t % 2]
                for nh2 in range(2):
                    for kc in range(DC):
                        EM("pe", lambda e, kc=kc, nh2=nh2, bb=bb: e.matmul(
                            ps_big[t % 2][:, nh2 * 512:(nh2 + 1) * 512],
                            h2sb[sl][:, kc, bb * 128:(bb + 1) * 128],
                            wb[2][:, kc, nh2 * 512:(nh2 + 1) * 512],
                            start=(kc == 0), stop=False),
                           reads=[B_h2[sl], B_wb[2]], writes=[bg])
                    EM("pe", lambda e, nh2=nh2: e.matmul(
                        ps_big[t % 2][:, nh2 * 512:(nh2 + 1) * 512], ones_bf[:],
                        ffnb_row[0:1, nh2 * 512:(nh2 + 1) * 512], start=False, stop=True),
                       reads=[B_const], writes=[bg])
                EM("act", lambda e, t=t: e.activation(tsb[t % 2][:], ps_big[t % 2][:], AF.Tanh),
                   reads=[bg], writes=[B_t[t % 2]])
                EM("dve", lambda e, t=t: e.tensor_tensor(resid3[:, t, :], resid3[:, t, :],
                                                         tsb[t % 2][:], op=OP.add),
                   reads=[B_resid[t], B_t[t % 2]], writes=[B_resid[t]])
        fctx.close()
        p.fence()

    def attn_b():
        bctx = ExitStack()
        wk = p.sb("wk", [128, DC, D], BF16, bctx)
        wv = p.sb("wv", [128, DC, D], BF16, bctx)
        B_wk, B_wv = p.buf("wk"), p.buf("wv")
        ksb = [p.sb(f"ksb{i}", [128, DC, 512], BF16, bctx) for i in range(2)]
        kfe = [p.sb(f"kfe{i}", [128, D], BF16, bctx) for i in range(2)]
        kfo = [p.sb(f"kfo{i}", [128, D], BF16, bctx) for i in range(2)]
        vev = [p.sb(f"vev{i}", [128, D], BF16, bctx) for i in range(2)]
        vod = [p.sb(f"vod{i}", [128, D], BF16, bctx) for i in range(2)]
        kv_all = p.sb("kv_all", [128, 8 * 130], F32, bctx)
        B_ksb = [p.buf(f"ksb{i}") for i in range(2)]
        B_kfe = [p.buf(f"kfe{i}") for i in range(2)]
        B_kfo = [p.buf(f"kfo{i}") for i in range(2)]
        B_vev = [p.buf(f"vev{i}") for i in range(2)]
        B_vod = [p.buf(f"vod{i}") for i in range(2)]
        B_kv_all = p.buf("kv_all")
        B_ccin, B_ccout = p.buf("cc_in"), p.buf("cc_out")
        for i in range(2):
            for bufT, bufB in ((kfe[i], B_kfe[i]), (kfo[i], B_kfo[i]),
                               (vev[i], B_vev[i]), (vod[i], B_vod[i])):
                EM("dve", lambda e, bufT=bufT: e.memset(bufT[:], 0.0), writes=[bufB])
        pview = lambda pst: pst.ap().rearrange("q (pr j) -> q pr j", pr=8)
        ev3 = lambda sbT: sbT.ap().rearrange("q (pr j) -> q pr j", pr=8)

        dma("pool", wk[:], nc_in["L0_attn_b_Wk"].rearrange("(c q) n -> q c n", q=128),
            writes=[B_wk])
        dma("pool", wv[:], nc_in["L0_attn_b_Wv"].rearrange("(c q) n -> q c n", q=128),
            writes=[B_wv])

        ln_pass()
        if STOP_AFTER == "bln":
            bctx.close(); p.fence(); return

        kv3 = kv_all.ap().rearrange("q (pr j) -> q pr j", pr=8)
        EM("dve", lambda e: e.memset(kv_all[:], 0.0), writes=[B_kv_all])
        B_ksump = B_aux[1]
        for g in range(NG):
            sl = g % 2
            for oc in range(DC):
                mb = B_med[oc % 2]
                for kc in range(DC):
                    EM("pe", lambda e, oc=oc, kc=kc, g=g: e.matmul(
                        ps_med[oc % 2][:], wk[:, kc, oc * 128:(oc + 1) * 128],
                        xT3[:, kc, g * 512:(g + 1) * 512],
                        start=(kc == 0), stop=(kc == DC - 1)),
                       reads=[B_wk] + [B_xT[4 * g + j] for j in range(4)], writes=[mb])
                EM("act", lambda e, oc=oc: e.activation(ksb[sl][:, oc, :], ps_med[oc % 2][:],
                                                        AF.Copy),
                   reads=[mb], writes=[B_ksb[sl]])
            for bb in range(4):
                t = 4 * g + bb
                ts = t % 2
                vb = B_big[ts]
                for nh2 in range(2):
                    for kc in range(DC):
                        EM("pe", lambda e, kc=kc, nh2=nh2, t=t: e.matmul(
                            ps_big[ts][:, nh2 * 512:(nh2 + 1) * 512],
                            xT3[:, kc, t * 128:(t + 1) * 128],
                            wv[:, kc, nh2 * 512:(nh2 + 1) * 512],
                            start=(kc == 0), stop=(kc == DC - 1)),
                           reads=[B_xT[t], B_wv], writes=[vb])
                EM("act", lambda e, ts=ts: e.activation(
                    ev3(vev[ts])[:, :, 0:64], pview(ps_big[ts])[:, :, 0:64], AF.Copy),
                   reads=[vb], writes=[B_vev[ts]])
                EM("act", lambda e, ts=ts: e.activation(
                    ev3(vod[ts])[:, :, 64:128], pview(ps_big[ts])[:, :, 64:128], AF.Copy),
                   reads=[vb], writes=[B_vod[ts]])
                if PASSA_LEVEL < 2:
                    continue
                for pr in range(8):
                    for par in range(2):
                        wf_p = wfpad_e if par == 0 else wfpad_o
                        EM("pe", lambda e, pr=pr, par=par, bb=bb, sl=sl, wf_p=wf_p: e.matmul(
                            ps_big[ts][:, pr * 128 + par * 64: pr * 128 + par * 64 + 64],
                            ksb[sl][:, pr, bb * 128:(bb + 1) * 128],
                            wf_p[:], start=((pr == 0 or pr == 4) and par == 0),
                            stop=False, skip_group_check=True),
                           reads=[B_ksb[sl], B_const], writes=[vb])
                for nh2 in range(2):
                    EM("pe", lambda e, nh2=nh2, ts=ts: e.matmul(
                        ps_big[ts][:, nh2 * 512:(nh2 + 1) * 512], ones_bf[:],
                        bf_row16[0:1, nh2 * 512:(nh2 + 1) * 512], start=False, stop=True,
                        skip_group_check=True),
                       reads=[B_const], writes=[vb])
                EM("act", lambda e, ts=ts: e.activation(
                    ev3(kfe[ts])[:, :, 0:64], pview(ps_big[ts])[:, :, 0:64], AF.Relu),
                   reads=[vb], writes=[B_kfe[ts]])
                EM("act", lambda e, ts=ts: e.activation(
                    ev3(kfo[ts])[:, :, 64:128], pview(ps_big[ts])[:, :, 64:128], AF.Relu),
                   reads=[vb], writes=[B_kfo[ts]])
                if PASSA_LEVEL < 3:
                    continue
                for pr in range(8):
                    mi = pr // 4
                    mcol = (pr % 4) * 128
                    EM("pe", lambda e, pr=pr, ts=ts, mi=mi, mcol=mcol: e.matmul(
                        ps_med[mi][:, mcol:mcol + 128],
                        kfe[ts][:, pr * 128:(pr + 1) * 128],
                        vev[ts][:, pr * 128:(pr + 1) * 128],
                        start=(pr == 0 or pr == 4), stop=False, skip_group_check=True),
                       reads=[B_kfe[ts], B_vev[ts]], writes=[B_med[mi]])
                    EM("pe", lambda e, pr=pr, ts=ts, mi=mi, mcol=mcol: e.matmul(
                        ps_med[mi][:, mcol:mcol + 128],
                        kfo[ts][:, pr * 128:(pr + 1) * 128],
                        vod[ts][:, pr * 128:(pr + 1) * 128],
                        start=False, stop=(pr == 3 or pr == 7), skip_group_check=True),
                       reads=[B_kfo[ts], B_vod[ts]], writes=[B_med[mi]])
                    EM("pe", lambda e, pr=pr, ts=ts: e.matmul(
                        ps_aux[1][:, pr * 2: pr * 2 + 1],
                        kfe[ts][:, pr * 128:(pr + 1) * 128], ones_col_bf[:],
                        start=(pr == 0), stop=False, skip_group_check=True),
                       reads=[B_kfe[ts], B_const], writes=[B_ksump])
                    EM("pe", lambda e, pr=pr, ts=ts: e.matmul(
                        ps_aux[1][:, pr * 2 + 1: pr * 2 + 2],
                        kfo[ts][:, pr * 128:(pr + 1) * 128], ones_col_bf[:],
                        start=False, stop=(pr == 7), skip_group_check=True),
                       reads=[B_kfo[ts], B_const], writes=[B_ksump])
                EM("dve", lambda e: e.tensor_tensor(
                    kv3[:, 0:4, 0:128], kv3[:, 0:4, 0:128],
                    ps_med[0].ap().rearrange("q (pr j) -> q pr j", pr=4), op=OP.add),
                   reads=[B_med[0], B_kv_all], writes=[B_kv_all])
                EM("dve", lambda e: e.tensor_tensor(
                    kv3[:, 4:8, 0:128], kv3[:, 4:8, 0:128],
                    ps_med[1].ap().rearrange("q (pr j) -> q pr j", pr=4), op=OP.add),
                   reads=[B_med[1], B_kv_all], writes=[B_kv_all])
                EM("dve", lambda e: e.tensor_tensor(
                    kv3[:, :, 128:130], kv3[:, :, 128:130],
                    ps_aux[1].ap()[:, 0:16].rearrange("q (pr j) -> q pr j", pr=8), op=OP.add),
                   reads=[B_ksump, B_kv_all], writes=[B_kv_all])

        if STOP_AFTER == "kvacc":
            dbg_extra.append(("dbg_kv", kv_all, [128, 8 * 130], F32))
            return
        dma("sp", cc_in, kv_all[:], reads=[B_kv_all], writes=[B_ccin])
        EM("pool", lambda e: e.collective_compute(
            "AllReduce", OP.add, replica_groups=[[0, 1], [2, 3], [4, 5], [6, 7]],
            ins=[cc_in.opt()], outs=[cc_out.opt()]),
           reads=[B_ccin], writes=[B_ccout], asyncwb=True, hard_in=True, cc=True)

        bctx.close()
        p.fence()
        bctx = ExitStack()
        wq = p.sb("wq", [128, DC, D], BF16, bctx)
        wo = p.sb("wo", [128, DC, D], BF16, bctx)
        B_wq, B_wo = p.buf("wq"), p.buf("wo")
        dma("pool", wq[:], nc_in["L0_attn_b_Wq"].rearrange("(c q) n -> q c n", q=128),
            writes=[B_wq])
        dma("pool", wo[:], nc_in["L0_attn_b_Wo"].rearrange("(c q) n -> q c n", q=128),
            writes=[B_wo])

        kvt = p.sb("kvt", [128, 8 * 130], F32, bctx)
        kvt_bf = p.sb("kvt_bf", [128, 8 * 130], BF16, bctx)
        B_kvt, B_kvtbf = p.buf("kvt"), p.buf("kvt_bf")
        dma("sp", kvt[:], cc_out, reads=[B_ccout], writes=[B_kvt])
        EM("dve", lambda e: e.tensor_copy(kvt_bf[:], kvt[:]), reads=[B_kvt], writes=[B_kvtbf])
        kvt3 = kvt_bf.ap().rearrange("q (pr j) -> q pr j", pr=8)
        if STOP_AFTER == "kv":
            dbg_extra.append(("dbg_kv", kvt, [128, 8 * 130], F32))
            return

        qsb = [p.sb(f"qsb{i}", [128, 512], BF16, bctx) for i in range(2)]
        qfsb = [p.sb(f"qfsb{i}", [128, DC, 512], BF16, bctx) for i in range(2)]
        nrm = [p.sb(f"nrm{i}", [128, NH], F32, bctx) for i in range(2)]
        aob = [p.sb(f"aob{i}", [128, D], BF16, bctx) for i in range(2)]
        aoT = [p.sb(f"aoT{i}", [128, D], BF16, bctx) for i in range(2)]
        B_qsb = [p.buf(f"qsb{i}") for i in range(2)]
        B_qf = [p.buf(f"qfsb{i}") for i in range(2)]
        B_nrm = [p.buf(f"nrm{i}") for i in range(2)]
        B_ao = [p.buf(f"aob{i}") for i in range(2)]
        B_aoT = [p.buf(f"aoT{i}") for i in range(2)]

        for g in range(NG):
            sl = g % 2
            for oc in range(DC):
                mb = B_med[oc % 2]
                qb = B_qsb[oc % 2]
                for kc in range(DC):
                    EM("pe", lambda e, oc=oc, kc=kc, g=g: e.matmul(
                        ps_med[oc % 2][:], wq[:, kc, oc * 128:(oc + 1) * 128],
                        xT3[:, kc, g * 512:(g + 1) * 512],
                        start=(kc == 0), stop=(kc == DC - 1)),
                       reads=[B_wq] + [B_xT[4 * g + j] for j in range(4)], writes=[mb])
                EM("act", lambda e, oc=oc: e.activation(qsb[oc % 2][:], ps_med[oc % 2][:],
                                                        AF.Copy),
                   reads=[mb], writes=[qb])
                ax = B_aux[oc % 2]
                EM("pe", lambda e, oc=oc: e.matmul(
                    ps_aux[oc % 2][:], wfpair[:], qsb[oc % 2][:],
                    start=True, stop=True),
                   reads=[qb, B_const], writes=[ax])
                EM("act", lambda e, oc=oc, sl=sl: e.activation(
                    qfsb[sl][:, oc, :], ps_aux[oc % 2][:], AF.Relu, bias=bf_col[:, 0:1],
                    scale=1.0),
                   reads=[ax], hard_reads=[B_const], writes=[B_qf[sl]])
            for bb in range(4):
                t = 4 * g + bb
                ts = t % 2
                bg = B_big[ts]
                ax = B_aux[ts]
                for pr in range(8):
                    EM("pe", lambda e, pr=pr, bb=bb, sl=sl, ts=ts: e.matmul(
                        ps_big[ts][:, pr * 128:(pr + 1) * 128],
                        qfsb[sl][:, pr, bb * 128:(bb + 1) * 128],
                        kvt3[:, pr, 0:128], start=True, stop=True,
                        skip_group_check=True),
                       reads=[B_qf[sl], B_kvtbf], writes=[bg])
                    EM("pe", lambda e, pr=pr, bb=bb, sl=sl, ts=ts: e.matmul(
                        ps_aux[ts][:, pr * 2: pr * 2 + 2],
                        qfsb[sl][:, pr, bb * 128:(bb + 1) * 128],
                        kvt3[:, pr, 128:130], start=True, stop=True,
                        skip_group_check=True),
                       reads=[B_qf[sl], B_kvtbf], writes=[ax])
                EM("dve", lambda e, ts=ts: e.tensor_scalar(nrm[ts][:], ps_aux[ts][:, 0:NH],
                                                           1e-6, None, op0=OP.max),
                   reads=[ax], writes=[B_nrm[ts]])
                EM("dve", lambda e, ts=ts: e.reciprocal(nrm[ts][:], nrm[ts][:]),
                   reads=[B_nrm[ts]], writes=[B_nrm[ts]], asyncwb=True, hard_in=True)
                for hh in range(NH):
                    EM("dve", lambda e, hh=hh, ts=ts: e.tensor_scalar(
                        aob[ts][:, hh * 64:(hh + 1) * 64], ps_big[ts][:, hh * 64:(hh + 1) * 64],
                        nrm[ts][:, hh:hh + 1], None, op0=OP.mult),
                       reads=[bg], hard_reads=[B_nrm[ts]], writes=[B_ao[ts]])
                for c in range(DC):
                    EM("pe", lambda e, c=c, ts=ts: e.matmul(
                        ps_big[ts][:, c * 128:(c + 1) * 128], aob[ts][:, c * 128:(c + 1) * 128],
                        id_bf[:], start=True, stop=True),
                       reads=[B_ao[ts], B_const], writes=[bg])
                EM("act", lambda e, ts=ts: e.activation(aoT[ts][:], ps_big[ts][:], AF.Copy),
                   reads=[bg], writes=[B_aoT[ts]])
                for nh2 in range(2):
                    for kc in range(DC):
                        EM("pe", lambda e, kc=kc, nh2=nh2, ts=ts: e.matmul(
                            ps_big[ts][:, nh2 * 512:(nh2 + 1) * 512],
                            aoT[ts][:, kc * 128:(kc + 1) * 128],
                            wo[:, kc, nh2 * 512:(nh2 + 1) * 512],
                            start=(kc == 0), stop=False),
                           reads=[B_aoT[ts], B_wo], writes=[bg])
                    EM("pe", lambda e, nh2=nh2, ts=ts: e.matmul(
                        ps_big[ts][:, nh2 * 512:(nh2 + 1) * 512], ones_bf[:],
                        bo_row[0:1, nh2 * 512:(nh2 + 1) * 512], start=False, stop=True),
                       reads=[B_const], writes=[bg])
                EM("dve", lambda e, t=t, ts=ts: e.tensor_tensor(
                    resid3[:, t, :], resid3[:, t, :], ps_big[ts][:], op=OP.add),
                   reads=[B_resid[t], bg], writes=[B_resid[t]])
        bctx.close()
        p.fence()


    subs = [attn_a, ffn, attn_b, ffn]
    for i in range(n_sub):
        subs[i]()

    B_y = p.buf("y")
    dma("sp", y_out.rearrange("(t q) d -> q t d", q=128), resid3,
        reads=B_resid, writes=[B_y])
    if DEBUG_DUMP:
        p.fence()
        for name, sb_t, shape, dt in [
            ("dbg_logits", logits_all, [128, 6 * NT], F32),
            ("dbg_probs", probs_all, [128, 6 * NT], F32),
            ("dbg_mv", mv_all, [128, 2 * NT], F32),
            ("dbg_rstd", rstd_all, [128, NT], F32),
            ("dbg_xT", xT, [128, DC * TOK], BF16),
            ("dbg_probsT", probsT, [6, TOK], BF16),
            ("dbg_sm_e", sm_e[1], [128, 6], F32),
            ("dbg_sm_c", sm_c[1], [128, 6], F32),
            ("dbg_sm_m", sm_m[1], [128, 1], F32),
            ("dbg_sm_m2", sm_m2[1], [128, 1], F32),
            ("dbg_sm_s", sm_s[1], [128, 1], F32),
        ] + dbg_extra:
            o = nc.dram_tensor(name, shape, dt, kind="ExternalOutput").ap()
            dma("sp", o, sb_t[:], reads=[], writes=[p.buf(name)])
    p.final_wait("sp")
    return nc


def kernel(**inputs):
    x = np.ascontiguousarray(np.asarray(inputs["L0_attn_a_x"], dtype=np.float32)
                             .reshape(B * L, D))
    common = {n: np.ascontiguousarray(np.asarray(inputs[n], dtype=np.float32))
              for n in WEIGHT_NAMES}
    ident = np.eye(128, dtype=np.float32)
    nc = build_program(4)
    in_maps = []
    for c in range(NCORES):
        m = dict(common)
        m["x_shard"] = np.ascontiguousarray(x[c * TOK:(c + 1) * TOK])
        m["ident"] = ident
        in_maps.append(m)
    res = run_bass_kernel_spmd(nc, in_maps, core_ids=list(range(NCORES)))
    y = np.concatenate([res.results[c]["y"] for c in range(NCORES)], axis=0)
    return y.reshape(B, L, D).astype(np.float32)
